# version 01
# baseline (speedup 1.0000x reference)
import numpy as np
import ml_dtypes
import concourse.bass as bass
import concourse.mybir as mybir

F32 = mybir.dt.float32
BF16 = mybir.dt.bfloat16
FP8 = mybir.dt.float8e4
AF = mybir.ActivationFunctionType
ALU = mybir.AluOpType
AX = mybir.AxisListType

T = 4096
D = 1024
INV224 = float(np.float32(1.0 / 224.0))


class Sem:
    def __init__(self, nc, name):
        self.h = nc.alloc_semaphore(name)
        self.n = 0

    def inc(self, instr, k=1):
        instr.then_inc(self.h, k)
        self.n += k
        return self.n

    def inc16(self, instr):
        return self.inc(instr, 16)


class KB:
    def __init__(self, nc):
        self.nc = nc
        self.LD = Sem(nc, "LD")
        self.GD = Sem(nc, "GD")
        self.TE = Sem(nc, "TE")
        self.VE = Sem(nc, "VE")
        self.SE = Sem(nc, "SE")
        self.sync, self.gp = nc.sync, nc.gpsimd
        self.t, self.v, self.s = nc.tensor, nc.vector, nc.scalar
        self.st = {}
        self._uid = 0

    def uid(self):
        self._uid += 1
        return self._uid


def host_constants():
    DIM_HEAD, WINDOW, QK_SCALE = 64, 512, 8.0
    half_idx = np.arange(0, DIM_HEAD, 2, dtype=np.float64)
    inv_freq = 1.0 / 10000.0 ** (half_idx / DIM_HEAD)
    tt = np.arange(T, dtype=np.float64)
    freqs = tt[:, None] * inv_freq[None, :]
    freqs = np.concatenate([freqs, freqs], axis=-1)
    base = (half_idx + 0.4 * DIM_HEAD) / (1.4 * DIM_HEAD)
    power = (tt - T // 2) / float(WINDOW // 2)
    scale = base[None, :] ** power[:, None]
    scale = np.concatenate([scale, scale], axis=-1)
    cos, sin = np.cos(freqs), np.sin(freqs)
    sign = np.where(np.arange(DIM_HEAD) < 32, -1.0, 1.0)
    a1q = (QK_SCALE * cos * scale).T
    a2q = (QK_SCALE * sin * scale).T * sign[:, None]
    a1k = (cos / scale).T
    a2k = (sin / scale).T * sign[:, None]
    rep = lambda a: np.concatenate([a, a], axis=0).astype(np.float32)
    r = np.arange(128)
    prot = np.zeros((128, 128), np.float32)
    for m in range(128):
        prot[(m + 32) % 64 + 64 * (m // 64), m] = 1.0
    return {
        "A1q": rep(a1q), "A2q": rep(a2q), "A1k": rep(a1k), "A2k": rep(a2k),
        "identb": np.eye(128, dtype=ml_dtypes.bfloat16),
        "identf": np.eye(128, dtype=np.float32),
        "maskL": (r[None, :] <= r[:, None]).astype(ml_dtypes.bfloat16),
        "maskU": (r[:, None] <= r[None, :]).astype(ml_dtypes.bfloat16),
        "prot": prot,
        "onesall": np.ones((128, 128), np.float32),
        "onesallb": np.ones((128, 4), ml_dtypes.bfloat16),
        "onesrowb": np.ones((1, 128), ml_dtypes.bfloat16),
        "ones2f": np.stack([(r < 64).astype(np.float32), (r >= 64).astype(np.float32)], 1),
        "bc2f": np.stack([(r < 64).astype(np.float32), (r >= 64).astype(np.float32)])[:, :],
        "epscol": np.full((128, 1), 1e-5, np.float32),
    }


def alloc_shared(kb):
    nc, st = kb.nc, kb.st
    st["identb"] = nc.alloc_sbuf_tensor("sb_identb", [128, 128], BF16)
    st["identf"] = nc.alloc_sbuf_tensor("sb_identf", [128, 128], F32)
    st["maskL"] = nc.alloc_sbuf_tensor("sb_maskL", [128, 128], BF16)
    st["maskU"] = nc.alloc_sbuf_tensor("sb_maskU", [128, 128], BF16)
    st["prot"] = nc.alloc_sbuf_tensor("sb_prot", [128, 128], F32)
    st["onesall"] = nc.alloc_sbuf_tensor("sb_onesall", [128, 128], F32)
    st["onesallb"] = nc.alloc_sbuf_tensor("sb_onesallb", [128, 4], BF16)
    st["onesrowb"] = nc.alloc_sbuf_tensor("sb_onesrowb", [1, 128], BF16)
    st["ones2f"] = nc.alloc_sbuf_tensor("sb_ones2f", [128, 2], F32)
    st["epscol"] = nc.alloc_sbuf_tensor("sb_epscol", [128, 1], F32)
    st["bc2f"] = nc.alloc_sbuf_tensor("sb_bc2f", [2, 128], F32)


def emit_shared_init(kb, consts_in):
    sync, v = kb.sync, kb.v
    st = kb.st
    for name in ("identb", "identf", "maskL", "maskU", "prot", "onesall",
                 "onesallb", "onesrowb", "ones2f", "epscol", "bc2f"):
        kb.LD.inc16(sync.dma_start(st[name][:], consts_in[name].ap()))
    kb.VE.inc(v.engine_nop())



def emit_ln_natural(kb, x, z, cols, sq):
    v, s = kb.v, kb.s
    v.reduce_sum(cols[:, 0:1], x[:], AX.X)
    v.drain()
    mean_tok = kb.VE.inc(v.tensor_scalar(cols[:, 1:2], cols[:, 0:1], 1.0 / D, None, ALU.mult))
    s.wait_ge(kb.VE.h, mean_tok)
    sqd = kb.SE.inc(s.activation(sq[:], x[:], AF.Square))
    v.wait_ge(kb.SE.h, sqd)
    v.reduce_sum(cols[:, 2:3], sq[:], AX.X)
    v.drain()
    v.tensor_scalar(cols[:, 3:4], cols[:, 2:3], 1.0 / D, None, ALU.mult)
    v.tensor_mul(cols[:, 4:5], cols[:, 1:2], cols[:, 1:2])
    v.drain()
    var_tok = kb.VE.inc(v.tensor_sub(cols[:, 5:6], cols[:, 3:4], cols[:, 4:5]))
    s.wait_ge(kb.VE.h, var_tok)
    sqr = kb.SE.inc(s.activation(cols[:, 6:7], cols[:, 5:6], AF.Sqrt, bias=kb.st["epscol"][:, 0:1]))
    v.wait_ge(kb.SE.h, sqr)
    v.reciprocal(cols[:, 7:8], cols[:, 6:7])
    v.drain()
    v.scalar_tensor_tensor(cols[:, 8:9], cols[:, 1:2], -1.0,
                           cols[:, 7:8], ALU.mult, ALU.mult)
    v.drain()
    return kb.VE.inc(v.tensor_scalar(z[:], x[:], cols[:, 7:8], cols[:, 8:9],
                                     ALU.mult, ALU.add))



def emit_quant(kb, src, tmp_f32, tmp_fp8, out, qcols, blocks, width=128):
    v = kb.v
    n = blocks * width
    am, s2, r2 = qcols[:, 0:blocks], qcols[:, blocks:2 * blocks], qcols[:, 2 * blocks:3 * blocks]
    r3 = lambda ap: ap[:, 0:n].rearrange("p (b i) -> p b i", i=width)
    bc = lambda ap: ap.unsqueeze(2).broadcast_to([128, blocks, width])
    v.tensor_reduce(am, r3(src), AX.X, ALU.max, apply_absolute_value=True)
    v.drain()
    v.tensor_scalar(s2, am, INV224, 2e-12, ALU.mult, ALU.max)
    v.drain()
    v.reciprocal(r2, s2)
    v.drain()
    v.tensor_tensor(r3(tmp_f32), r3(src), bc(r2), ALU.mult)
    v.drain()
    v.tensor_copy(tmp_fp8[:, 0:n], tmp_f32[:, 0:n])
    v.drain()
    return v.tensor_tensor(r3(out), r3(tmp_fp8), bc(s2), ALU.mult)



SERIAL_A0 = False


def emit_A0(kb, x_src, y_dst):
    nc, sync, gp, t, v, s = kb.nc, kb.sync, kb.gp, kb.t, kb.v, kb.s
    st = kb.st
    identf = st["identf"]
    with nc.sbuf_tensor("a0_x", [128, 2 * 1024], F32) as xt, \
         nc.sbuf_tensor("a0_z", [128, 1024], F32) as zt, \
         nc.sbuf_tensor("a0_sq", [128, 1024], F32) as sqt, \
         nc.sbuf_tensor("a0_cols", [128, 16], F32) as cols, \
         nc.sbuf_tensor("a0_yst", [128, 2 * 1024], F32) as yst, \
         nc.psum_tensor("a0_ps", [128, 1024], F32) as ps0, \
         nc.psum_tensor("a0_ps2", [128, 1024], F32) as ps1:
        store_toks = [0, 0]
        prev_t = 0
        prev_z = 0
        for tt in range(T // 128):
            par = tt % 2
            x = xt[:, par * 1024:(par + 1) * 1024]
            ys = yst[:, par * 1024:(par + 1) * 1024]
            ps = (ps0, ps1)[par]
            sync.wait_ge(kb.SE.h, store_toks[par])
            ldn = kb.LD.inc16(sync.dma_start(x, x_src.ap()[tt * 128:(tt + 1) * 128, :]))
            v.wait_ge(kb.LD.h, ldn)
            v.wait_ge(kb.TE.h, prev_t)
            s.wait_ge(kb.LD.h, ldn)
            zdone = emit_ln_natural(kb, x, zt, cols, sqt)
            t.wait_ge(kb.VE.h, zdone)
            t.wait_ge(kb.SE.h, store_toks[par])
            for c in range(8):
                ins = t.transpose(ps[:, c * 128:(c + 1) * 128],
                                  zt[:, c * 128:(c + 1) * 128], identf[:])
            prev_t = tdone = kb.TE.inc(ins)
            s.wait_ge(kb.TE.h, tdone)
            store_toks[par] = kb.SE.inc(s.copy(ys, ps[:]))
            gp.wait_ge(kb.SE.h, store_toks[par])
            kb.GD.inc16(gp.dma_start(
                y_dst.ap()[:, :, 1 + tt * 128:1 + (tt + 1) * 128].transpose([1, 0, 2]),
                ys.rearrange("p (c i) -> p c i", i=128)))
            if SERIAL_A0:
                for e in (kb.sync, kb.t, kb.v, kb.s, kb.gp):
                    e.wait_ge(kb.GD.h, kb.GD.n)
                    e.wait_ge(kb.VE.h, kb.VE.n)
                    e.wait_ge(kb.SE.h, kb.SE.n)
                    e.wait_ge(kb.TE.h, kb.TE.n)
                    e.wait_ge(kb.LD.h, kb.LD.n)
        gp.wait_ge(kb.GD.h, kb.GD.n)
        done = kb.GD.n
    for e in (kb.sync, kb.t, kb.v, kb.s):
        e.wait_ge(kb.GD.h, done)



def load_ffn_attn_weights(kb, l, ins, wts, pn):
    nc, sync, gp, t, v, s = kb.nc, kb.sync, kb.gp, kb.t, kb.v, kb.s
    for kc in range(8):
        kb.GD.inc16(gp.dma_start(
            wts["wqkv"][:, kc * 1536:(kc + 1) * 1536],
            ins["L0_ffn_wqkv"].ap()[l, kc * 128:(kc + 1) * 128, :]))
    for ic in range(4):
        kb.GD.inc16(gp.dma_start(
            wts["wo"][:, ic * 1024:(ic + 1) * 1024],
            ins["L0_ffn_wo"].ap()[l, ic * 128:(ic + 1) * 128, :]))
    qs, ks = ins["L0_ffn_q_scale"], ins["L0_ffn_k_scale"]
    for half in range(2):
        b = 64 * half
        kb.LD.inc16(sync.dma_start(wts["qscol"][b:b + 64, 0:1], qs.ap()[l, :].unsqueeze(1)))
        kb.LD.inc16(sync.dma_start(wts["kscol"][b:b + 64, 0:1], ks.ap()[l, :].unsqueeze(1)))
        kb.LD.inc16(sync.dma_start(wts["qssh"][b:b + 32, 0:1], qs.ap()[l, 32:64].unsqueeze(1)))
        kb.LD.inc16(sync.dma_start(wts["qssh"][b + 32:b + 64, 0:1], qs.ap()[l, 0:32].unsqueeze(1)))
        kb.LD.inc16(sync.dma_start(wts["kssh"][b:b + 32, 0:1], ks.ap()[l, 32:64].unsqueeze(1)))
        kb.LD.inc16(sync.dma_start(wts["kssh"][b + 32:b + 64, 0:1], ks.ap()[l, 0:32].unsqueeze(1)))
    ldr1 = kb.LD.inc16(sync.dma_start(wts["qrow"][0:1, 0:64], qs.ap()[l, :].unsqueeze(0)))
    ldr = kb.LD.inc16(sync.dma_start(wts["krow"][0:1, 0:64], ks.ap()[l, :].unsqueeze(0)))
    v.wait_ge(kb.LD.h, ldr)
    v.tensor_reduce(wts["qrow"][0:1, 64:65], wts["qrow"][0:1, 0:64], AX.X, ALU.max,
                    apply_absolute_value=True)
    v.tensor_reduce(wts["krow"][0:1, 64:65], wts["krow"][0:1, 0:64], AX.X, ALU.max,
                    apply_absolute_value=True)
    v.drain()
    v.tensor_mul(wts["qrow"][0:1, 65:66], wts["qrow"][0:1, 64:65], wts["krow"][0:1, 64:65])
    v.drain()
    cn = kb.VE.inc(v.tensor_scalar(wts["qrow"][0:1, 66:67], wts["qrow"][0:1, 65:66],
                                   -16.0, None, ALU.mult))
    t.wait_ge(kb.VE.h, cn)
    mmn = kb.TE.inc(t.matmul(pn.ps[:, 0:1], kb.st["onesall"][0:1, :],
                             wts["qrow"][0:1, 66:67], start=True, stop=True))
    s.wait_ge(kb.TE.h, mmn)
    tok = kb.SE.inc(s.copy(wts["negC"][:], pn.ps[:, 0:1]))
    pn.read_by(kb.SE.h, tok)
    return tok


def alloc_ffn_attn(kb, ns):
    import contextlib
    nc = kb.nc
    es = contextlib.ExitStack()
    a = lambda n, sh, dt: es.enter_context(nc.sbuf_tensor(ns + n, sh, dt))
    wts = {
        "wqkv": a("wqkv", [128, 8 * 1536], BF16),
        "wo": a("wo", [128, 4 * 1024], BF16),
        "qscol": a("qscol", [128, 1], F32), "qssh": a("qssh", [128, 1], F32),
        "kscol": a("kscol", [128, 1], F32), "kssh": a("kssh", [128, 1], F32),
        "qrow": a("qrow", [1, 80], F32), "krow": a("krow", [1, 80], F32),
        "negC": a("negC", [128, 1], F32),
    }
    bufs = {
        "yw": a("yw", [128, 8 * 512], F32),
        "ywb": a("ywb", [128, 8 * 513], BF16),
        "guard": a("guard", [128, 8], F32),
        "tabs": a("tabs", [128, 4 * 512], F32),
        "tabsrc": a("tabsrc", [128, 4 * 512], F32),
        "qh": a("qh", [128, 4 * 512], BF16),
        "kh0": a("kh0", [128, 4 * 512], BF16), "kh1": a("kh1", [128, 4 * 512], BF16),
        "vv0": a("vv0", [128, 4 * 520], BF16), "vv1": a("vv1", [128, 4 * 520], BF16),
        "qn0": a("qn0", [128, 512], F32),
        "sqf": a("sqf", [128, 512], F32),
        "rowa": a("rowa", [2, 512], F32), "rowb": a("rowb", [2, 512], F32),
        "tt1": a("tt1", [128, 512], F32), "tt2": a("tt2", [128, 512], F32),
        "h1": a("h1", [128, 512], F32), "h2": a("h2", [128, 512], F32),
        "expb": a("expb", [128, 2 * 512], BF16),
        "rowr": a("rowr", [128, 512], F32),
        "bcf": a("bcf", [128, 512], F32),
        "oT": a("oT", [128, 4 * 512], BF16),
    }
    return es, wts, bufs


VALID_SPAN = {kc: (max(0, (kc - 4)) * 128, min(512, kc * 128 + 128)) for kc in range(8)}


class Bank:
    def __init__(self, ps):
        self.ps = ps
        self.rd = None

    def guard(self, eng):
        if self.rd is not None:
            eng.wait_ge(*self.rd)

    def read_by(self, semh, val):
        self.rd = (semh, val)


def emit_ffn_attn(kb, l, y_sc, ins, A_dram):
    nc, sync, gp, t, v, s = kb.nc, kb.sync, kb.gp, kb.t, kb.v, kb.s
    st = kb.st
    u = kb.uid()
    es, wts, bu = alloc_ffn_attn(kb, f"fa{u}_")
    with es, \
         nc.psum_tensor(f"fa{u}_g0", [128, 512], F32) as pg0, \
         nc.psum_tensor(f"fa{u}_g1", [128, 512], F32) as pg1, \
         nc.psum_tensor(f"fa{u}_g2", [128, 512], F32) as pg2, \
         nc.psum_tensor(f"fa{u}_g3", [128, 512], F32) as pg3, \
         nc.psum_tensor(f"fa{u}_g4", [128, 512], F32) as pg4, \
         nc.psum_tensor(f"fa{u}_g5", [128, 512], F32) as pg5, \
         nc.psum_tensor(f"fa{u}_row", [2, 512], F32) as prow_ps, \
         nc.psum_tensor(f"fa{u}_srow", [1, 512], F32) as srow_ps:
        g = [Bank(p) for p in (pg0, pg1, pg2, pg3, pg4, pg5)]
        growA, growB = Bank(prow_ps), Bank(srow_ps)
        negc_tok = load_ffn_attn_weights(kb, l, ins, wts, g[0])
        v.memset(bu["guard"][:], 0.0)
        for vb in (bu["vv0"], bu["vv1"]):
            v.memset(vb[:].rearrange("p (a h i) -> p a h i", h=8, i=65)[:, :, :, 64:65], 1.0)
        init_tok = kb.VE.inc(v.engine_nop())

        for w in range(8):
            kv_cur = (bu["kh0"], bu["kh1"])[w % 2]
            kv_prv = (bu["kh1"], bu["kh0"])[w % 2]
            v_cur = (bu["vv0"], bu["vv1"])[w % 2]
            v_prv = (bu["vv1"], bu["vv0"])[w % 2]
            sync.wait_ge(kb.GD.h, kb.GD.n)
            sync.wait_ge(kb.TE.h, kb.TE.n)
            ldy = kb.LD.inc16(sync.dma_start(
                bu["yw"][:].rearrange("p (c i) -> p c i", i=512),
                y_sc.ap()[:, :, 1 + w * 512:1 + (w + 1) * 512].transpose([1, 0, 2])))
            ldt = ldy
            for j, A in enumerate(A_dram):
                ldt = kb.LD.inc16(sync.dma_start(
                    bu["tabsrc"][:, j * 512:(j + 1) * 512], A.ap()[:, w * 512:(w + 1) * 512]))
            v.wait_ge(kb.LD.h, ldt)
            v.wait_ge(kb.VE.h, init_tok)
            v.tensor_scalar_mul(bu["tabs"][:, 0:512], bu["tabsrc"][:, 0:512], wts["qscol"][:])
            v.tensor_scalar_mul(bu["tabs"][:, 512:1024], bu["tabsrc"][:, 512:1024], wts["qssh"][:])
            v.tensor_scalar_mul(bu["tabs"][:, 1024:1536], bu["tabsrc"][:, 1024:1536], wts["kscol"][:])
            v.tensor_scalar_mul(bu["tabs"][:, 1536:2048], bu["tabsrc"][:, 1536:2048], wts["kssh"][:])
            for c in range(8):
                v.tensor_copy(bu["ywb"][:, c * 513 + 1:(c + 1) * 513],
                              bu["yw"][:, c * 512:(c + 1) * 512])
                v.tensor_copy(bu["ywb"][:, c * 513:c * 513 + 1], bu["guard"][:, c:c + 1])
            for c in range(8):
                v.tensor_copy(bu["guard"][:, c:c + 1], bu["yw"][:, c * 512 + 511:c * 512 + 512])
            hT = kb.VE.inc(v.engine_nop())

            def hsrc(kc, tokslice=None):
                off = 1 if kc < 4 else 0
                base = kc * 513 + off
                if tokslice is None:
                    return bu["ywb"][:, base:base + 512]
                return bu["ywb"][:, base + tokslice[0]:base + tokslice[1]]

            t.wait_ge(kb.VE.h, hT)
            t.wait_ge(kb.SE.h, negc_tok)
            for oc in range(8):
                par = oc % 2
                pqk = g[par]
                is_q = oc < 4
                c = oc if is_q else oc - 4
                dst = bu["qh"] if is_q else kv_cur
                T1 = bu["tabs"][:, 0:512] if is_q else bu["tabs"][:, 1024:1536]
                T2 = bu["tabs"][:, 512:1024] if is_q else bu["tabs"][:, 1536:2048]
                pqk.guard(t)
                for kc in range(8):
                    mm = t.matmul(pqk.ps[:],
                                  wts["wqkv"][:, kc * 1536 + oc * 128:kc * 1536 + (oc + 1) * 128],
                                  hsrc(kc), start=(kc == 0), stop=(kc == 7))
                mmq = kb.TE.inc(mm)
                s.wait_ge(kb.TE.h, mmq)
                s.wait_ge(kb.VE.h, kb.VE.n)
                s.copy(bu["qn0"][:], pqk.ps[:])
                sq = kb.SE.inc(s.activation(bu["sqf"][:], pqk.ps[:], AF.Square))
                pqk.read_by(kb.SE.h, sq)
                t.wait_ge(kb.SE.h, sq)
                growA.guard(t)
                mms = kb.TE.inc(t.matmul(prow_ps[:], st["ones2f"][:], bu["sqf"][:],
                                         start=True, stop=True))
                g[2].guard(t)
                mrot = kb.TE.inc(t.matmul(g[2].ps[:], st["prot"][:], bu["qn0"][:],
                                          start=True, stop=True))
                s.wait_ge(kb.TE.h, mms)
                sr = kb.SE.inc(s.activation(bu["rowa"][:], prow_ps[:], AF.Sqrt))
                growA.read_by(kb.SE.h, sr)
                v.wait_ge(kb.SE.h, sr)
                v.tensor_scalar_max(bu["rowa"][:], bu["rowa"][:], 1e-12)
                v.drain()
                rr = kb.VE.inc(v.reciprocal(bu["rowb"][:], bu["rowa"][:]))
                t.wait_ge(kb.VE.h, rr)
                g[3].guard(t)
                mb = kb.TE.inc(t.matmul(g[3].ps[:], st["bc2f"][:], bu["rowb"][:],
                                        start=True, stop=True))
                v.wait_ge(kb.TE.h, max(mb, mrot))
                v.tensor_tensor(bu["tt1"][:], g[3].ps[:], T1, ALU.mult)
                g3r = kb.VE.inc(v.tensor_tensor(bu["tt2"][:], g[3].ps[:], T2, ALU.mult))
                g[3].read_by(kb.VE.h, g3r)
                v.drain()
                v.tensor_tensor(bu["h1"][:], bu["qn0"][:], bu["tt1"][:], ALU.mult)
                g2r = kb.VE.inc(v.tensor_tensor(bu["h2"][:], g[2].ps[:], bu["tt2"][:], ALU.mult))
                g[2].read_by(kb.VE.h, g2r)
                v.drain()
                kb.VE.inc(v.tensor_tensor(dst[:, c * 512:(c + 1) * 512],
                                          bu["h1"][:], bu["h2"][:], ALU.add))
            for tt in range(4):
                par = tt % 2
                pvm = g[par]
                pvm.guard(t)
                for kc in range(8):
                    mm = t.matmul(pvm.ps[:], hsrc(kc, (tt * 128, (tt + 1) * 128)),
                                  wts["wqkv"][:, kc * 1536 + 1024:kc * 1536 + 1536],
                                  start=(kc == 0), stop=(kc == 7))
                mmv = kb.TE.inc(mm)
                s.wait_ge(kb.TE.h, mmv)
                vz = kb.SE.inc(s.copy(
                    v_cur[:, tt * 520:(tt + 1) * 520].rearrange("p (h i) -> p h i", i=65)[:, :, 0:64],
                    pvm.ps[:].rearrange("p (h i) -> p h i", i=64)))
                pvm.read_by(kb.SE.h, vz)
            kcs = [kc for kc in range(8) if (w > 0 or kc >= 4)]
            vready = kb.SE.n
            t.wait_ge(kb.SE.h, vready)
            for h in range(8):
                c, sub = h // 2, h % 2
                P0, P1 = sub * 64, sub * 64 + 64
                pvb = g[4 + (h % 2)]
                pvb.guard(t)
                if sub == 1:
                    growB.guard(t)
                last_exp_or_mask = None
                for i, kc in enumerate(kcs):
                    lo, hi = VALID_SPAN[kc]
                    span = hi - lo
                    par = i % 2
                    psc = g[par]
                    kbuf = kv_prv if kc < 4 else kv_cur
                    vbuf = v_prv if kc < 4 else v_cur
                    kcol = (kc % 4) * 128
                    psc.guard(t)
                    mm = t.matmul(psc.ps[:, 0:span],
                                  kbuf[P0:P1, c * 512 + kcol:c * 512 + kcol + 128],
                                  bu["qh"][P0:P1, c * 512 + lo:c * 512 + hi],
                                  start=True, stop=True,
                                  tile_position=(P0, 0))
                    sc_tok = kb.TE.inc(mm)
                    s.wait_ge(kb.TE.h, sc_tok)
                    s.wait_ge(kb.TE.h, kb.TE.n if i >= 2 else 0)
                    ex = kb.SE.inc(s.activation(bu["expb"][:, par * 512 + lo:par * 512 + hi],
                                                psc.ps[:, 0:span], AF.Exp, bias=wts["negC"][:]))
                    psc.read_by(kb.SE.h, ex)
                    dep_sem, dep_val = kb.SE.h, ex
                    if kc < 4:
                        off = par * 512 + kc * 128
                        v.wait_ge(kb.SE.h, ex)
                        mt = kb.VE.inc(v.tensor_mul(bu["expb"][:, off:off + 128],
                                                    bu["expb"][:, off:off + 128], st["maskL"][:]))
                        dep_sem, dep_val = kb.VE.h, mt
                    else:
                        off = par * 512 + (kc - 4) * 128
                        v.wait_ge(kb.SE.h, ex)
                        mt = kb.VE.inc(v.tensor_mul(bu["expb"][:, off:off + 128],
                                                    bu["expb"][:, off:off + 128], st["maskU"][:]))
                        dep_sem, dep_val = kb.VE.h, mt
                    if lo > 0:
                        v.wait_ge(kb.SE.h, ex)
                        mz = kb.VE.inc(v.memset(bu["expb"][:, par * 512:par * 512 + lo], 0.0))
                        dep_sem, dep_val = kb.VE.h, mz
                    if hi < 512:
                        v.wait_ge(kb.SE.h, ex)
                        mz = kb.VE.inc(v.memset(bu["expb"][:, par * 512 + hi:par * 512 + 512], 0.0))
                        dep_sem, dep_val = kb.VE.h, mz
                    t.wait_ge(dep_sem, dep_val)
                    vbase = (kc % 4) * 520 + h * 65
                    eall = bu["expb"][:, par * 512:par * 512 + 512]
                    if sub == 0:
                        mm2 = t.matmul(pvb.ps[0:65, 0:512], vbuf[:, vbase:vbase + 65],
                                       eall, start=(i == 0), stop=(i == len(kcs) - 1),
                                       skip_group_check=True)
                        pv_tok = kb.TE.inc(mm2)
                    else:
                        mm2 = t.matmul(pvb.ps[64:128, 0:512], vbuf[:, vbase:vbase + 64],
                                       eall, start=(i == 0), stop=(i == len(kcs) - 1),
                                       skip_group_check=True, tile_position=(0, 64))
                        mm3 = t.matmul(srow_ps[0:1, 0:512], st["onesallb"][:, 0:1],
                                       eall, start=(i == 0), stop=(i == len(kcs) - 1),
                                       skip_group_check=True)
                        pv_tok = kb.TE.inc(mm3)
                v.wait_ge(kb.TE.h, pv_tok)
                if sub == 0:
                    rc = kb.VE.inc(v.reciprocal(bu["rowr"][64:65, :], pvb.ps[64:65, :]))
                    t.wait_ge(kb.VE.h, rc)
                    g[2].guard(t)
                    mb2 = kb.TE.inc(t.matmul(g[2].ps[0:64, :], st["onesall"][64:65, 0:64],
                                             bu["rowr"][64:65, :], start=True, stop=True))
                    s.wait_ge(kb.TE.h, mb2)
                    sc2 = kb.SE.inc(s.copy(bu["bcf"][0:64, :], g[2].ps[0:64, :]))
                    g[2].read_by(kb.SE.h, sc2)
                    v.wait_ge(kb.SE.h, sc2)
                    ord_ = kb.VE.inc(v.tensor_tensor(
                        bu["oT"][0:64, c * 512:(c + 1) * 512],
                        pvb.ps[0:64, :], bu["bcf"][0:64, :], ALU.mult))
                else:
                    rc = kb.VE.inc(v.reciprocal(bu["rowr"][0:1, :], srow_ps[0:1, :]))
                    growB.read_by(kb.VE.h, rc)
                    t.wait_ge(kb.VE.h, rc)
                    g[2].guard(t)
                    mb2 = kb.TE.inc(t.matmul(g[2].ps[64:128, :], st["onesall"][0:1, 0:64],
                                             bu["rowr"][0:1, :], start=True, stop=True,
                                             tile_position=(0, 64)))
                    s.wait_ge(kb.TE.h, mb2)
                    sc2 = kb.SE.inc(s.copy(bu["bcf"][64:128, :], g[2].ps[64:128, :]))
                    g[2].read_by(kb.SE.h, sc2)
                    v.wait_ge(kb.SE.h, sc2)
                    ord_ = kb.VE.inc(v.tensor_tensor(
                        bu["oT"][64:128, c * 512:(c + 1) * 512],
                        pvb.ps[64:128, :], bu["bcf"][64:128, :], ALU.mult))
                pvb.read_by(kb.VE.h, ord_)
            oT_ready = kb.VE.n
            t.wait_ge(kb.VE.h, oT_ready)
            for oc in range(8):
                par = oc % 2
                pw = g[par]
                pw.guard(t)
                for ic in range(4):
                    mm = t.matmul(pw.ps[:],
                                  wts["wo"][:, ic * 1024 + oc * 128:ic * 1024 + (oc + 1) * 128],
                                  bu["oT"][:, ic * 512:(ic + 1) * 512],
                                  start=(ic == 0), stop=(ic == 3))
                mmw = kb.TE.inc(mm)
                v.wait_ge(kb.TE.h, mmw)
                yu = kb.VE.inc(v.tensor_add(bu["yw"][:, oc * 512:(oc + 1) * 512],
                                            bu["yw"][:, oc * 512:(oc + 1) * 512], pw.ps[:]))
                pw.read_by(kb.VE.h, yu)
            gp.wait_ge(kb.VE.h, kb.VE.n)
            kb.GD.inc16(gp.dma_start(
                y_sc.ap()[:, :, 1 + w * 512:1 + (w + 1) * 512].transpose([1, 0, 2]),
                bu["yw"][:].rearrange("p (c i) -> p c i", i=512)))
        gp.wait_ge(kb.GD.h, kb.GD.n)
        done = kb.GD.n
    for e in (kb.sync, kb.t, kb.v, kb.s):
        e.wait_ge(kb.GD.h, done)



W1CH = [(wc * 512, min(2730, wc * 512 + 512)) for wc in range(6)]


def emit_ffn_mlp(kb, l, y_sc, ins):
    import contextlib
    nc, sync, gp, t, v, s = kb.nc, kb.sync, kb.gp, kb.t, kb.v, kb.s
    st = kb.st
    es = contextlib.ExitStack()
    u = kb.uid()
    a = lambda n, sh, dt: es.enter_context(nc.sbuf_tensor(f"mlp{u}_" + n, sh, dt))
    w1 = a("w1", [128, 8 * 5460], BF16)
    w2 = a("w2", [128, 22 * 1024], BF16)
    b1a = a("b1a", [128, 22], F32)
    b1g = a("b1g", [128, 22], F32)
    lg = a("lg", [128, 8], F32)
    lb = a("lb", [128, 8], F32)
    b2c = a("b2c", [128, 8], F32)
    ytg = a("ytg", [128, 8 * 257], F32)
    hsq = a("hsq", [128, 8 * 257], F32)
    hT = a("hT", [128, 8 * 256], BF16)
    guard = a("guard", [128, 8], F32)
    rows = a("rows", [1, 1024], F32)
    gel = a("gel", [128, 256], F32)
    abuf = a("abuf", [128, 256], F32)
    mT = a("mT", [128, 4 * 256], BF16)
    with es, \
         nc.psum_tensor(f"mlp_y0{u}", [128, 512], F32) as py0, \
         nc.psum_tensor(f"mlp_y1{u}", [128, 512], F32) as py1, \
         nc.psum_tensor(f"mlp_y2{u}", [128, 512], F32) as py2, \
         nc.psum_tensor(f"mlp_y3{u}", [128, 512], F32) as py3, \
         nc.psum_tensor(f"mlp_a{u}", [128, 512], F32) as pa, \
         nc.psum_tensor(f"mlp_g{u}", [128, 512], F32) as pg, \
         nc.psum_tensor(f"mlp_r{u}", [1, 512], F32) as pr, \
         nc.psum_tensor(f"mlp_bc{u}", [128, 512], F32) as pbc:
        py = [Bank(p) for p in (py0, py1, py2, py3)]
        ba, bg, brow, bbc = Bank(pa), Bank(pg), Bank(pr), Bank(pbc)
        for kc in range(8):
            kb.GD.inc16(gp.dma_start(w1[:, kc * 5460:(kc + 1) * 5460],
                                     ins["L0_ffn_w1"].ap()[l, kc * 128:(kc + 1) * 128, :]))
        for rc in range(21):
            kb.GD.inc16(gp.dma_start(w2[:, rc * 1024:(rc + 1) * 1024],
                                     ins["L0_ffn_w2"].ap()[l, rc * 128:(rc + 1) * 128, :]))
        kb.GD.inc16(gp.dma_start(w2[0:42, 21 * 1024:22 * 1024],
                                 ins["L0_ffn_w2"].ap()[l, 2688:2730, :]))
        b1 = ins["L0_ffn_b1"]
        ncd = nc.allow_non_contiguous_dma(reason="small bias-column loads")
        ncd.__enter__()
        kb.LD.inc16(sync.dma_start(b1a[:, 0:21],
                                   b1.ap()[l, 0:2688].rearrange("(c p) -> p c", p=128)))
        kb.LD.inc16(sync.dma_start(b1a[0:42, 21:22], b1.ap()[l, 2688:2730].unsqueeze(1)))
        kb.LD.inc16(sync.dma_start(b1g[:, 0:21],
                                   b1.ap()[l, 2730:5418].rearrange("(c p) -> p c", p=128)))
        kb.LD.inc16(sync.dma_start(b1g[0:42, 21:22], b1.ap()[l, 5418:5460].unsqueeze(1)))
        kb.LD.inc16(sync.dma_start(lg[:], ins["L0_ffn_ln_g"].ap()[l, :].rearrange("(c p) -> p c", p=128)))
        kb.LD.inc16(sync.dma_start(lb[:], ins["L0_ffn_ln_b"].ap()[l, :].rearrange("(c p) -> p c", p=128)))
        wtok = kb.LD.inc16(sync.dma_start(b2c[:], ins["L0_ffn_b2"].ap()[l, :].rearrange("(c p) -> p c", p=128)))
        ncd.__exit__(None, None, None)
        gtok = kb.GD.n
        v.memset(guard[:], 0.0)
        kb.VE.inc(v.engine_nop())
        t.wait_ge(kb.LD.h, wtok)
        t.wait_ge(kb.GD.h, gtok)
        s.wait_ge(kb.LD.h, wtok)

        for tc in range(16):
            sync.wait_ge(kb.GD.h, kb.GD.n)
            sync.wait_ge(kb.TE.h, kb.TE.n)
            sync.wait_ge(kb.VE.h, kb.VE.n)
            ldy = kb.LD.inc16(sync.dma_start(
                ytg[:].rearrange("p (c i) -> p c i", i=257)[:, :, 1:257],
                y_sc.ap()[:, :, 1 + tc * 256:1 + (tc + 1) * 256].transpose([1, 0, 2])))
            v.wait_ge(kb.LD.h, ldy)
            v.wait_ge(kb.SE.h, kb.SE.n)
            for c in range(8):
                v.tensor_copy(ytg[:, c * 257:c * 257 + 1], guard[:, c:c + 1])
            for c in range(8):
                v.tensor_copy(guard[:, c:c + 1], ytg[:, c * 257 + 256:c * 257 + 257])
            for c in range(8):
                v.tensor_mul(hsq[:, c * 257:(c + 1) * 257], ytg[:, c * 257:(c + 1) * 257],
                             ytg[:, c * 257:(c + 1) * 257])
            hready = kb.VE.inc(v.engine_nop())

            def ysrc(buf, kc):
                base = kc * 257 + (1 if kc < 4 else 0)
                return buf[:, base:base + 256]

            t.wait_ge(kb.VE.h, hready)
            brow.guard(t)
            for kc in range(8):
                t.matmul(pr[0:1, 0:256], st["onesall"][:, 0:1], ysrc(ytg, kc),
                         start=(kc == 0), stop=(kc == 7), skip_group_check=True)
            for kc in range(8):
                mm = t.matmul(pr[0:1, 256:512], st["onesall"][:, 0:1], ysrc(hsq, kc),
                              start=False, stop=(kc == 7), skip_group_check=True)
            mmr = kb.TE.inc(mm)
            v.wait_ge(kb.TE.h, mmr)
            v.tensor_scalar(rows[0:1, 0:256], pr[0:1, 0:256], 1.0 / D, None, ALU.mult)
            v.tensor_scalar(rows[0:1, 256:512], pr[0:1, 256:512], 1.0 / D, None, ALU.mult)
            rowdone0 = kb.VE.inc(v.engine_nop())
            brow.read_by(kb.VE.h, rowdone0)
            v.drain()
            v.tensor_mul(rows[0:1, 512:768], rows[0:1, 0:256], rows[0:1, 0:256])
            v.drain()
            vv = kb.VE.inc(v.tensor_sub(rows[0:1, 256:512], rows[0:1, 256:512], rows[0:1, 512:768]))
            s.wait_ge(kb.VE.h, vv)
            sr = kb.SE.inc(s.activation(rows[0:1, 512:768], rows[0:1, 256:512], AF.Sqrt,
                                        bias=kb.st["epscol"][0:1, 0:1]))
            v.wait_ge(kb.SE.h, sr)
            v.reciprocal(rows[0:1, 256:512], rows[0:1, 512:768])
            v.drain()
            nm = kb.VE.inc(v.scalar_tensor_tensor(rows[0:1, 512:768], rows[0:1, 0:256], -1.0,
                                                  rows[0:1, 256:512], ALU.mult, ALU.mult))
            t.wait_ge(kb.VE.h, nm)
            bbc.guard(t)
            t.matmul(pbc[0:64, 0:256], st["onesall"][0:1, 0:64], rows[0:1, 256:512],
                     start=True, stop=False, skip_group_check=True)
            t.matmul(pbc[64:128, 0:256], st["onesall"][0:1, 0:64], rows[0:1, 256:512],
                     start=True, stop=False, skip_group_check=True, tile_position=(0, 64))
            t.matmul(pbc[0:64, 256:512], st["onesall"][0:1, 0:64], rows[0:1, 512:768],
                     start=False, stop=False, skip_group_check=True)
            mmb = kb.TE.inc(t.matmul(pbc[64:128, 256:512], st["onesall"][0:1, 0:64],
                                     rows[0:1, 512:768], start=False, stop=True,
                                     skip_group_check=True, tile_position=(0, 64)))
            v.wait_ge(kb.TE.h, mmb)
            htoks = []
            for kc in range(8):
                v.tensor_tensor(hsq[:, kc * 257:kc * 257 + 256], ysrc(ytg, kc),
                                pbc[:, 0:256], ALU.mult)
                v.drain()
                v.tensor_add(hsq[:, kc * 257:kc * 257 + 256],
                             hsq[:, kc * 257:kc * 257 + 256], pbc[:, 256:512])
                v.drain()
                htoks.append(kb.VE.inc(v.tensor_scalar(hT[:, kc * 256:(kc + 1) * 256],
                                                       hsq[:, kc * 257:kc * 257 + 256],
                                                       lg[:, kc:kc + 1], lb[:, kc:kc + 1],
                                                       ALU.mult, ALU.add)))
            bbc.read_by(kb.VE.h, kb.VE.n)
            hTtok = htoks[-1]
            t.wait_ge(kb.VE.h, hTtok)
            for wc, (c0, c1) in enumerate(W1CH):
                nsub = (c1 - c0 + 127) // 128
                for sub in range(nsub):
                    s0 = c0 + sub * 128
                    s1 = min(c1, s0 + 128)
                    wdt = s1 - s0
                    rc = s0 // 128
                    ba.guard(t)
                    for kc in range(8):
                        t.matmul(pa[0:wdt, 0:256],
                                 w1[:, kc * 5460 + s0:kc * 5460 + s1],
                                 hT[:, kc * 256:(kc + 1) * 256],
                                 start=(kc == 0), stop=(kc == 7))
                    bg.guard(t)
                    for kc in range(8):
                        mmg = t.matmul(pg[0:wdt, 0:256],
                                       w1[:, kc * 5460 + 2730 + s0:kc * 5460 + 2730 + s1],
                                       hT[:, kc * 256:(kc + 1) * 256],
                                       start=(kc == 0), stop=(kc == 7))
                    mmg = kb.TE.inc(mmg)
                    s.wait_ge(kb.TE.h, mmg)
                    sg = kb.SE.inc(s.activation(gel[0:wdt, :], pg[0:wdt, 0:256], AF.Gelu,
                                                bias=b1g[0:wdt, rc:rc + 1]))
                    bg.read_by(kb.SE.h, sg)
                    v.wait_ge(kb.SE.h, sg)
                    v.tensor_scalar(abuf[0:wdt, :], pa[0:wdt, 0:256], b1a[0:wdt, rc:rc + 1],
                                    None, ALU.add)
                    v.drain()
                    av = kb.VE.inc(v.tensor_mul(mT[0:wdt, (sub % 4) * 256:(sub % 4) * 256 + 256],
                                                abuf[0:wdt, :], gel[0:wdt, :]))
                    ba.read_by(kb.VE.h, av)
                    t.wait_ge(kb.VE.h, av)
                    first = (wc == 0 and sub == 0)
                    last = (wc == 5 and sub == nsub - 1)
                    for oc in range(8):
                        bank, half = py[oc // 2], (oc % 2) * 256
                        if first:
                            bank.guard(t)
                        mm2 = t.matmul(bank.ps[:, half:half + 256],
                                       w2[0:wdt, rc * 1024 + oc * 128:rc * 1024 + (oc + 1) * 128],
                                       mT[0:wdt, (sub % 4) * 256:(sub % 4) * 256 + 256],
                                       start=(first and oc % 2 == 0), stop=last,
                                       skip_group_check=True)
                    kb.TE.inc(mm2)
            y2done = kb.TE.n
            v.wait_ge(kb.TE.h, y2done)
            for oc in range(8):
                bank, half = py[oc // 2], (oc % 2) * 256
                yo = kb.VE.inc(v.scalar_tensor_tensor(
                    ytg[:, oc * 257 + 1:oc * 257 + 257], bank.ps[:, half:half + 256],
                    b2c[:, oc:oc + 1], ytg[:, oc * 257 + 1:oc * 257 + 257],
                    ALU.add, ALU.add))
                bank.read_by(kb.VE.h, yo)
            gp.wait_ge(kb.VE.h, kb.VE.n)
            kb.GD.inc16(gp.dma_start(
                y_sc.ap()[:, :, 1 + tc * 256:1 + (tc + 1) * 256].transpose([1, 0, 2]),
                ytg[:].rearrange("p (c i) -> p c i", i=257)[:, :, 1:257]))
        gp.wait_ge(kb.GD.h, kb.GD.n)
        done = kb.GD.n
    for e in (kb.sync, kb.t, kb.v, kb.s):
        e.wait_ge(kb.GD.h, done)



def emit_weight_quant(kb, wdram, kcs, wq, stage, fp8st, qcols, blocks):
    sync, v = kb.sync, kb.v
    N = blocks * 128
    for kc in range(kcs):
        ldw = kb.LD.inc16(sync.dma_start(stage[:, 0:N], wdram.ap()[kc * 128:(kc + 1) * 128, :]))
        v.wait_ge(kb.LD.h, ldw)
        tok = kb.VE.inc(emit_quant(kb, stage[:, 0:N], stage[:, 0:N], fp8st,
                                   wq[:, kc * N:(kc + 1) * N], qcols, blocks))
        sync.wait_ge(kb.VE.h, tok)
    return kb.VE.n


def emit_dense_block(kb, x0_src, y_sc, x1_sc, x2_sc, qT_sc, ins):
    import contextlib
    nc, sync, gp, t, v, s = kb.nc, kb.sync, kb.gp, kb.t, kb.v, kb.s
    st = kb.st
    es = contextlib.ExitStack()
    a = lambda n, sh, dt: es.enter_context(nc.sbuf_tensor("dn_" + n, sh, dt))
    kT = a("kT", [128, 8 * 4096], BF16)
    vres = a("vres", [128, 32 * 1040], BF16)
    bcols = a("bcols", [128, 24], F32)
    obias = a("obias", [128, 8], F32)
    vbrow = a("vbrow", [1, 1024], BF16)
    ncq = a("ncq", [2, 26], F32)
    negrow = a("negrow", [1, 16], F32)
    negc16 = a("negc16", [128, 16], F32)
    qcolsw = a("qcolsw", [128, 72], F32)
    with es:
        es1 = contextlib.ExitStack()
        b = lambda n, sh, dt: es1.enter_context(nc.sbuf_tensor("b1_" + n, sh, dt))
        wq = b("wq", [128, 8 * 3072], BF16)
        fp8s = b("fp8s", [128, 3072], FP8)
        with nc.sbuf_tensor("b1_wst", [128, 3072], F32) as wst:
            emit_weight_quant(kb, ins["L0_attn_qkv_kernel"], 8, wq, wst,
                              fp8s[:, 0:3072], qcolsw, 24)
        xt = b("xt", [128, 1024], F32)
        zt = b("zt", [128, 1024], F32)
        sqt = b("sqt", [128, 1024], F32)
        zdq = b("zdq", [128, 1024], BF16)
        zdqT = b("zdqT", [128, 1024], BF16)
        qst = b("qst", [128, 8 * 128], BF16)
        cols = b("cols", [128, 16], F32)
        q2f = b("q2f", [128, 128], F32)
        with es1, \
             nc.psum_tensor("b1_p0", [128, 512], F32) as p0, \
             nc.psum_tensor("b1_p1", [128, 512], F32) as p1, \
             nc.psum_tensor("b1_p2", [128, 512], F32) as p2, \
             nc.psum_tensor("b1_p3", [128, 512], F32) as p3, \
             nc.psum_tensor("b1_pn", [128, 1024], F32) as pn, \
             nc.psum_tensor("b1_pb", [128, 1024], BF16) as pb, \
             nc.psum_tensor("b1_pr", [2, 512], F32) as pr:
            bk = [Bank(p) for p in (p0, p1, p2, p3)]
            bn, bb, br = Bank(pn), Bank(pb), Bank(pr)
            with nc.allow_non_contiguous_dma(reason="small bias cols"):
                kb.LD.inc16(sync.dma_start(
                    bcols[:], ins["L0_attn_qkv_bias"].ap().rearrange("(c p) -> p c", p=128)))
                btok = kb.LD.inc16(sync.dma_start(
                    obias[:], ins["L0_attn_out_bias"].ap().rearrange("(c p) -> p c", p=128)))
            kb.GD.inc16(gp.dma_start(vbrow[:], ins["L0_attn_qkv_bias"].ap()[2048:3072].unsqueeze(0)))
            v.memset(ncq[:], 0.0)
            vo = kb.VE.inc(v.memset(
                vres[:].rearrange("p (a h i) -> p a h i", h=16, i=65)[:, :, :, 64:65], 1.0))
            s.wait_ge(kb.LD.h, btok)
            t.wait_ge(kb.VE.h, vo)
            t.wait_ge(kb.GD.h, kb.GD.n)
            for tc in range(32):
                sync.wait_ge(kb.TE.h, kb.TE.n)
                sync.wait_ge(kb.VE.h, kb.VE.n)
                sync.wait_ge(kb.GD.h, kb.GD.n)
                ldx = kb.LD.inc16(sync.dma_start(xt[:], x0_src.ap()[tc * 128:(tc + 1) * 128, :]))
                ldy = kb.LD.inc16(sync.dma_start(
                    zt[:].rearrange("p (c i) -> p c i", i=128),
                    y_sc.ap()[:, :, 1 + tc * 128:1 + (tc + 1) * 128].transpose([1, 0, 2])))
                t.wait_ge(kb.LD.h, ldy)
                bn.guard(t)
                for c in range(8):
                    mm = t.transpose(pn[:, c * 128:(c + 1) * 128], zt[:, c * 128:(c + 1) * 128],
                                     st["identf"][:])
                ty = kb.TE.inc(mm)
                v.wait_ge(kb.TE.h, ty)
                v.wait_ge(kb.LD.h, ldx)
                v.wait_ge(kb.SE.h, kb.SE.n)
                x1done = kb.VE.inc(v.scalar_tensor_tensor(xt[:], pn[:], 0.5, xt[:],
                                                          ALU.mult, ALU.add))
                bn.read_by(kb.VE.h, x1done)
                gp.wait_ge(kb.VE.h, x1done)
                kb.GD.inc16(gp.dma_start(x1_sc.ap()[tc * 128:(tc + 1) * 128, :], xt[:]))
                s.wait_ge(kb.VE.h, x1done)
                emit_ln_natural(kb, xt[:], zt[:], cols, sqt)
                qd = kb.VE.inc(emit_quant(kb, zt[:], sqt[:], fp8s[:, 0:1024], zdq[:],
                                          qcolsw, 8))
                t.wait_ge(kb.VE.h, qd)
                bb.guard(t)
                for c in range(8):
                    mm = t.transpose(pb[:, c * 128:(c + 1) * 128],
                                     zdq[:, c * 128:(c + 1) * 128], st["identb"][:])
                tz = kb.TE.inc(mm)
                s.wait_ge(kb.TE.h, tz)
                ztok = kb.SE.inc(s.copy(zdqT[:], pb[:]))
                bb.read_by(kb.SE.h, ztok)
                t.wait_ge(kb.SE.h, ztok)
                for oc in range(16):
                    par = oc % 2
                    pq = bk[par]
                    pq.guard(t)
                    for c in range(8):
                        mm = t.matmul(pq.ps[:, 0:128],
                                      wq[:, c * 3072 + oc * 128:c * 3072 + (oc + 1) * 128],
                                      zdqT[:, c * 128:(c + 1) * 128],
                                      start=(c == 0), stop=(c == 7))
                    mq = kb.TE.inc(mm)
                    is_q = oc < 8
                    dst = (qst[:, oc * 128:(oc + 1) * 128] if is_q else
                           kT[:, (oc - 8) * 4096 + tc * 128:(oc - 8) * 4096 + (tc + 1) * 128])
                    v.wait_ge(kb.TE.h, mq)
                    ev = kb.VE.inc(v.tensor_scalar(dst, pq.ps[:, 0:128],
                                                   bcols[:, oc:oc + 1], None, ALU.add))
                    pq.read_by(kb.VE.h, ev)
                    s.wait_ge(kb.VE.h, ev)
                    sq2 = kb.SE.inc(s.activation(q2f[:], dst, AF.Square))
                    t.wait_ge(kb.SE.h, sq2)
                    br.guard(t)
                    ms = kb.TE.inc(t.matmul(pr[:, 0:128], st["ones2f"][:], q2f[:],
                                            start=True, stop=True))
                    v.wait_ge(kb.TE.h, ms)
                    v.tensor_reduce(ncq[0:2, 24:25], pr[:, 0:128], AX.X, ALU.max)
                    v.drain()
                    base = 0 if is_q else 8
                    cidx = base + (oc % 8)
                    mx = kb.VE.inc(v.tensor_max(ncq[0:2, cidx:cidx + 1],
                                                ncq[0:2, cidx:cidx + 1], ncq[0:2, 24:25]))
                    br.read_by(kb.VE.h, mx)
                    v.drain()
                gp.wait_ge(kb.SE.h, kb.SE.n)
                kb.GD.inc16(gp.dma_start(
                    qT_sc.ap()[:, :, tc * 128:(tc + 1) * 128].transpose([1, 0, 2]),
                    qst[:].rearrange("p (c i) -> p c i", i=128)))
                for vc in range(2):
                    pv = bk[2 + vc]
                    pv.guard(t)
                    for c in range(8):
                        t.matmul(pv.ps[:], zdqT[:, c * 128:(c + 1) * 128],
                                 wq[:, c * 3072 + 2048 + vc * 512:c * 3072 + 2048 + (vc + 1) * 512],
                                 start=(c == 0), stop=False)
                    mmv = t.matmul(pv.ps[:], st["onesrowb"][:, :],
                                   vbrow[:, vc * 512:(vc + 1) * 512],
                                   start=False, stop=True)
                    mvt = kb.TE.inc(mmv)
                    s.wait_ge(kb.TE.h, mvt)
                    vz = kb.SE.inc(s.copy(
                        vres[:, tc * 1040 + vc * 520:tc * 1040 + (vc + 1) * 520]
                        .rearrange("p (h i) -> p h i", i=65)[:, :, 0:64],
                        pv.ps[:].rearrange("p (h i) -> p h i", i=64)))
                    pv.read_by(kb.SE.h, vz)
            v.wait_ge(kb.SE.h, kb.SE.n)
            s.wait_ge(kb.VE.h, kb.VE.n)
            sqn = kb.SE.inc(s.activation(ncq[0:2, 16:24], ncq[0:2, 0:8], AF.Sqrt))
            s.drain()
            sqk = kb.SE.inc(s.activation(ncq[0:2, 0:8], ncq[0:2, 8:16], AF.Sqrt))
            v.wait_ge(kb.SE.h, sqk)
            v.tensor_mul(ncq[0:2, 8:16], ncq[0:2, 16:24], ncq[0:2, 0:8])
            v.drain()
            cn = kb.VE.inc(v.tensor_scalar(ncq[0:2, 16:24], ncq[0:2, 8:16],
                                           -0.1255, None, ALU.mult))
            gp.wait_ge(kb.VE.h, cn)
            with nc.allow_non_contiguous_dma(reason="tiny negC shuffle"):
                gp.dma_start(negrow[:].rearrange("a (c p) -> a c p", p=2)[:, :, 0:1],
                             ncq[0:1, 16:24]).then_inc(kb.GD.h, 16)
                kb.GD.n += 16
                ng = kb.GD.inc16(gp.dma_start(
                    negrow[:].rearrange("a (c p) -> a c p", p=2)[:, :, 1:2],
                    ncq[1:2, 16:24]))
            t.wait_ge(kb.GD.h, ng)
            bn.guard(t)
            mmn = kb.TE.inc(t.matmul(pn[:, 0:16], st["onesall"][0:1, :], negrow[:],
                                     start=True, stop=True))
            s.wait_ge(kb.TE.h, mmn)
            nct = kb.SE.inc(s.copy(negc16[:], pn[:, 0:16]))
            bn.read_by(kb.SE.h, nct)
            for e in (kb.sync, kb.t, kb.v):
                e.wait_ge(kb.SE.h, kb.SE.n)
            fin1 = kb.GD.n
            gp.wait_ge(kb.GD.h, fin1)
        es2 = contextlib.ExitStack()
        b2 = lambda n, sh, dt: es2.enter_context(nc.sbuf_tensor("b2_" + n, sh, dt))
        wout = b2("wout", [128, 8 * 1024], BF16)
        fp8s = b2("fp8s2", [128, 1024], FP8)
        with nc.sbuf_tensor("b2_wst2", [128, 1024], F32) as wst2:
            emit_weight_quant(kb, ins["L0_attn_out_kernel"], 8, wout, wst2,
                              fp8s[:, 0:1024], qcolsw, 8)
        qT = b2("qT", [128, 8 * 512], BF16)
        expb = b2("expb", [128, 4 * 512], BF16)
        oT = b2("oT", [128, 8 * 512], BF16)
        odqT = b2("odqT", [128, 8 * 512], BF16)
        oNat = b2("oNat", [128, 1024], BF16)
        odq = b2("odq", [128, 1024], BF16)
        outT = b2("outT", [128, 8 * 512], BF16)
        x1t = b2("x1t", [128, 1024], F32)
        z3 = b2("z3", [128, 1024], F32)
        sq3 = b2("sq3", [128, 1024], F32)
        x2t = x1t
        ys3 = sq3
        rowr = b2("rowr", [128, 512], F32)
        bcf = b2("bcf", [128, 512], F32)
        cols2 = b2("cols2", [128, 16], F32)
        with es2, \
             nc.psum_tensor("b2_a0", [128, 512], F32) as pa0, \
             nc.psum_tensor("b2_a1", [128, 512], F32) as pa1, \
             nc.psum_tensor("b2_b0", [128, 512], F32) as pb0, \
             nc.psum_tensor("b2_b1", [128, 512], F32) as pb1, \
             nc.psum_tensor("b2_v0", [128, 512], F32) as pv0, \
             nc.psum_tensor("b2_v1", [128, 512], F32) as pv1, \
             nc.psum_tensor("b2_sr", [1, 512], F32) as psr, \
             nc.psum_tensor("b2_bT", [128, 1024], BF16) as pbT:
            bA = [Bank(pa0), Bank(pa1)]
            bB = [Bank(pb0), Bank(pb1)]
            bpA, bpB, bsr, bbt = Bank(pv0), Bank(pv1), Bank(psr), Bank(pbT)
            t.wait_ge(kb.VE.h, kb.VE.n)
            for qc in range(8):
                sync.wait_ge(kb.TE.h, kb.TE.n)
                ldq = kb.LD.inc16(sync.dma_start(
                    qT[:].rearrange("p (c i) -> p c i", i=512),
                    qT_sc.ap()[:, :, qc * 512:(qc + 1) * 512].transpose([1, 0, 2])))
                t.wait_ge(kb.LD.h, ldq)
                for hp in range(8):
                    h0, h1 = 2 * hp, 2 * hp + 1
                    bpA.guard(t)
                    bpB.guard(t)
                    bsr.guard(t)
                    for kc in range(32):
                        par = kc % 2
                        pA, pB = bA[par], bB[par]
                        pA.guard(t)
                        mmA = t.matmul(pA.ps[:], kT[0:64, hp * 4096 + kc * 128:hp * 4096 + (kc + 1) * 128],
                                       qT[0:64, hp * 512:(hp + 1) * 512],
                                       start=True, stop=True)
                        pB.guard(t)
                        mmB = t.matmul(pB.ps[:], kT[64:128, hp * 4096 + kc * 128:hp * 4096 + (kc + 1) * 128],
                                       qT[64:128, hp * 512:(hp + 1) * 512],
                                       start=True, stop=True, tile_position=(64, 0))
                        mmBt = kb.TE.inc(mmB)
                        s.wait_ge(kb.TE.h, mmBt)
                        exA = kb.SE.inc(s.activation(expb[:, (par * 2) * 512:(par * 2) * 512 + 512],
                                                     pA.ps[:], AF.Exp, bias=negc16[:, h0:h0 + 1],
                                                     scale=0.125))
                        pA.read_by(kb.SE.h, exA)
                        exB = kb.SE.inc(s.activation(expb[:, (par * 2 + 1) * 512:(par * 2 + 1) * 512 + 512],
                                                     pB.ps[:], AF.Exp, bias=negc16[:, h1:h1 + 1],
                                                     scale=0.125))
                        pB.read_by(kb.SE.h, exB)
                        t.wait_ge(kb.SE.h, exB)
                        eA = expb[:, (par * 2) * 512:(par * 2) * 512 + 512]
                        eB = expb[:, (par * 2 + 1) * 512:(par * 2 + 1) * 512 + 512]
                        t.matmul(pv0[0:65, :], vres[:, kc * 1040 + h0 * 65:kc * 1040 + h0 * 65 + 65],
                                 eA, start=(kc == 0), stop=(kc == 31), skip_group_check=True)
                        t.matmul(pv1[64:128, :], vres[:, kc * 1040 + h1 * 65:kc * 1040 + h1 * 65 + 64],
                                 eB, start=(kc == 0), stop=(kc == 31), skip_group_check=True,
                                 tile_position=(0, 64))
                        pvt = kb.TE.inc(t.matmul(psr[0:1, :], st["onesallb"][:, 0:1], eB,
                                                 start=(kc == 0), stop=(kc == 31),
                                                 skip_group_check=True))
                    v.wait_ge(kb.TE.h, pvt)
                    rc0 = kb.VE.inc(v.reciprocal(rowr[64:65, :], pv0[64:65, :]))
                    t.wait_ge(kb.VE.h, rc0)
                    bA[0].guard(t)
                    mb0 = kb.TE.inc(t.matmul(pa0[0:64, :], st["onesall"][64:65, 0:64],
                                             rowr[64:65, :], start=True, stop=True))
                    s.wait_ge(kb.TE.h, mb0)
                    sc0 = kb.SE.inc(s.copy(bcf[0:64, :], pa0[0:64, :]))
                    bA[0].read_by(kb.SE.h, sc0)
                    v.wait_ge(kb.SE.h, sc0)
                    o0 = kb.VE.inc(v.tensor_tensor(oT[0:64, hp * 512:(hp + 1) * 512],
                                                   pv0[0:64, :], bcf[0:64, :], ALU.mult))
                    bpA.read_by(kb.VE.h, o0)
                    rc1 = kb.VE.inc(v.reciprocal(rowr[0:1, :], psr[0:1, :]))
                    bsr.read_by(kb.VE.h, rc1)
                    t.wait_ge(kb.VE.h, rc1)
                    bA[1].guard(t)
                    mb1 = kb.TE.inc(t.matmul(pa1[64:128, :], st["onesall"][0:1, 0:64],
                                             rowr[0:1, :], start=True, stop=True,
                                             tile_position=(0, 64)))
                    s.wait_ge(kb.TE.h, mb1)
                    sc1 = kb.SE.inc(s.copy(bcf[64:128, :], pa1[64:128, :]))
                    bA[1].read_by(kb.SE.h, sc1)
                    v.wait_ge(kb.SE.h, sc1)
                    o1 = kb.VE.inc(v.tensor_tensor(oT[64:128, hp * 512:(hp + 1) * 512],
                                                   pv1[64:128, :], bcf[64:128, :], ALU.mult))
                    bpB.read_by(kb.VE.h, o1)
                for tt in range(4):
                    oTready = kb.VE.n
                    t.wait_ge(kb.VE.h, oTready)
                    bbt.guard(t)
                    for c in range(8):
                        mm = t.transpose(pbT[:, c * 128:(c + 1) * 128],
                                         oT[:, c * 512 + tt * 128:c * 512 + (tt + 1) * 128],
                                         st["identb"][:])
                    to1 = kb.TE.inc(mm)
                    s.wait_ge(kb.TE.h, to1)
                    so1 = kb.SE.inc(s.copy(oNat[:], pbT[:]))
                    bbt.read_by(kb.SE.h, so1)
                    v.wait_ge(kb.SE.h, so1)
                    qo = kb.VE.inc(emit_quant(kb, oNat[:], sq3[:], fp8s[:, 0:1024],
                                              odq[:], qcolsw, 8))
                    t.wait_ge(kb.VE.h, qo)
                    bbt.guard(t)
                    for c in range(8):
                        mm = t.transpose(pbT[:, c * 128:(c + 1) * 128],
                                         odq[:, c * 128:(c + 1) * 128], st["identb"][:])
                    to2 = kb.TE.inc(mm)
                    s.wait_ge(kb.TE.h, to2)
                    so2 = kb.SE.inc(s.copy(
                        odqT[:].rearrange("p (c i) -> p c i", i=512)[:, :, tt * 128:(tt + 1) * 128],
                        pbT[:].rearrange("p (c i) -> p c i", i=128)))
                    bbt.read_by(kb.SE.h, so2)
                odqready = kb.SE.n
                t.wait_ge(kb.SE.h, odqready)
                for oc in range(8):
                    par = oc % 2
                    pO = bA[par]
                    pO.guard(t)
                    for c in range(8):
                        mm = t.matmul(pO.ps[:], wout[:, c * 1024 + oc * 128:c * 1024 + (oc + 1) * 128],
                                      odqT[:, c * 512:(c + 1) * 512],
                                      start=(c == 0), stop=(c == 7))
                    mo = kb.TE.inc(mm)
                    v.wait_ge(kb.TE.h, mo)
                    ot = kb.VE.inc(v.tensor_scalar(outT[:, oc * 512:(oc + 1) * 512],
                                                   pO.ps[:], obias[:, oc:oc + 1], None, ALU.add))
                    pO.read_by(kb.VE.h, ot)
                outTready = kb.VE.n
                t.wait_ge(kb.VE.h, outTready)
                for tt in range(4):
                    tok0 = tc0 = qc * 512 + tt * 128
                    sync.wait_ge(kb.VE.h, kb.VE.n)
                    sync.wait_ge(kb.SE.h, kb.SE.n)
                    sync.wait_ge(kb.GD.h, kb.GD.n)
                    ldx1 = kb.LD.inc16(sync.dma_start(x1t[:], x1_sc.ap()[tc0:tc0 + 128, :]))
                    bbt.guard(t)
                    for c in range(8):
                        mm = t.transpose(pbT[:, c * 128:(c + 1) * 128],
                                         outT[:, c * 512 + tt * 128:c * 512 + (tt + 1) * 128],
                                         st["identb"][:])
                    tx = kb.TE.inc(mm)
                    v.wait_ge(kb.TE.h, tx)
                    v.wait_ge(kb.LD.h, ldx1)
                    x2d = kb.VE.inc(v.tensor_add(x2t[:], x1t[:], pbT[:]))
                    bbt.read_by(kb.VE.h, x2d)
                    gp.wait_ge(kb.VE.h, x2d)
                    kb.GD.inc16(gp.dma_start(x2_sc.ap()[tc0:tc0 + 128, :], x2t[:]))
                    s.wait_ge(kb.VE.h, x2d)
                    emit_ln_natural(kb, x2t[:], z3[:], cols2, sq3)
                    zl = kb.VE.n
                    t.wait_ge(kb.VE.h, zl)
                    s.wait_ge(kb.GD.h, kb.GD.n)
                    for half in range(2):
                        bA[0].guard(t)
                        for c in range(4):
                            cc = half * 4 + c
                            mm = t.transpose(pa0[:, c * 128:(c + 1) * 128],
                                             z3[:, cc * 128:(cc + 1) * 128], st["identf"][:])
                        tz3 = kb.TE.inc(mm)
                        s.wait_ge(kb.TE.h, tz3)
                        sz3 = kb.SE.inc(s.copy(ys3[:, half * 512:(half + 1) * 512], pa0[:]))
                        bA[0].read_by(kb.SE.h, sz3)
                    gp.wait_ge(kb.SE.h, kb.SE.n)
                    kb.GD.inc16(gp.dma_start(
                        y_sc.ap()[:, :, 1 + tc0:1 + tc0 + 128].transpose([1, 0, 2]),
                        ys3[:].rearrange("p (c i) -> p c i", i=128)))
        gp.wait_ge(kb.GD.h, kb.GD.n)
        done = kb.GD.n
    for e in (kb.sync, kb.t, kb.v, kb.s):
        e.wait_ge(kb.GD.h, done)



def emit_Cfin(kb, x2_sc, y_sc, out_dram):
    nc, sync, gp, t, v, s = kb.nc, kb.sync, kb.gp, kb.t, kb.v, kb.s
    st = kb.st
    with nc.sbuf_tensor("cf_x", [128, 1024], F32) as xt, \
         nc.sbuf_tensor("cf_y", [128, 1024], F32) as yt, \
         nc.sbuf_tensor("cf_o", [128, 2 * 1024], F32) as ot, \
         nc.psum_tensor("cf_p", [128, 1024], F32) as pn:
        bn = Bank(pn)
        for tc in range(32):
            par = tc % 2
            o = ot[:, par * 1024:(par + 1) * 1024]
            sync.wait_ge(kb.GD.h, max(0, kb.GD.n - 16))
            sync.wait_ge(kb.TE.h, kb.TE.n)
            ldx = kb.LD.inc16(sync.dma_start(xt[:], x2_sc.ap()[tc * 128:(tc + 1) * 128, :]))
            ldy = kb.LD.inc16(sync.dma_start(
                yt[:].rearrange("p (c i) -> p c i", i=128),
                y_sc.ap()[:, :, 1 + tc * 128:1 + (tc + 1) * 128].transpose([1, 0, 2])))
            t.wait_ge(kb.LD.h, ldy)
            bn.guard(t)
            for c in range(8):
                mm = t.transpose(pn[:, c * 128:(c + 1) * 128], yt[:, c * 128:(c + 1) * 128],
                                 st["identf"][:])
            ty = kb.TE.inc(mm)
            v.wait_ge(kb.TE.h, ty)
            v.wait_ge(kb.LD.h, ldx)
            v.wait_ge(kb.GD.h, max(0, kb.GD.n - 16))
            xo = kb.VE.inc(v.scalar_tensor_tensor(o, pn[:], 0.5, xt[:], ALU.mult, ALU.add))
            bn.read_by(kb.VE.h, xo)
            gp.wait_ge(kb.VE.h, xo)
            kb.GD.inc16(gp.dma_start(out_dram.ap()[tc * 128:(tc + 1) * 128, :], o))
        gp.wait_ge(kb.GD.h, kb.GD.n)



WSHAPES_ALL = {
    "L0_attn_qkv_kernel": (1024, 3072), "L0_attn_qkv_bias": (3072,),
    "L0_attn_out_kernel": (1024, 1024), "L0_attn_out_bias": (1024,),
    "L0_ffn_wqkv": (2, 1024, 1536), "L0_ffn_q_scale": (2, 64),
    "L0_ffn_k_scale": (2, 64), "L0_ffn_wo": (2, 512, 1024),
    "L0_ffn_ln_g": (2, 1024), "L0_ffn_ln_b": (2, 1024),
    "L0_ffn_w1": (2, 1024, 5460), "L0_ffn_b1": (2, 5460),
    "L0_ffn_w2": (2, 2730, 1024), "L0_ffn_b2": (2, 1024),
}


def build_full():
    nc = bass.Bass("TRN2", target_bir_lowering=False, detect_race_conditions=False)
    ins = {n: nc.dram_tensor(n, list(sh), F32, kind="ExternalInput")
           for n, sh in WSHAPES_ALL.items()}
    xcore = nc.dram_tensor("xcore", [T, D], F32, kind="ExternalInput")
    consts = host_constants()
    cds = {}
    for n, a in consts.items():
        a = np.asarray(a)
        dt = F32 if a.dtype == np.float32 else BF16
        cds[n] = nc.dram_tensor("c_" + n, list(a.shape), dt, kind="ExternalInput")
    y_sc = nc.dram_tensor("y_sc", [8, 128, 1 + T], F32, kind="Internal")
    x1_sc = nc.dram_tensor("x1_sc", [T, D], F32, kind="Internal")
    x2_sc = nc.dram_tensor("x2_sc", [T, D], F32, kind="Internal")
    qT_sc = nc.dram_tensor("qT_sc", [8, 128, T], BF16, kind="Internal")
    xout = nc.dram_tensor("xout", [T, D], F32, kind="ExternalOutput")
    A_dram = [cds[n] for n in ("A1q", "A2q", "A1k", "A2k")]
    kb = KB(nc)
    alloc_shared(kb)
    emit_shared_init(kb, cds)
    emit_A0(kb, xcore, y_sc)
    for l in (0, 1):
        emit_ffn_attn(kb, l, y_sc, ins, A_dram)
        emit_ffn_mlp(kb, l, y_sc, ins)
    emit_dense_block(kb, xcore, y_sc, x1_sc, x2_sc, qT_sc, ins)
    for l in (0, 1):
        emit_ffn_attn(kb, l, y_sc, ins, A_dram)
        emit_ffn_mlp(kb, l, y_sc, ins)
    emit_Cfin(kb, x2_sc, y_sc, xout)
    return nc, consts


def kernel(**inputs):
    from concourse.bass_utils import run_bass_kernel_spmd
    nc, consts = build_full()
    base = {n: np.ascontiguousarray(np.asarray(inputs[n], dtype=np.float32))
            for n in WSHAPES_ALL}
    for n, a in consts.items():
        base["c_" + n] = np.ascontiguousarray(a)
    x = np.asarray(inputs["L0_attn_x"], dtype=np.float32)
    in_maps = []
    for i in range(8):
        m = dict(base)
        m["xcore"] = np.ascontiguousarray(x[i])
        in_maps.append(m)
    res = run_bass_kernel_spmd(nc, in_maps, core_ids=list(range(8)))
    return np.stack([np.asarray(r["xout"]) for r in res.results], axis=0)


# version 02
# speedup vs baseline: 1.4215x; 1.4215x over previous
import numpy as np
import ml_dtypes
import concourse.bass as bass
import concourse.mybir as mybir

F32 = mybir.dt.float32
BF16 = mybir.dt.bfloat16
FP8 = mybir.dt.float8e4
AF = mybir.ActivationFunctionType
ALU = mybir.AluOpType
AX = mybir.AxisListType

T = 4096
D = 1024
INV224 = float(np.float32(1.0 / 224.0))


class Sem:
    def __init__(self, nc, name):
        self.h = nc.alloc_semaphore(name)
        self.n = 0

    def inc(self, instr, k=1):
        instr.then_inc(self.h, k)
        self.n += k
        return self.n

    def inc16(self, instr):
        return self.inc(instr, 16)


class KB:
    def __init__(self, nc):
        self.nc = nc
        self.LD = Sem(nc, "LD")
        self.GD = Sem(nc, "GD")
        self.TE = Sem(nc, "TE")
        self.VE = Sem(nc, "VE")
        self.SE = Sem(nc, "SE")
        self.sync, self.gp = nc.sync, nc.gpsimd
        self.t, self.v, self.s = nc.tensor, nc.vector, nc.scalar
        self.st = {}
        self._uid = 0

    def uid(self):
        self._uid += 1
        return self._uid


def host_constants():
    DIM_HEAD, WINDOW, QK_SCALE = 64, 512, 8.0
    half_idx = np.arange(0, DIM_HEAD, 2, dtype=np.float64)
    inv_freq = 1.0 / 10000.0 ** (half_idx / DIM_HEAD)
    tt = np.arange(T, dtype=np.float64)
    freqs = tt[:, None] * inv_freq[None, :]
    freqs = np.concatenate([freqs, freqs], axis=-1)
    base = (half_idx + 0.4 * DIM_HEAD) / (1.4 * DIM_HEAD)
    power = (tt - T // 2) / float(WINDOW // 2)
    scale = base[None, :] ** power[:, None]
    scale = np.concatenate([scale, scale], axis=-1)
    cos, sin = np.cos(freqs), np.sin(freqs)
    sign = np.where(np.arange(DIM_HEAD) < 32, -1.0, 1.0)
    a1q = (QK_SCALE * cos * scale).T
    a2q = (QK_SCALE * sin * scale).T * sign[:, None]
    a1k = (cos / scale).T
    a2k = (sin / scale).T * sign[:, None]
    rep = lambda a: np.concatenate([a, a], axis=0).astype(np.float32)
    r = np.arange(128)
    prot = np.zeros((128, 128), np.float32)
    for m in range(128):
        prot[(m + 32) % 64 + 64 * (m // 64), m] = 1.0
    return {
        "A1q": rep(a1q), "A2q": rep(a2q), "A1k": rep(a1k), "A2k": rep(a2k),
        "identb": np.eye(128, dtype=ml_dtypes.bfloat16),
        "identf": np.eye(128, dtype=np.float32),
        "maskL": (r[None, :] <= r[:, None]).astype(ml_dtypes.bfloat16),
        "maskU": (r[:, None] <= r[None, :]).astype(ml_dtypes.bfloat16),
        "prot": prot,
        "onesall": np.ones((128, 128), np.float32),
        "onesallb": np.ones((128, 4), ml_dtypes.bfloat16),
        "onesrowb": np.ones((1, 128), ml_dtypes.bfloat16),
        "ones2f": np.stack([(r < 64).astype(np.float32), (r >= 64).astype(np.float32)], 1),
        "bc2f": np.stack([(r < 64).astype(np.float32), (r >= 64).astype(np.float32)])[:, :],
        "epscol": np.full((128, 1), 1e-5, np.float32),
    }


def alloc_shared(kb):
    nc, st = kb.nc, kb.st
    st["identb"] = nc.alloc_sbuf_tensor("sb_identb", [128, 128], BF16)
    st["identf"] = nc.alloc_sbuf_tensor("sb_identf", [128, 128], F32)
    st["maskL"] = nc.alloc_sbuf_tensor("sb_maskL", [128, 128], BF16)
    st["maskU"] = nc.alloc_sbuf_tensor("sb_maskU", [128, 128], BF16)
    st["prot"] = nc.alloc_sbuf_tensor("sb_prot", [128, 128], F32)
    st["onesall"] = nc.alloc_sbuf_tensor("sb_onesall", [128, 128], F32)
    st["onesallb"] = nc.alloc_sbuf_tensor("sb_onesallb", [128, 4], BF16)
    st["onesrowb"] = nc.alloc_sbuf_tensor("sb_onesrowb", [1, 128], BF16)
    st["ones2f"] = nc.alloc_sbuf_tensor("sb_ones2f", [128, 2], F32)
    st["epscol"] = nc.alloc_sbuf_tensor("sb_epscol", [128, 1], F32)
    st["bc2f"] = nc.alloc_sbuf_tensor("sb_bc2f", [2, 128], F32)


def emit_shared_init(kb, consts_in):
    sync, v = kb.sync, kb.v
    st = kb.st
    for name in ("identb", "identf", "maskL", "maskU", "prot", "onesall",
                 "onesallb", "onesrowb", "ones2f", "epscol", "bc2f"):
        kb.LD.inc16(sync.dma_start(st[name][:], consts_in[name].ap()))
    kb.VE.inc(v.engine_nop())



def emit_ln_natural(kb, x, z, cols, sq):
    v, s = kb.v, kb.s
    v.reduce_sum(cols[:, 0:1], x[:], AX.X)
    v.drain()
    mean_tok = kb.VE.inc(v.tensor_scalar(cols[:, 1:2], cols[:, 0:1], 1.0 / D, None, ALU.mult))
    s.wait_ge(kb.VE.h, mean_tok)
    sqd = kb.SE.inc(s.activation(sq[:], x[:], AF.Square))
    v.wait_ge(kb.SE.h, sqd)
    v.reduce_sum(cols[:, 2:3], sq[:], AX.X)
    v.drain()
    v.tensor_scalar(cols[:, 3:4], cols[:, 2:3], 1.0 / D, None, ALU.mult)
    v.tensor_mul(cols[:, 4:5], cols[:, 1:2], cols[:, 1:2])
    v.drain()
    var_tok = kb.VE.inc(v.tensor_sub(cols[:, 5:6], cols[:, 3:4], cols[:, 4:5]))
    s.wait_ge(kb.VE.h, var_tok)
    sqr = kb.SE.inc(s.activation(cols[:, 6:7], cols[:, 5:6], AF.Sqrt, bias=kb.st["epscol"][:, 0:1]))
    v.wait_ge(kb.SE.h, sqr)
    v.reciprocal(cols[:, 7:8], cols[:, 6:7])
    v.drain()
    v.scalar_tensor_tensor(cols[:, 8:9], cols[:, 1:2], -1.0,
                           cols[:, 7:8], ALU.mult, ALU.mult)
    v.drain()
    return kb.VE.inc(v.tensor_scalar(z[:], x[:], cols[:, 7:8], cols[:, 8:9],
                                     ALU.mult, ALU.add))



def emit_quant(kb, src, tmp_f32, tmp_fp8, out, qcols, blocks, width=128):
    v = kb.v
    n = blocks * width
    am, s2, r2 = qcols[:, 0:blocks], qcols[:, blocks:2 * blocks], qcols[:, 2 * blocks:3 * blocks]
    r3 = lambda ap: ap[:, 0:n].rearrange("p (b i) -> p b i", i=width)
    bc = lambda ap: ap.unsqueeze(2).broadcast_to([128, blocks, width])
    v.tensor_reduce(am, r3(src), AX.X, ALU.max, apply_absolute_value=True)
    v.drain()
    v.tensor_scalar(s2, am, INV224, 2e-12, ALU.mult, ALU.max)
    v.drain()
    v.reciprocal(r2, s2)
    v.drain()
    v.tensor_tensor(r3(tmp_f32), r3(src), bc(r2), ALU.mult)
    v.drain()
    v.tensor_copy(tmp_fp8[:, 0:n], tmp_f32[:, 0:n])
    v.drain()
    return v.tensor_tensor(r3(out), r3(tmp_fp8), bc(s2), ALU.mult)



SERIAL_A0 = False


def emit_A0(kb, x_src, y_dst):
    nc, sync, gp, t, v, s = kb.nc, kb.sync, kb.gp, kb.t, kb.v, kb.s
    st = kb.st
    identf = st["identf"]
    with nc.sbuf_tensor("a0_x", [128, 2 * 1024], F32) as xt, \
         nc.sbuf_tensor("a0_z", [128, 1024], F32) as zt, \
         nc.sbuf_tensor("a0_sq", [128, 1024], F32) as sqt, \
         nc.sbuf_tensor("a0_cols", [128, 16], F32) as cols, \
         nc.sbuf_tensor("a0_yst", [128, 2 * 1024], F32) as yst, \
         nc.psum_tensor("a0_ps", [128, 1024], F32) as ps0, \
         nc.psum_tensor("a0_ps2", [128, 1024], F32) as ps1:
        store_toks = [0, 0]
        prev_t = 0
        prev_z = 0
        for tt in range(T // 128):
            par = tt % 2
            x = xt[:, par * 1024:(par + 1) * 1024]
            ys = yst[:, par * 1024:(par + 1) * 1024]
            ps = (ps0, ps1)[par]
            sync.wait_ge(kb.SE.h, store_toks[par])
            ldn = kb.LD.inc16(sync.dma_start(x, x_src.ap()[tt * 128:(tt + 1) * 128, :]))
            v.wait_ge(kb.LD.h, ldn)
            v.wait_ge(kb.TE.h, prev_t)
            s.wait_ge(kb.LD.h, ldn)
            zdone = emit_ln_natural(kb, x, zt, cols, sqt)
            t.wait_ge(kb.VE.h, zdone)
            t.wait_ge(kb.SE.h, store_toks[par])
            for c in range(8):
                ins = t.transpose(ps[:, c * 128:(c + 1) * 128],
                                  zt[:, c * 128:(c + 1) * 128], identf[:])
            prev_t = tdone = kb.TE.inc(ins)
            s.wait_ge(kb.TE.h, tdone)
            store_toks[par] = kb.SE.inc(s.copy(ys, ps[:]))
            gp.wait_ge(kb.SE.h, store_toks[par])
            kb.GD.inc16(gp.dma_start(
                y_dst.ap()[:, :, 1 + tt * 128:1 + (tt + 1) * 128].transpose([1, 0, 2]),
                ys.rearrange("p (c i) -> p c i", i=128)))
            if SERIAL_A0:
                for e in (kb.sync, kb.t, kb.v, kb.s, kb.gp):
                    e.wait_ge(kb.GD.h, kb.GD.n)
                    e.wait_ge(kb.VE.h, kb.VE.n)
                    e.wait_ge(kb.SE.h, kb.SE.n)
                    e.wait_ge(kb.TE.h, kb.TE.n)
                    e.wait_ge(kb.LD.h, kb.LD.n)
        gp.wait_ge(kb.GD.h, kb.GD.n)
        done = kb.GD.n
    for e in (kb.sync, kb.t, kb.v, kb.s):
        e.wait_ge(kb.GD.h, done)



def load_ffn_attn_weights(kb, l, ins, wts, pn):
    nc, sync, gp, t, v, s = kb.nc, kb.sync, kb.gp, kb.t, kb.v, kb.s
    for kc in range(8):
        kb.GD.inc16(gp.dma_start(
            wts["wqkv"][:, kc * 1536:(kc + 1) * 1536],
            ins["L0_ffn_wqkv"].ap()[l, kc * 128:(kc + 1) * 128, :]))
    for ic in range(4):
        kb.GD.inc16(gp.dma_start(
            wts["wo"][:, ic * 1024:(ic + 1) * 1024],
            ins["L0_ffn_wo"].ap()[l, ic * 128:(ic + 1) * 128, :]))
    qs, ks = ins["L0_ffn_q_scale"], ins["L0_ffn_k_scale"]
    for half in range(2):
        b = 64 * half
        kb.LD.inc16(sync.dma_start(wts["qscol"][b:b + 64, 0:1], qs.ap()[l, :].unsqueeze(1)))
        kb.LD.inc16(sync.dma_start(wts["kscol"][b:b + 64, 0:1], ks.ap()[l, :].unsqueeze(1)))
        kb.LD.inc16(sync.dma_start(wts["qssh"][b:b + 32, 0:1], qs.ap()[l, 32:64].unsqueeze(1)))
        kb.LD.inc16(sync.dma_start(wts["qssh"][b + 32:b + 64, 0:1], qs.ap()[l, 0:32].unsqueeze(1)))
        kb.LD.inc16(sync.dma_start(wts["kssh"][b:b + 32, 0:1], ks.ap()[l, 32:64].unsqueeze(1)))
        kb.LD.inc16(sync.dma_start(wts["kssh"][b + 32:b + 64, 0:1], ks.ap()[l, 0:32].unsqueeze(1)))
    ldr1 = kb.LD.inc16(sync.dma_start(wts["qrow"][0:1, 0:64], qs.ap()[l, :].unsqueeze(0)))
    ldr = kb.LD.inc16(sync.dma_start(wts["krow"][0:1, 0:64], ks.ap()[l, :].unsqueeze(0)))
    v.wait_ge(kb.LD.h, ldr)
    v.tensor_reduce(wts["qrow"][0:1, 64:65], wts["qrow"][0:1, 0:64], AX.X, ALU.max,
                    apply_absolute_value=True)
    v.tensor_reduce(wts["krow"][0:1, 64:65], wts["krow"][0:1, 0:64], AX.X, ALU.max,
                    apply_absolute_value=True)
    v.drain()
    v.tensor_mul(wts["qrow"][0:1, 65:66], wts["qrow"][0:1, 64:65], wts["krow"][0:1, 64:65])
    v.drain()
    cn = kb.VE.inc(v.tensor_scalar(wts["qrow"][0:1, 66:67], wts["qrow"][0:1, 65:66],
                                   -16.0, None, ALU.mult))
    t.wait_ge(kb.VE.h, cn)
    mmn = kb.TE.inc(t.matmul(pn.ps[:, 0:1], kb.st["onesall"][0:1, :],
                             wts["qrow"][0:1, 66:67], start=True, stop=True))
    s.wait_ge(kb.TE.h, mmn)
    tok = kb.SE.inc(s.copy(wts["negC"][:], pn.ps[:, 0:1]))
    pn.read_by(kb.SE.h, tok)
    return tok


def alloc_ffn_attn(kb, ns):
    import contextlib
    nc = kb.nc
    es = contextlib.ExitStack()
    a = lambda n, sh, dt: es.enter_context(nc.sbuf_tensor(ns + n, sh, dt))
    wts = {
        "wqkv": a("wqkv", [128, 8 * 1536], BF16),
        "wo": a("wo", [128, 4 * 1024], BF16),
        "qscol": a("qscol", [128, 1], F32), "qssh": a("qssh", [128, 1], F32),
        "kscol": a("kscol", [128, 1], F32), "kssh": a("kssh", [128, 1], F32),
        "qrow": a("qrow", [1, 80], F32), "krow": a("krow", [1, 80], F32),
        "negC": a("negC", [128, 1], F32),
    }
    bufs = {
        "yw": a("yw", [128, 8 * 512], F32),
        "ywb": a("ywb", [128, 8 * 513], BF16),
        "guard": a("guard", [128, 8], F32),
        "tabs": a("tabs", [128, 4 * 512], F32),
        "tabsrc": a("tabsrc", [128, 4 * 512], F32),
        "qh": a("qh", [128, 4 * 512], BF16),
        "kh0": a("kh0", [128, 4 * 512], BF16), "kh1": a("kh1", [128, 4 * 512], BF16),
        "vv0": a("vv0", [128, 4 * 520], BF16), "vv1": a("vv1", [128, 4 * 520], BF16),
        "qn0": a("qn0", [128, 2 * 512], F32),
        "sqf": a("sqf", [128, 2 * 512], F32),
        "rowa": a("rowa", [2, 512], F32), "rowb": a("rowb", [2, 512], F32),
        "tt1": a("tt1", [128, 512], F32), "tt2": a("tt2", [128, 512], F32),
        "h1": a("h1", [128, 512], F32), "h2": a("h2", [128, 512], F32),
        "expb": a("expb", [128, 2 * 512], BF16),
        "rowr": a("rowr", [128, 512], F32),
        "bcf": a("bcf", [128, 512], F32),
        "oT": a("oT", [128, 4 * 512], BF16),
    }
    return es, wts, bufs


VALID_SPAN = {kc: (max(0, (kc - 4)) * 128, min(512, kc * 128 + 128)) for kc in range(8)}


class Bank:
    def __init__(self, ps):
        self.ps = ps
        self.rd = None

    def guard(self, eng):
        if self.rd is not None:
            eng.wait_ge(*self.rd)

    def read_by(self, semh, val):
        self.rd = (semh, val)


def emit_ffn_attn(kb, l, y_sc, ins, A_dram):
    nc, sync, gp, t, v, s = kb.nc, kb.sync, kb.gp, kb.t, kb.v, kb.s
    st = kb.st
    u = kb.uid()
    es, wts, bu = alloc_ffn_attn(kb, f"fa{u}_")
    with es, \
         nc.psum_tensor(f"fa{u}_g0", [128, 512], F32) as pg0, \
         nc.psum_tensor(f"fa{u}_g1", [128, 512], F32) as pg1, \
         nc.psum_tensor(f"fa{u}_g2", [128, 512], F32) as pg2, \
         nc.psum_tensor(f"fa{u}_g3", [128, 512], F32) as pg3, \
         nc.psum_tensor(f"fa{u}_g4", [128, 512], F32) as pg4, \
         nc.psum_tensor(f"fa{u}_g5", [128, 512], F32) as pg5, \
         nc.psum_tensor(f"fa{u}_row", [2, 512], F32) as prow_ps, \
         nc.psum_tensor(f"fa{u}_srow", [1, 512], F32) as srow_ps:
        g = [Bank(p) for p in (pg0, pg1, pg2, pg3, pg4, pg5)]
        growA, growB = Bank(prow_ps), Bank(srow_ps)
        negc_tok = load_ffn_attn_weights(kb, l, ins, wts, g[0])
        v.memset(bu["guard"][:], 0.0)
        for vb in (bu["vv0"], bu["vv1"]):
            v.memset(vb[:].rearrange("p (a h i) -> p a h i", h=8, i=65)[:, :, :, 64:65], 1.0)
        init_tok = kb.VE.inc(v.engine_nop())
        t.wait_ge(kb.SE.h, negc_tok)

        for w in range(8):
            kv_cur = (bu["kh0"], bu["kh1"])[w % 2]
            kv_prv = (bu["kh1"], bu["kh0"])[w % 2]
            v_cur = (bu["vv0"], bu["vv1"])[w % 2]
            v_prv = (bu["vv1"], bu["vv0"])[w % 2]
            sync.wait_ge(kb.GD.h, kb.GD.n)
            sync.wait_ge(kb.TE.h, kb.TE.n)
            ldy = kb.LD.inc16(sync.dma_start(
                bu["yw"][:].rearrange("p (c i) -> p c i", i=512),
                y_sc.ap()[:, :, 1 + w * 512:1 + (w + 1) * 512].transpose([1, 0, 2])))
            ldt = ldy
            for j, A in enumerate(A_dram):
                ldt = kb.LD.inc16(sync.dma_start(
                    bu["tabsrc"][:, j * 512:(j + 1) * 512], A.ap()[:, w * 512:(w + 1) * 512]))
            v.wait_ge(kb.LD.h, ldt)
            v.wait_ge(kb.VE.h, init_tok)
            v.tensor_scalar_mul(bu["tabs"][:, 0:512], bu["tabsrc"][:, 0:512], wts["qscol"][:])
            v.tensor_scalar_mul(bu["tabs"][:, 512:1024], bu["tabsrc"][:, 512:1024], wts["qssh"][:])
            v.tensor_scalar_mul(bu["tabs"][:, 1024:1536], bu["tabsrc"][:, 1024:1536], wts["kscol"][:])
            v.tensor_scalar_mul(bu["tabs"][:, 1536:2048], bu["tabsrc"][:, 1536:2048], wts["kssh"][:])
            for c in range(8):
                v.tensor_copy(bu["ywb"][:, c * 513 + 1:(c + 1) * 513],
                              bu["yw"][:, c * 512:(c + 1) * 512])
                v.tensor_copy(bu["ywb"][:, c * 513:c * 513 + 1], bu["guard"][:, c:c + 1])
            for c in range(8):
                v.tensor_copy(bu["guard"][:, c:c + 1], bu["yw"][:, c * 512 + 511:c * 512 + 512])
            hT = kb.VE.inc(v.engine_nop())

            def hsrc(kc, tokslice=None):
                base = kc * 513 + (1 if kc < 4 else 0)
                if tokslice is None:
                    return bu["ywb"][:, base:base + 512]
                return bu["ywb"][:, base + tokslice[0]:base + tokslice[1]]

            t.wait_ge(kb.VE.h, hT)
            state = {}

            def emit_mmq(oc):
                par = oc % 2
                pqk = g[par]
                pqk.guard(t)
                for kc in range(8):
                    mm = t.matmul(pqk.ps[:],
                                  wts["wqkv"][:, kc * 1536 + oc * 128:kc * 1536 + (oc + 1) * 128],
                                  hsrc(kc), start=(kc == 0), stop=(kc == 7))
                state[("mmq", oc)] = kb.TE.inc(mm)

            def emit_tailA(oc):
                par = oc % 2
                pqk = g[par]
                qn0 = bu["qn0"][:, par * 512:(par + 1) * 512]
                sqf = bu["sqf"][:, par * 512:(par + 1) * 512]
                s.wait_ge(kb.TE.h, state[("mmq", oc)])
                s.wait_ge(kb.VE.h, state.get(("fin", oc - 2), 0))
                s.copy(qn0, pqk.ps[:])
                s.drain()
                sq = kb.SE.inc(s.activation(sqf, pqk.ps[:], AF.Square))
                pqk.read_by(kb.SE.h, sq)
                state[("sq", oc)] = sq

            def emit_tailB(oc):
                par = oc % 2
                qn0 = bu["qn0"][:, par * 512:(par + 1) * 512]
                sqf = bu["sqf"][:, par * 512:(par + 1) * 512]
                is_q = oc < 4
                c = oc if is_q else oc - 4
                dst = bu["qh"] if is_q else kv_cur
                T1 = bu["tabs"][:, 0:512] if is_q else bu["tabs"][:, 1024:1536]
                T2 = bu["tabs"][:, 512:1024] if is_q else bu["tabs"][:, 1536:2048]
                t.wait_ge(kb.SE.h, state[("sq", oc)])
                growA.guard(t)
                mms = kb.TE.inc(t.matmul(prow_ps[:], st["ones2f"][:], sqf,
                                         start=True, stop=True))
                g[2].guard(t)
                mrot = kb.TE.inc(t.matmul(g[2].ps[:], st["prot"][:], qn0,
                                          start=True, stop=True))
                s.wait_ge(kb.TE.h, mms)
                sr = kb.SE.inc(s.activation(bu["rowa"][:], prow_ps[:], AF.Sqrt))
                growA.read_by(kb.SE.h, sr)
                v.wait_ge(kb.SE.h, sr)
                v.tensor_scalar_max(bu["rowa"][:], bu["rowa"][:], 1e-12)
                v.drain()
                rr = kb.VE.inc(v.reciprocal(bu["rowb"][:], bu["rowa"][:]))
                t.wait_ge(kb.VE.h, rr)
                g[3].guard(t)
                mb = kb.TE.inc(t.matmul(g[3].ps[:], st["bc2f"][:], bu["rowb"][:],
                                        start=True, stop=True))
                v.wait_ge(kb.TE.h, max(mb, mrot))
                v.tensor_tensor(bu["tt1"][:], g[3].ps[:], T1, ALU.mult)
                g3r = kb.VE.inc(v.tensor_tensor(bu["tt2"][:], g[3].ps[:], T2, ALU.mult))
                g[3].read_by(kb.VE.h, g3r)
                v.drain()
                v.tensor_tensor(bu["h1"][:], qn0, bu["tt1"][:], ALU.mult)
                g2r = kb.VE.inc(v.tensor_tensor(bu["h2"][:], g[2].ps[:], bu["tt2"][:], ALU.mult))
                g[2].read_by(kb.VE.h, g2r)
                v.drain()
                fin = kb.VE.inc(v.tensor_tensor(dst[:, c * 512:(c + 1) * 512],
                                                bu["h1"][:], bu["h2"][:], ALU.add))
                state[("fin", oc)] = fin

            for oc in range(10):
                if oc < 8:
                    emit_mmq(oc)
                if oc >= 1 and oc - 1 < 8:
                    emit_tailA(oc - 1)
                if oc >= 2:
                    emit_tailB(oc - 2)
            for tt in range(4):
                par = tt % 2
                pvm = g[par]
                pvm.guard(t)
                for kc in range(8):
                    mm = t.matmul(pvm.ps[:], hsrc(kc, (tt * 128, (tt + 1) * 128)),
                                  wts["wqkv"][:, kc * 1536 + 1024:kc * 1536 + 1536],
                                  start=(kc == 0), stop=(kc == 7))
                mmv = kb.TE.inc(mm)
                s.wait_ge(kb.TE.h, mmv)
                vz = kb.SE.inc(s.copy(
                    v_cur[:, tt * 520:(tt + 1) * 520].rearrange("p (h i) -> p h i", i=65)[:, :, 0:64],
                    pvm.ps[:].rearrange("p (h i) -> p h i", i=64)))
                pvm.read_by(kb.SE.h, vz)
            vready = kb.SE.n
            t.wait_ge(kb.SE.h, vready)
            kcs = [kc for kc in range(8) if (w > 0 or kc >= 4)]
            hstate = {}

            def head_kc_loop(h):
                c, sub = h // 2, h % 2
                P0, P1 = sub * 64, sub * 64 + 64
                pvb = g[4 + (h % 2)]
                pvb.guard(t)
                if sub == 1:
                    growB.guard(t)
                toks = {}
                n = len(kcs)
                for it in range(n + 2):
                    if it < n:
                        kc = kcs[it]
                        lo, hi = VALID_SPAN[kc]
                        par = it % 2
                        psc = g[par]
                        kbuf = kv_prv if kc < 4 else kv_cur
                        kcol = (kc % 4) * 128
                        psc.guard(t)
                        mm = t.matmul(psc.ps[:, 0:hi - lo],
                                      kbuf[P0:P1, c * 512 + kcol:c * 512 + kcol + 128],
                                      bu["qh"][P0:P1, c * 512 + lo:c * 512 + hi],
                                      start=True, stop=True, tile_position=(P0, 0))
                        toks[("sc", it)] = kb.TE.inc(mm)
                    if 1 <= it <= n:
                        i = it - 1
                        kc = kcs[i]
                        lo, hi = VALID_SPAN[kc]
                        par = i % 2
                        psc = g[par]
                        s.wait_ge(kb.TE.h, toks[("sc", i)])
                        s.wait_ge(kb.TE.h, toks.get(("pv", i - 2), 0))
                        s.wait_ge(kb.VE.h, toks.get(("mz", i - 2), 0))
                        ex = kb.SE.inc(s.activation(bu["expb"][:, par * 512 + lo:par * 512 + hi],
                                                    psc.ps[:, 0:hi - lo], AF.Exp,
                                                    bias=wts["negC"][:]))
                        psc.read_by(kb.SE.h, ex)
                        dep = (kb.SE.h, ex)
                        if kc < 4:
                            off = par * 512 + kc * 128
                            v.wait_ge(kb.SE.h, ex)
                            mt = kb.VE.inc(v.tensor_mul(bu["expb"][:, off:off + 128],
                                                        bu["expb"][:, off:off + 128], st["maskL"][:]))
                            dep = (kb.VE.h, mt)
                        else:
                            off = par * 512 + (kc - 4) * 128
                            v.wait_ge(kb.SE.h, ex)
                            mt = kb.VE.inc(v.tensor_mul(bu["expb"][:, off:off + 128],
                                                        bu["expb"][:, off:off + 128], st["maskU"][:]))
                            dep = (kb.VE.h, mt)
                        if lo > 0:
                            mz = kb.VE.inc(v.memset(bu["expb"][:, par * 512:par * 512 + lo], 0.0))
                            dep = (kb.VE.h, mz)
                        if hi < 512:
                            mz = kb.VE.inc(v.memset(bu["expb"][:, par * 512 + hi:par * 512 + 512], 0.0))
                            dep = (kb.VE.h, mz)
                        toks[("mz", i)] = dep[1] if dep[0] is kb.VE.h else 0
                        toks[("dep", i)] = dep
                    if it >= 2:
                        i = it - 2
                        kc = kcs[i]
                        par = i % 2
                        dsem, dval = toks[("dep", i)]
                        t.wait_ge(dsem, dval)
                        vbuf = v_prv if kc < 4 else v_cur
                        vbase = (kc % 4) * 520 + h * 65
                        eall = bu["expb"][:, par * 512:par * 512 + 512]
                        if sub == 0:
                            mm2 = t.matmul(pvb.ps[0:65, 0:512], vbuf[:, vbase:vbase + 65],
                                           eall, start=(i == 0), stop=(i == n - 1),
                                           skip_group_check=True)
                            toks[("pv", i)] = kb.TE.inc(mm2)
                        else:
                            t.matmul(pvb.ps[64:128, 0:512], vbuf[:, vbase:vbase + 64],
                                     eall, start=(i == 0), stop=(i == n - 1),
                                     skip_group_check=True, tile_position=(0, 64))
                            mm3 = t.matmul(srow_ps[0:1, 0:512], st["onesallb"][:, 0:1],
                                           eall, start=(i == 0), stop=(i == n - 1),
                                           skip_group_check=True)
                            toks[("pv", i)] = kb.TE.inc(mm3)
                hstate[("pvdone", h)] = toks[("pv", len(kcs) - 1)]

            def head_norm(h):
                c, sub = h // 2, h % 2
                P0, P1 = sub * 64, sub * 64 + 64
                pvb = g[4 + (h % 2)]
                v.wait_ge(kb.TE.h, hstate[("pvdone", h)])
                if sub == 0:
                    rc = kb.VE.inc(v.reciprocal(bu["rowr"][64:65, :], pvb.ps[64:65, :]))
                    t.wait_ge(kb.VE.h, rc)
                    g[2].guard(t)
                    mb2 = kb.TE.inc(t.matmul(g[2].ps[0:64, :], st["onesall"][64:65, 0:64],
                                             bu["rowr"][64:65, :], start=True, stop=True))
                    s.wait_ge(kb.TE.h, mb2)
                    sc2 = kb.SE.inc(s.copy(bu["bcf"][0:64, :], g[2].ps[0:64, :]))
                    g[2].read_by(kb.SE.h, sc2)
                    v.wait_ge(kb.SE.h, sc2)
                    ord_ = kb.VE.inc(v.tensor_tensor(
                        bu["oT"][0:64, c * 512:(c + 1) * 512],
                        pvb.ps[0:64, :], bu["bcf"][0:64, :], ALU.mult))
                else:
                    rc = kb.VE.inc(v.reciprocal(bu["rowr"][0:1, :], srow_ps[0:1, :]))
                    growB.read_by(kb.VE.h, rc)
                    t.wait_ge(kb.VE.h, rc)
                    g[2].guard(t)
                    mb2 = kb.TE.inc(t.matmul(g[2].ps[64:128, :], st["onesall"][0:1, 0:64],
                                             bu["rowr"][0:1, :], start=True, stop=True,
                                             tile_position=(0, 64)))
                    s.wait_ge(kb.TE.h, mb2)
                    sc2 = kb.SE.inc(s.copy(bu["bcf"][64:128, :], g[2].ps[64:128, :]))
                    g[2].read_by(kb.SE.h, sc2)
                    v.wait_ge(kb.SE.h, sc2)
                    ord_ = kb.VE.inc(v.tensor_tensor(
                        bu["oT"][64:128, c * 512:(c + 1) * 512],
                        pvb.ps[64:128, :], bu["bcf"][64:128, :], ALU.mult))
                pvb.read_by(kb.VE.h, ord_)

            for h in range(9):
                if h < 8:
                    head_kc_loop(h)
                if h >= 1:
                    head_norm(h - 1)
            oT_ready = kb.VE.n
            t.wait_ge(kb.VE.h, oT_ready)
            for oc in range(8):
                par = oc % 2
                pw = g[par]
                pw.guard(t)
                for ic in range(4):
                    mm = t.matmul(pw.ps[:],
                                  wts["wo"][:, ic * 1024 + oc * 128:ic * 1024 + (oc + 1) * 128],
                                  bu["oT"][:, ic * 512:(ic + 1) * 512],
                                  start=(ic == 0), stop=(ic == 3))
                mmw = kb.TE.inc(mm)
                v.wait_ge(kb.TE.h, mmw)
                yu = kb.VE.inc(v.tensor_add(bu["yw"][:, oc * 512:(oc + 1) * 512],
                                            bu["yw"][:, oc * 512:(oc + 1) * 512], pw.ps[:]))
                pw.read_by(kb.VE.h, yu)
            gp.wait_ge(kb.VE.h, kb.VE.n)
            kb.GD.inc16(gp.dma_start(
                y_sc.ap()[:, :, 1 + w * 512:1 + (w + 1) * 512].transpose([1, 0, 2]),
                bu["yw"][:].rearrange("p (c i) -> p c i", i=512)))
        gp.wait_ge(kb.GD.h, kb.GD.n)
        done = kb.GD.n
    for e in (kb.sync, kb.t, kb.v, kb.s):
        e.wait_ge(kb.GD.h, done)



W1CH = [(wc * 512, min(2730, wc * 512 + 512)) for wc in range(6)]


def emit_ffn_mlp(kb, l, y_sc, ins):
    import contextlib
    nc, sync, gp, t, v, s = kb.nc, kb.sync, kb.gp, kb.t, kb.v, kb.s
    st = kb.st
    es = contextlib.ExitStack()
    u = kb.uid()
    a = lambda n, sh, dt: es.enter_context(nc.sbuf_tensor(f"mlp{u}_" + n, sh, dt))
    w1 = a("w1", [128, 8 * 5460], BF16)
    w2 = a("w2", [128, 22 * 1024], BF16)
    b1a = a("b1a", [128, 22], F32)
    b1g = a("b1g", [128, 22], F32)
    lg = a("lg", [128, 8], F32)
    lb = a("lb", [128, 8], F32)
    b2c = a("b2c", [128, 8], F32)
    ytg = a("ytg", [128, 2 * 8 * 257], F32)
    hsq = a("hsq", [128, 8 * 257], F32)
    hT = a("hT", [128, 2 * 8 * 256], BF16)
    guard = a("guard", [128, 8], F32)
    rows = a("rows", [1, 1024], F32)
    gel = a("gel", [128, 2 * 256], F32)
    abuf = a("abuf", [128, 2 * 256], F32)
    mT = a("mT", [128, 2 * 256], BF16)
    SUBS = []
    for wc, (c0, c1) in enumerate(W1CH):
        for sub in range((c1 - c0 + 127) // 128):
            s0 = c0 + sub * 128
            SUBS.append((s0, min(c1, s0 + 128)))
    with es, \
         nc.psum_tensor(f"mlpp{u}_y0", [128, 512], F32) as py0, \
         nc.psum_tensor(f"mlpp{u}_y1", [128, 512], F32) as py1, \
         nc.psum_tensor(f"mlpp{u}_y2", [128, 512], F32) as py2, \
         nc.psum_tensor(f"mlpp{u}_y3", [128, 512], F32) as py3, \
         nc.psum_tensor(f"mlpp{u}_ag0", [128, 512], F32) as pag0, \
         nc.psum_tensor(f"mlpp{u}_ag1", [128, 512], F32) as pag1, \
         nc.psum_tensor(f"mlpp{u}_r", [1, 512], F32) as pr, \
         nc.psum_tensor(f"mlpp{u}_bc", [128, 512], F32) as pbc:
        py = [Bank(p) for p in (py0, py1, py2, py3)]
        pag = [Bank(pag0), Bank(pag1)]
        brow, bbc = Bank(pr), Bank(pbc)
        for kc in range(8):
            kb.GD.inc16(gp.dma_start(w1[:, kc * 5460:(kc + 1) * 5460],
                                     ins["L0_ffn_w1"].ap()[l, kc * 128:(kc + 1) * 128, :]))
        for rc in range(21):
            kb.GD.inc16(gp.dma_start(w2[:, rc * 1024:(rc + 1) * 1024],
                                     ins["L0_ffn_w2"].ap()[l, rc * 128:(rc + 1) * 128, :]))
        kb.GD.inc16(gp.dma_start(w2[0:42, 21 * 1024:22 * 1024],
                                 ins["L0_ffn_w2"].ap()[l, 2688:2730, :]))
        b1 = ins["L0_ffn_b1"]
        ncd = nc.allow_non_contiguous_dma(reason="small bias-column loads")
        ncd.__enter__()
        kb.LD.inc16(sync.dma_start(b1a[:, 0:21],
                                   b1.ap()[l, 0:2688].rearrange("(c p) -> p c", p=128)))
        kb.LD.inc16(sync.dma_start(b1a[0:42, 21:22], b1.ap()[l, 2688:2730].unsqueeze(1)))
        kb.LD.inc16(sync.dma_start(b1g[:, 0:21],
                                   b1.ap()[l, 2730:5418].rearrange("(c p) -> p c", p=128)))
        kb.LD.inc16(sync.dma_start(b1g[0:42, 21:22], b1.ap()[l, 5418:5460].unsqueeze(1)))
        kb.LD.inc16(sync.dma_start(lg[:], ins["L0_ffn_ln_g"].ap()[l, :].rearrange("(c p) -> p c", p=128)))
        kb.LD.inc16(sync.dma_start(lb[:], ins["L0_ffn_ln_b"].ap()[l, :].rearrange("(c p) -> p c", p=128)))
        wtok = kb.LD.inc16(sync.dma_start(b2c[:], ins["L0_ffn_b2"].ap()[l, :].rearrange("(c p) -> p c", p=128)))
        ncd.__exit__(None, None, None)
        gtok = kb.GD.n
        v.memset(guard[:], 0.0)
        kb.VE.inc(v.engine_nop())
        t.wait_ge(kb.LD.h, wtok)
        t.wait_ge(kb.GD.h, gtok)
        s.wait_ge(kb.LD.h, wtok)
        v.wait_ge(kb.LD.h, wtok)

        stt_ = {}
        def ysl(tc, kc, shifted):
            base = (tc % 2) * 2056 + kc * 257 + (0 if shifted else 1)
            return ytg[:, base:base + 256]

        def emit_load(tc):
            sync.wait_ge(kb.GD.h, stt_.get(("store", tc - 2), 0))
            stt_[("ld", tc)] = kb.LD.inc16(sync.dma_start(
                ytg[:, (tc % 2) * 2056:(tc % 2) * 2056 + 2056]
                .rearrange("p (c i) -> p c i", i=257)[:, :, 1:257],
                y_sc.ap()[:, :, 1 + tc * 256:1 + (tc + 1) * 256].transpose([1, 0, 2])))

        def emit_prep(tc):
            par = tc % 2
            b0 = par * 2056
            v.wait_ge(kb.LD.h, stt_[("ld", tc)])
            for c in range(8):
                v.tensor_copy(ytg[:, b0 + c * 257:b0 + c * 257 + 1], guard[:, c:c + 1])
            for c in range(8):
                v.tensor_copy(guard[:, c:c + 1], ytg[:, b0 + c * 257 + 256:b0 + c * 257 + 257])
            v.drain()
            for c in range(8):
                sl = ytg[:, b0 + c * 257:b0 + (c + 1) * 257]
                v.tensor_mul(hsq[:, c * 257:(c + 1) * 257], sl, sl)
            hready = kb.VE.inc(v.engine_nop())
            t.wait_ge(kb.VE.h, hready)
            brow.guard(t)
            for kc in range(8):
                t.matmul(pr[0:1, 0:256], st["onesall"][:, 0:1], ysl(tc, kc, kc >= 4),
                         start=(kc == 0), stop=(kc == 7), skip_group_check=True)
            for kc in range(8):
                sl = hsq[:, kc * 257 + (0 if kc >= 4 else 1):kc * 257 + (0 if kc >= 4 else 1) + 256]
                mm = t.matmul(pr[0:1, 256:512], st["onesall"][:, 0:1], sl,
                              start=False, stop=(kc == 7), skip_group_check=True)
            mmr = kb.TE.inc(mm)
            v.wait_ge(kb.TE.h, mmr)
            v.tensor_scalar(rows[0:1, 0:256], pr[0:1, 0:256], 1.0 / D, None, ALU.mult)
            rd0 = kb.VE.inc(v.tensor_scalar(rows[0:1, 256:512], pr[0:1, 256:512],
                                            1.0 / D, None, ALU.mult))
            brow.read_by(kb.VE.h, rd0)
            v.drain()
            v.tensor_mul(rows[0:1, 512:768], rows[0:1, 0:256], rows[0:1, 0:256])
            v.drain()
            vv = kb.VE.inc(v.tensor_sub(rows[0:1, 256:512], rows[0:1, 256:512],
                                        rows[0:1, 512:768]))
            s.wait_ge(kb.VE.h, vv)
            sr = kb.SE.inc(s.activation(rows[0:1, 512:768], rows[0:1, 256:512], AF.Sqrt,
                                        bias=kb.st["epscol"][0:1, 0:1]))
            v.wait_ge(kb.SE.h, sr)
            v.reciprocal(rows[0:1, 256:512], rows[0:1, 512:768])
            v.drain()
            nm = kb.VE.inc(v.scalar_tensor_tensor(rows[0:1, 512:768], rows[0:1, 0:256], -1.0,
                                                  rows[0:1, 256:512], ALU.mult, ALU.mult))
            t.wait_ge(kb.VE.h, nm)
            bbc.guard(t)
            t.matmul(pbc[0:64, 0:256], st["onesall"][0:1, 0:64], rows[0:1, 256:512],
                     start=True, stop=False, skip_group_check=True)
            t.matmul(pbc[64:128, 0:256], st["onesall"][0:1, 0:64], rows[0:1, 256:512],
                     start=True, stop=False, skip_group_check=True, tile_position=(0, 64))
            t.matmul(pbc[0:64, 256:512], st["onesall"][0:1, 0:64], rows[0:1, 512:768],
                     start=False, stop=False, skip_group_check=True)
            mmb = kb.TE.inc(t.matmul(pbc[64:128, 256:512], st["onesall"][0:1, 0:64],
                                     rows[0:1, 512:768], start=False, stop=True,
                                     skip_group_check=True, tile_position=(0, 64)))
            v.wait_ge(kb.TE.h, mmb)
            last = 0
            for kc in range(8):
                sh = kc >= 4
                sl = hsq[:, kc * 257:kc * 257 + 256]
                v.tensor_tensor(sl, ysl(tc, kc, sh), pbc[:, 0:256], ALU.mult)
                v.drain()
                v.tensor_add(sl, sl, pbc[:, 256:512])
                v.drain()
                last = kb.VE.inc(v.tensor_scalar(
                    hT[:, par * 2048 + kc * 256:par * 2048 + (kc + 1) * 256], sl,
                    lg[:, kc:kc + 1], lb[:, kc:kc + 1], ALU.mult, ALU.add))
            bbc.read_by(kb.VE.h, last)
            stt_[("hT", tc)] = last

        def emit_ag(tc, si):
            s0, s1 = SUBS[si]
            wdt = s1 - s0
            par = si % 2
            bank = pag[par]
            hTb = (tc % 2) * 2048
            bank.guard(t)
            if si == 0:
                t.wait_ge(kb.VE.h, stt_[("hT", tc)])
            for kc in range(8):
                t.matmul(bank.ps[0:wdt, 0:256],
                         w1[:, kc * 5460 + s0:kc * 5460 + s1],
                         hT[:, hTb + kc * 256:hTb + (kc + 1) * 256],
                         start=(kc == 0), stop=False, skip_group_check=True)
            for kc in range(8):
                mmg = t.matmul(bank.ps[0:wdt, 256:512],
                               w1[:, kc * 5460 + 2730 + s0:kc * 5460 + 2730 + s1],
                               hT[:, hTb + kc * 256:hTb + (kc + 1) * 256],
                               start=False, stop=(kc == 7), skip_group_check=True)
            stt_[("ag", si)] = kb.TE.inc(mmg)
            rc = s0 // 128
            s.wait_ge(kb.TE.h, stt_[("ag", si)])
            s.wait_ge(kb.VE.h, stt_.get(("mdone", si - 2), 0))
            sg = kb.SE.inc(s.activation(gel[0:wdt, par * 256:par * 256 + 256],
                                        bank.ps[0:wdt, 256:512], AF.Gelu,
                                        bias=b1g[0:wdt, rc:rc + 1]))
            v.wait_ge(kb.SE.h, sg)
            v.tensor_scalar(abuf[0:wdt, par * 256:par * 256 + 256], bank.ps[0:wdt, 0:256],
                            b1a[0:wdt, rc:rc + 1], None, ALU.add)
            v.drain()
            md = kb.VE.inc(v.tensor_mul(mT[0:wdt, par * 256:par * 256 + 256],
                                        abuf[0:wdt, par * 256:par * 256 + 256],
                                        gel[0:wdt, par * 256:par * 256 + 256]))
            bank.read_by(kb.VE.h, md)
            stt_[("mdone", si)] = md

        def emit_w2(si, first_any, last_any):
            s0, s1 = SUBS[si]
            wdt = s1 - s0
            par = si % 2
            rc = s0 // 128
            t.wait_ge(kb.VE.h, stt_[("mdone", si)])
            for oc in range(8):
                bank, half = py[oc // 2], (oc % 2) * 256
                if first_any and oc % 2 == 0:
                    bank.guard(t)
                mm2 = t.matmul(bank.ps[:, half:half + 256],
                               w2[0:wdt, rc * 1024 + oc * 128:rc * 1024 + (oc + 1) * 128],
                               mT[0:wdt, par * 256:par * 256 + 256],
                               start=(first_any and oc % 2 == 0), stop=last_any,
                               skip_group_check=True)
            return kb.TE.inc(mm2)

        NS = len(SUBS)
        emit_load(0)
        emit_prep(0)
        for tc in range(16):
            if tc + 1 < 16:
                emit_load(tc + 1)
            for si in range(NS + 1):
                if si < NS:
                    emit_ag(tc, si)
                if si >= 1:
                    y2tok = emit_w2(si - 1, si - 1 == 0, si - 1 == NS - 1)
                if si == 2 and tc + 1 < 16:
                    emit_prep(tc + 1)
            par = tc % 2
            b0 = par * 2056
            v.wait_ge(kb.TE.h, y2tok)
            for oc in range(8):
                bank, half = py[oc // 2], (oc % 2) * 256
                yo = kb.VE.inc(v.scalar_tensor_tensor(
                    ytg[:, b0 + oc * 257 + 1:b0 + oc * 257 + 257], bank.ps[:, half:half + 256],
                    b2c[:, oc:oc + 1], ytg[:, b0 + oc * 257 + 1:b0 + oc * 257 + 257],
                    ALU.add, ALU.add))
                bank.read_by(kb.VE.h, yo)
            gp.wait_ge(kb.VE.h, kb.VE.n)
            stt_[("store", tc)] = kb.GD.inc16(gp.dma_start(
                y_sc.ap()[:, :, 1 + tc * 256:1 + (tc + 1) * 256].transpose([1, 0, 2]),
                ytg[:, b0:b0 + 2056].rearrange("p (c i) -> p c i", i=257)[:, :, 1:257]))
        gp.wait_ge(kb.GD.h, kb.GD.n)
        done = kb.GD.n
    for e in (kb.sync, kb.t, kb.v, kb.s):
        e.wait_ge(kb.GD.h, done)



def emit_weight_quant(kb, wdram, kcs, wq, stage, fp8st, qcols, blocks):
    sync, v = kb.sync, kb.v
    N = blocks * 128
    for kc in range(kcs):
        ldw = kb.LD.inc16(sync.dma_start(stage[:, 0:N], wdram.ap()[kc * 128:(kc + 1) * 128, :]))
        v.wait_ge(kb.LD.h, ldw)
        tok = kb.VE.inc(emit_quant(kb, stage[:, 0:N], stage[:, 0:N], fp8st,
                                   wq[:, kc * N:(kc + 1) * N], qcols, blocks))
        sync.wait_ge(kb.VE.h, tok)
    return kb.VE.n


def emit_dense_block(kb, x0_src, y_sc, x1_sc, x2_sc, qT_sc, ins):
    import contextlib
    nc, sync, gp, t, v, s = kb.nc, kb.sync, kb.gp, kb.t, kb.v, kb.s
    st = kb.st
    es = contextlib.ExitStack()
    a = lambda n, sh, dt: es.enter_context(nc.sbuf_tensor("dn_" + n, sh, dt))
    kT = a("kT", [128, 8 * 4096], BF16)
    vres = a("vres", [128, 32 * 1040], BF16)
    bcols = a("bcols", [128, 24], F32)
    obias = a("obias", [128, 8], F32)
    vbrow = a("vbrow", [1, 1024], BF16)
    ncq = a("ncq", [2, 26], F32)
    negrow = a("negrow", [1, 16], F32)
    negc16 = a("negc16", [128, 16], F32)
    qcolsw = a("qcolsw", [128, 72], F32)
    with es:
        es1 = contextlib.ExitStack()
        b = lambda n, sh, dt: es1.enter_context(nc.sbuf_tensor("b1_" + n, sh, dt))
        wq = b("wq", [128, 8 * 3072], BF16)
        fp8s = b("fp8s", [128, 3072], FP8)
        with nc.sbuf_tensor("b1_wst", [128, 3072], F32) as wst:
            emit_weight_quant(kb, ins["L0_attn_qkv_kernel"], 8, wq, wst,
                              fp8s[:, 0:3072], qcolsw, 24)
        xt = b("xt", [128, 1024], F32)
        zt = b("zt", [128, 1024], F32)
        sqt = b("sqt", [128, 1024], F32)
        zdq = b("zdq", [128, 1024], BF16)
        zdqT = b("zdqT", [128, 1024], BF16)
        qst = b("qst", [128, 8 * 128], BF16)
        cols = b("cols", [128, 16], F32)
        q2f = b("q2f", [128, 128], F32)
        with es1, \
             nc.psum_tensor("b1_p0", [128, 512], F32) as p0, \
             nc.psum_tensor("b1_p1", [128, 512], F32) as p1, \
             nc.psum_tensor("b1_p2", [128, 512], F32) as p2, \
             nc.psum_tensor("b1_p3", [128, 512], F32) as p3, \
             nc.psum_tensor("b1_pn", [128, 1024], F32) as pn, \
             nc.psum_tensor("b1_pb", [128, 1024], BF16) as pb, \
             nc.psum_tensor("b1_pr", [2, 512], F32) as pr:
            bk = [Bank(p) for p in (p0, p1, p2, p3)]
            bn, bb, br = Bank(pn), Bank(pb), Bank(pr)
            with nc.allow_non_contiguous_dma(reason="small bias cols"):
                kb.LD.inc16(sync.dma_start(
                    bcols[:], ins["L0_attn_qkv_bias"].ap().rearrange("(c p) -> p c", p=128)))
                btok = kb.LD.inc16(sync.dma_start(
                    obias[:], ins["L0_attn_out_bias"].ap().rearrange("(c p) -> p c", p=128)))
            kb.GD.inc16(gp.dma_start(vbrow[:], ins["L0_attn_qkv_bias"].ap()[2048:3072].unsqueeze(0)))
            v.memset(ncq[:], 0.0)
            vo = kb.VE.inc(v.memset(
                vres[:].rearrange("p (a h i) -> p a h i", h=16, i=65)[:, :, :, 64:65], 1.0))
            s.wait_ge(kb.LD.h, btok)
            t.wait_ge(kb.VE.h, vo)
            t.wait_ge(kb.GD.h, kb.GD.n)
            for tc in range(32):
                sync.wait_ge(kb.TE.h, kb.TE.n)
                sync.wait_ge(kb.VE.h, kb.VE.n)
                sync.wait_ge(kb.GD.h, kb.GD.n)
                ldx = kb.LD.inc16(sync.dma_start(xt[:], x0_src.ap()[tc * 128:(tc + 1) * 128, :]))
                ldy = kb.LD.inc16(sync.dma_start(
                    zt[:].rearrange("p (c i) -> p c i", i=128),
                    y_sc.ap()[:, :, 1 + tc * 128:1 + (tc + 1) * 128].transpose([1, 0, 2])))
                t.wait_ge(kb.LD.h, ldy)
                bn.guard(t)
                for c in range(8):
                    mm = t.transpose(pn[:, c * 128:(c + 1) * 128], zt[:, c * 128:(c + 1) * 128],
                                     st["identf"][:])
                ty = kb.TE.inc(mm)
                v.wait_ge(kb.TE.h, ty)
                v.wait_ge(kb.LD.h, ldx)
                v.wait_ge(kb.SE.h, kb.SE.n)
                x1done = kb.VE.inc(v.scalar_tensor_tensor(xt[:], pn[:], 0.5, xt[:],
                                                          ALU.mult, ALU.add))
                bn.read_by(kb.VE.h, x1done)
                gp.wait_ge(kb.VE.h, x1done)
                kb.GD.inc16(gp.dma_start(x1_sc.ap()[tc * 128:(tc + 1) * 128, :], xt[:]))
                s.wait_ge(kb.VE.h, x1done)
                emit_ln_natural(kb, xt[:], zt[:], cols, sqt)
                qd = kb.VE.inc(emit_quant(kb, zt[:], sqt[:], fp8s[:, 0:1024], zdq[:],
                                          qcolsw, 8))
                t.wait_ge(kb.VE.h, qd)
                bb.guard(t)
                for c in range(8):
                    mm = t.transpose(pb[:, c * 128:(c + 1) * 128],
                                     zdq[:, c * 128:(c + 1) * 128], st["identb"][:])
                tz = kb.TE.inc(mm)
                s.wait_ge(kb.TE.h, tz)
                ztok = kb.SE.inc(s.copy(zdqT[:], pb[:]))
                bb.read_by(kb.SE.h, ztok)
                t.wait_ge(kb.SE.h, ztok)
                for oc in range(16):
                    par = oc % 2
                    pq = bk[par]
                    pq.guard(t)
                    for c in range(8):
                        mm = t.matmul(pq.ps[:, 0:128],
                                      wq[:, c * 3072 + oc * 128:c * 3072 + (oc + 1) * 128],
                                      zdqT[:, c * 128:(c + 1) * 128],
                                      start=(c == 0), stop=(c == 7))
                    mq = kb.TE.inc(mm)
                    is_q = oc < 8
                    dst = (qst[:, oc * 128:(oc + 1) * 128] if is_q else
                           kT[:, (oc - 8) * 4096 + tc * 128:(oc - 8) * 4096 + (tc + 1) * 128])
                    v.wait_ge(kb.TE.h, mq)
                    ev = kb.VE.inc(v.tensor_scalar(dst, pq.ps[:, 0:128],
                                                   bcols[:, oc:oc + 1], None, ALU.add))
                    pq.read_by(kb.VE.h, ev)
                    s.wait_ge(kb.VE.h, ev)
                    sq2 = kb.SE.inc(s.activation(q2f[:], dst, AF.Square))
                    t.wait_ge(kb.SE.h, sq2)
                    br.guard(t)
                    ms = kb.TE.inc(t.matmul(pr[:, 0:128], st["ones2f"][:], q2f[:],
                                            start=True, stop=True))
                    v.wait_ge(kb.TE.h, ms)
                    v.tensor_reduce(ncq[0:2, 24:25], pr[:, 0:128], AX.X, ALU.max)
                    v.drain()
                    base = 0 if is_q else 8
                    cidx = base + (oc % 8)
                    mx = kb.VE.inc(v.tensor_max(ncq[0:2, cidx:cidx + 1],
                                                ncq[0:2, cidx:cidx + 1], ncq[0:2, 24:25]))
                    br.read_by(kb.VE.h, mx)
                    v.drain()
                gp.wait_ge(kb.SE.h, kb.SE.n)
                kb.GD.inc16(gp.dma_start(
                    qT_sc.ap()[:, :, tc * 128:(tc + 1) * 128].transpose([1, 0, 2]),
                    qst[:].rearrange("p (c i) -> p c i", i=128)))
                for vc in range(2):
                    pv = bk[2 + vc]
                    pv.guard(t)
                    for c in range(8):
                        t.matmul(pv.ps[:], zdqT[:, c * 128:(c + 1) * 128],
                                 wq[:, c * 3072 + 2048 + vc * 512:c * 3072 + 2048 + (vc + 1) * 512],
                                 start=(c == 0), stop=False)
                    mmv = t.matmul(pv.ps[:], st["onesrowb"][:, :],
                                   vbrow[:, vc * 512:(vc + 1) * 512],
                                   start=False, stop=True)
                    mvt = kb.TE.inc(mmv)
                    s.wait_ge(kb.TE.h, mvt)
                    vz = kb.SE.inc(s.copy(
                        vres[:, tc * 1040 + vc * 520:tc * 1040 + (vc + 1) * 520]
                        .rearrange("p (h i) -> p h i", i=65)[:, :, 0:64],
                        pv.ps[:].rearrange("p (h i) -> p h i", i=64)))
                    pv.read_by(kb.SE.h, vz)
            v.wait_ge(kb.SE.h, kb.SE.n)
            s.wait_ge(kb.VE.h, kb.VE.n)
            sqn = kb.SE.inc(s.activation(ncq[0:2, 16:24], ncq[0:2, 0:8], AF.Sqrt))
            s.drain()
            sqk = kb.SE.inc(s.activation(ncq[0:2, 0:8], ncq[0:2, 8:16], AF.Sqrt))
            v.wait_ge(kb.SE.h, sqk)
            v.tensor_mul(ncq[0:2, 8:16], ncq[0:2, 16:24], ncq[0:2, 0:8])
            v.drain()
            cn = kb.VE.inc(v.tensor_scalar(ncq[0:2, 16:24], ncq[0:2, 8:16],
                                           -0.1255, None, ALU.mult))
            gp.wait_ge(kb.VE.h, cn)
            with nc.allow_non_contiguous_dma(reason="tiny negC shuffle"):
                gp.dma_start(negrow[:].rearrange("a (c p) -> a c p", p=2)[:, :, 0:1],
                             ncq[0:1, 16:24]).then_inc(kb.GD.h, 16)
                kb.GD.n += 16
                ng = kb.GD.inc16(gp.dma_start(
                    negrow[:].rearrange("a (c p) -> a c p", p=2)[:, :, 1:2],
                    ncq[1:2, 16:24]))
            t.wait_ge(kb.GD.h, ng)
            bn.guard(t)
            mmn = kb.TE.inc(t.matmul(pn[:, 0:16], st["onesall"][0:1, :], negrow[:],
                                     start=True, stop=True))
            s.wait_ge(kb.TE.h, mmn)
            nct = kb.SE.inc(s.copy(negc16[:], pn[:, 0:16]))
            bn.read_by(kb.SE.h, nct)
            for e in (kb.sync, kb.t, kb.v):
                e.wait_ge(kb.SE.h, kb.SE.n)
            fin1 = kb.GD.n
            gp.wait_ge(kb.GD.h, fin1)
        es2 = contextlib.ExitStack()
        b2 = lambda n, sh, dt: es2.enter_context(nc.sbuf_tensor("b2_" + n, sh, dt))
        wout = b2("wout", [128, 8 * 1024], BF16)
        fp8s = b2("fp8s2", [128, 1024], FP8)
        with nc.sbuf_tensor("b2_wst2", [128, 1024], F32) as wst2:
            emit_weight_quant(kb, ins["L0_attn_out_kernel"], 8, wout, wst2,
                              fp8s[:, 0:1024], qcolsw, 8)
        qT = b2("qT", [128, 8 * 512], BF16)
        expb = b2("expb", [128, 4 * 512], BF16)
        oT = b2("oT", [128, 8 * 512], BF16)
        odqT = b2("odqT", [128, 8 * 512], BF16)
        oNat = b2("oNat", [128, 1024], BF16)
        odq = b2("odq", [128, 1024], BF16)
        outT = b2("outT", [128, 8 * 512], BF16)
        x1t = b2("x1t", [128, 1024], F32)
        z3 = b2("z3", [128, 1024], F32)
        sq3 = b2("sq3", [128, 1024], F32)
        x2t = x1t
        ys3 = sq3
        rowr = b2("rowr", [128, 512], F32)
        bcf = b2("bcf", [128, 512], F32)
        cols2 = b2("cols2", [128, 16], F32)
        with es2, \
             nc.psum_tensor("b2_a0", [128, 512], F32) as pa0, \
             nc.psum_tensor("b2_a1", [128, 512], F32) as pa1, \
             nc.psum_tensor("b2_b0", [128, 512], F32) as pb0, \
             nc.psum_tensor("b2_b1", [128, 512], F32) as pb1, \
             nc.psum_tensor("b2_v0", [128, 512], F32) as pv0, \
             nc.psum_tensor("b2_v1", [128, 512], F32) as pv1, \
             nc.psum_tensor("b2_sr", [1, 512], F32) as psr, \
             nc.psum_tensor("b2_bT", [128, 1024], BF16) as pbT:
            bA = [Bank(pa0), Bank(pa1)]
            bB = [Bank(pb0), Bank(pb1)]
            bpA, bpB, bsr, bbt = Bank(pv0), Bank(pv1), Bank(psr), Bank(pbT)
            t.wait_ge(kb.VE.h, kb.VE.n)
            for qc in range(8):
                sync.wait_ge(kb.TE.h, kb.TE.n)
                ldq = kb.LD.inc16(sync.dma_start(
                    qT[:].rearrange("p (c i) -> p c i", i=512),
                    qT_sc.ap()[:, :, qc * 512:(qc + 1) * 512].transpose([1, 0, 2])))
                t.wait_ge(kb.LD.h, ldq)
                for hp in range(8):
                    h0, h1 = 2 * hp, 2 * hp + 1
                    bpA.guard(t)
                    bpB.guard(t)
                    bsr.guard(t)
                    exp_tok = {}
                    for it in range(34):
                        if it < 32:
                            kc = it
                            par = kc % 2
                            pA, pB = bA[par], bB[par]
                            pA.guard(t)
                            t.matmul(pA.ps[:], kT[0:64, hp * 4096 + kc * 128:hp * 4096 + (kc + 1) * 128],
                                     qT[0:64, hp * 512:(hp + 1) * 512],
                                     start=True, stop=True)
                            pB.guard(t)
                            mmB = t.matmul(pB.ps[:], kT[64:128, hp * 4096 + kc * 128:hp * 4096 + (kc + 1) * 128],
                                           qT[64:128, hp * 512:(hp + 1) * 512],
                                           start=True, stop=True, tile_position=(64, 0))
                            kb.TE.inc(mmB)
                            sc_tok = kb.TE.n
                        if 1 <= it <= 32:
                            kc = it - 1
                            par = kc % 2
                            pA, pB = bA[par], bB[par]
                            s.wait_ge(kb.TE.h, exp_tok.get(("sc", kc), kb.TE.n))
                            s.wait_ge(kb.TE.h, exp_tok.get(("pv", kc - 2), 0))
                            exA = kb.SE.inc(s.activation(expb[:, (par * 2) * 512:(par * 2) * 512 + 512],
                                                         pA.ps[:], AF.Exp, bias=negc16[:, h0:h0 + 1],
                                                         scale=0.125))
                            pA.read_by(kb.SE.h, exA)
                            exB = kb.SE.inc(s.activation(expb[:, (par * 2 + 1) * 512:(par * 2 + 1) * 512 + 512],
                                                         pB.ps[:], AF.Exp, bias=negc16[:, h1:h1 + 1],
                                                         scale=0.125))
                            pB.read_by(kb.SE.h, exB)
                            exp_tok[("ex", kc)] = exB
                        if it >= 2:
                            kc = it - 2
                            par = kc % 2
                            t.wait_ge(kb.SE.h, exp_tok[("ex", kc)])
                            eA = expb[:, (par * 2) * 512:(par * 2) * 512 + 512]
                            eB = expb[:, (par * 2 + 1) * 512:(par * 2 + 1) * 512 + 512]
                            t.matmul(pv0[0:65, :], vres[:, kc * 1040 + h0 * 65:kc * 1040 + h0 * 65 + 65],
                                     eA, start=(kc == 0), stop=(kc == 31), skip_group_check=True)
                            t.matmul(pv1[64:128, :], vres[:, kc * 1040 + h1 * 65:kc * 1040 + h1 * 65 + 64],
                                     eB, start=(kc == 0), stop=(kc == 31), skip_group_check=True,
                                     tile_position=(0, 64))
                            pvt = kb.TE.inc(t.matmul(psr[0:1, :], st["onesallb"][:, 0:1], eB,
                                                     start=(kc == 0), stop=(kc == 31),
                                                     skip_group_check=True))
                            exp_tok[("pv", kc)] = pvt
                        if it < 32:
                            exp_tok[("sc", it)] = sc_tok
                    v.wait_ge(kb.TE.h, pvt)
                    rc0 = kb.VE.inc(v.reciprocal(rowr[64:65, :], pv0[64:65, :]))
                    t.wait_ge(kb.VE.h, rc0)
                    bA[0].guard(t)
                    mb0 = kb.TE.inc(t.matmul(pa0[0:64, :], st["onesall"][64:65, 0:64],
                                             rowr[64:65, :], start=True, stop=True))
                    s.wait_ge(kb.TE.h, mb0)
                    sc0 = kb.SE.inc(s.copy(bcf[0:64, :], pa0[0:64, :]))
                    bA[0].read_by(kb.SE.h, sc0)
                    v.wait_ge(kb.SE.h, sc0)
                    o0 = kb.VE.inc(v.tensor_tensor(oT[0:64, hp * 512:(hp + 1) * 512],
                                                   pv0[0:64, :], bcf[0:64, :], ALU.mult))
                    bpA.read_by(kb.VE.h, o0)
                    rc1 = kb.VE.inc(v.reciprocal(rowr[0:1, :], psr[0:1, :]))
                    bsr.read_by(kb.VE.h, rc1)
                    t.wait_ge(kb.VE.h, rc1)
                    bA[1].guard(t)
                    mb1 = kb.TE.inc(t.matmul(pa1[64:128, :], st["onesall"][0:1, 0:64],
                                             rowr[0:1, :], start=True, stop=True,
                                             tile_position=(0, 64)))
                    s.wait_ge(kb.TE.h, mb1)
                    sc1 = kb.SE.inc(s.copy(bcf[64:128, :], pa1[64:128, :]))
                    bA[1].read_by(kb.SE.h, sc1)
                    v.wait_ge(kb.SE.h, sc1)
                    o1 = kb.VE.inc(v.tensor_tensor(oT[64:128, hp * 512:(hp + 1) * 512],
                                                   pv1[64:128, :], bcf[64:128, :], ALU.mult))
                    bpB.read_by(kb.VE.h, o1)
                for tt in range(4):
                    oTready = kb.VE.n
                    t.wait_ge(kb.VE.h, oTready)
                    bbt.guard(t)
                    for c in range(8):
                        mm = t.transpose(pbT[:, c * 128:(c + 1) * 128],
                                         oT[:, c * 512 + tt * 128:c * 512 + (tt + 1) * 128],
                                         st["identb"][:])
                    to1 = kb.TE.inc(mm)
                    s.wait_ge(kb.TE.h, to1)
                    so1 = kb.SE.inc(s.copy(oNat[:], pbT[:]))
                    bbt.read_by(kb.SE.h, so1)
                    v.wait_ge(kb.SE.h, so1)
                    qo = kb.VE.inc(emit_quant(kb, oNat[:], sq3[:], fp8s[:, 0:1024],
                                              odq[:], qcolsw, 8))
                    t.wait_ge(kb.VE.h, qo)
                    bbt.guard(t)
                    for c in range(8):
                        mm = t.transpose(pbT[:, c * 128:(c + 1) * 128],
                                         odq[:, c * 128:(c + 1) * 128], st["identb"][:])
                    to2 = kb.TE.inc(mm)
                    s.wait_ge(kb.TE.h, to2)
                    so2 = kb.SE.inc(s.copy(
                        odqT[:].rearrange("p (c i) -> p c i", i=512)[:, :, tt * 128:(tt + 1) * 128],
                        pbT[:].rearrange("p (c i) -> p c i", i=128)))
                    bbt.read_by(kb.SE.h, so2)
                odqready = kb.SE.n
                t.wait_ge(kb.SE.h, odqready)
                for oc in range(8):
                    par = oc % 2
                    pO = bA[par]
                    pO.guard(t)
                    for c in range(8):
                        mm = t.matmul(pO.ps[:], wout[:, c * 1024 + oc * 128:c * 1024 + (oc + 1) * 128],
                                      odqT[:, c * 512:(c + 1) * 512],
                                      start=(c == 0), stop=(c == 7))
                    mo = kb.TE.inc(mm)
                    v.wait_ge(kb.TE.h, mo)
                    ot = kb.VE.inc(v.tensor_scalar(outT[:, oc * 512:(oc + 1) * 512],
                                                   pO.ps[:], obias[:, oc:oc + 1], None, ALU.add))
                    pO.read_by(kb.VE.h, ot)
                outTready = kb.VE.n
                t.wait_ge(kb.VE.h, outTready)
                for tt in range(4):
                    tok0 = tc0 = qc * 512 + tt * 128
                    sync.wait_ge(kb.VE.h, kb.VE.n)
                    sync.wait_ge(kb.SE.h, kb.SE.n)
                    sync.wait_ge(kb.GD.h, kb.GD.n)
                    ldx1 = kb.LD.inc16(sync.dma_start(x1t[:], x1_sc.ap()[tc0:tc0 + 128, :]))
                    bbt.guard(t)
                    for c in range(8):
                        mm = t.transpose(pbT[:, c * 128:(c + 1) * 128],
                                         outT[:, c * 512 + tt * 128:c * 512 + (tt + 1) * 128],
                                         st["identb"][:])
                    tx = kb.TE.inc(mm)
                    v.wait_ge(kb.TE.h, tx)
                    v.wait_ge(kb.LD.h, ldx1)
                    x2d = kb.VE.inc(v.tensor_add(x2t[:], x1t[:], pbT[:]))
                    bbt.read_by(kb.VE.h, x2d)
                    gp.wait_ge(kb.VE.h, x2d)
                    kb.GD.inc16(gp.dma_start(x2_sc.ap()[tc0:tc0 + 128, :], x2t[:]))
                    s.wait_ge(kb.VE.h, x2d)
                    emit_ln_natural(kb, x2t[:], z3[:], cols2, sq3)
                    zl = kb.VE.n
                    t.wait_ge(kb.VE.h, zl)
                    s.wait_ge(kb.GD.h, kb.GD.n)
                    for half in range(2):
                        bA[0].guard(t)
                        for c in range(4):
                            cc = half * 4 + c
                            mm = t.transpose(pa0[:, c * 128:(c + 1) * 128],
                                             z3[:, cc * 128:(cc + 1) * 128], st["identf"][:])
                        tz3 = kb.TE.inc(mm)
                        s.wait_ge(kb.TE.h, tz3)
                        sz3 = kb.SE.inc(s.copy(ys3[:, half * 512:(half + 1) * 512], pa0[:]))
                        bA[0].read_by(kb.SE.h, sz3)
                    gp.wait_ge(kb.SE.h, kb.SE.n)
                    kb.GD.inc16(gp.dma_start(
                        y_sc.ap()[:, :, 1 + tc0:1 + tc0 + 128].transpose([1, 0, 2]),
                        ys3[:].rearrange("p (c i) -> p c i", i=128)))
        gp.wait_ge(kb.GD.h, kb.GD.n)
        done = kb.GD.n
    for e in (kb.sync, kb.t, kb.v, kb.s):
        e.wait_ge(kb.GD.h, done)



def emit_Cfin(kb, x2_sc, y_sc, out_dram):
    nc, sync, gp, t, v, s = kb.nc, kb.sync, kb.gp, kb.t, kb.v, kb.s
    st = kb.st
    with nc.sbuf_tensor("cf_x", [128, 1024], F32) as xt, \
         nc.sbuf_tensor("cf_y", [128, 1024], F32) as yt, \
         nc.sbuf_tensor("cf_o", [128, 2 * 1024], F32) as ot, \
         nc.psum_tensor("cf_p", [128, 1024], F32) as pn:
        bn = Bank(pn)
        for tc in range(32):
            par = tc % 2
            o = ot[:, par * 1024:(par + 1) * 1024]
            sync.wait_ge(kb.GD.h, max(0, kb.GD.n - 16))
            sync.wait_ge(kb.TE.h, kb.TE.n)
            ldx = kb.LD.inc16(sync.dma_start(xt[:], x2_sc.ap()[tc * 128:(tc + 1) * 128, :]))
            ldy = kb.LD.inc16(sync.dma_start(
                yt[:].rearrange("p (c i) -> p c i", i=128),
                y_sc.ap()[:, :, 1 + tc * 128:1 + (tc + 1) * 128].transpose([1, 0, 2])))
            t.wait_ge(kb.LD.h, ldy)
            bn.guard(t)
            for c in range(8):
                mm = t.transpose(pn[:, c * 128:(c + 1) * 128], yt[:, c * 128:(c + 1) * 128],
                                 st["identf"][:])
            ty = kb.TE.inc(mm)
            v.wait_ge(kb.TE.h, ty)
            v.wait_ge(kb.LD.h, ldx)
            v.wait_ge(kb.GD.h, max(0, kb.GD.n - 16))
            xo = kb.VE.inc(v.scalar_tensor_tensor(o, pn[:], 0.5, xt[:], ALU.mult, ALU.add))
            bn.read_by(kb.VE.h, xo)
            gp.wait_ge(kb.VE.h, xo)
            kb.GD.inc16(gp.dma_start(out_dram.ap()[tc * 128:(tc + 1) * 128, :], o))
        gp.wait_ge(kb.GD.h, kb.GD.n)



WSHAPES_ALL = {
    "L0_attn_qkv_kernel": (1024, 3072), "L0_attn_qkv_bias": (3072,),
    "L0_attn_out_kernel": (1024, 1024), "L0_attn_out_bias": (1024,),
    "L0_ffn_wqkv": (2, 1024, 1536), "L0_ffn_q_scale": (2, 64),
    "L0_ffn_k_scale": (2, 64), "L0_ffn_wo": (2, 512, 1024),
    "L0_ffn_ln_g": (2, 1024), "L0_ffn_ln_b": (2, 1024),
    "L0_ffn_w1": (2, 1024, 5460), "L0_ffn_b1": (2, 5460),
    "L0_ffn_w2": (2, 2730, 1024), "L0_ffn_b2": (2, 1024),
}


def build_full():
    nc = bass.Bass("TRN2", target_bir_lowering=False, detect_race_conditions=False)
    ins = {n: nc.dram_tensor(n, list(sh), F32, kind="ExternalInput")
           for n, sh in WSHAPES_ALL.items()}
    xcore = nc.dram_tensor("xcore", [T, D], F32, kind="ExternalInput")
    consts = host_constants()
    cds = {}
    for n, a in consts.items():
        a = np.asarray(a)
        dt = F32 if a.dtype == np.float32 else BF16
        cds[n] = nc.dram_tensor("c_" + n, list(a.shape), dt, kind="ExternalInput")
    y_sc = nc.dram_tensor("y_sc", [8, 128, 1 + T], F32, kind="Internal")
    x1_sc = nc.dram_tensor("x1_sc", [T, D], F32, kind="Internal")
    x2_sc = nc.dram_tensor("x2_sc", [T, D], F32, kind="Internal")
    qT_sc = nc.dram_tensor("qT_sc", [8, 128, T], BF16, kind="Internal")
    xout = nc.dram_tensor("xout", [T, D], F32, kind="ExternalOutput")
    A_dram = [cds[n] for n in ("A1q", "A2q", "A1k", "A2k")]
    kb = KB(nc)
    alloc_shared(kb)
    emit_shared_init(kb, cds)
    emit_A0(kb, xcore, y_sc)
    for l in (0, 1):
        emit_ffn_attn(kb, l, y_sc, ins, A_dram)
        emit_ffn_mlp(kb, l, y_sc, ins)
    emit_dense_block(kb, xcore, y_sc, x1_sc, x2_sc, qT_sc, ins)
    for l in (0, 1):
        emit_ffn_attn(kb, l, y_sc, ins, A_dram)
        emit_ffn_mlp(kb, l, y_sc, ins)
    emit_Cfin(kb, x2_sc, y_sc, xout)
    return nc, consts


def kernel(**inputs):
    from concourse.bass_utils import run_bass_kernel_spmd
    nc, consts = build_full()
    base = {n: np.ascontiguousarray(np.asarray(inputs[n], dtype=np.float32))
            for n in WSHAPES_ALL}
    for n, a in consts.items():
        base["c_" + n] = np.ascontiguousarray(a)
    x = np.asarray(inputs["L0_attn_x"], dtype=np.float32)
    in_maps = []
    for i in range(8):
        m = dict(base)
        m["xcore"] = np.ascontiguousarray(x[i])
        in_maps.append(m)
    res = run_bass_kernel_spmd(nc, in_maps, core_ids=list(range(8)))
    return np.stack([np.asarray(r["xout"]) for r in res.results], axis=0)


# version 12
# speedup vs baseline: 1.8767x; 1.3203x over previous
import numpy as np
import ml_dtypes
import concourse.bass as bass
import concourse.mybir as mybir

F32 = mybir.dt.float32
BF16 = mybir.dt.bfloat16
FP8 = mybir.dt.float8e4
AF = mybir.ActivationFunctionType
ALU = mybir.AluOpType
AX = mybir.AxisListType

T = 4096
D = 1024
INV224 = float(np.float32(1.0 / 224.0))


class Sem:
    def __init__(self, nc, name):
        self.h = nc.alloc_semaphore(name)
        self.n = 0

    def inc(self, instr, k=1):
        instr.then_inc(self.h, k)
        self.n += k
        return self.n

    def inc16(self, instr):
        return self.inc(instr, 16)


class KB:
    def __init__(self, nc):
        self.nc = nc
        self.LD = Sem(nc, "LD")
        self.GD = Sem(nc, "GD")
        self.TE = Sem(nc, "TE")
        self.VE = Sem(nc, "VE")
        self.SE = Sem(nc, "SE")
        self.GP = Sem(nc, "GP")
        self.sync, self.gp = nc.sync, nc.gpsimd
        self.t, self.v, self.s = nc.tensor, nc.vector, nc.scalar
        self.st = {}
        self._uid = 0

    def uid(self):
        self._uid += 1
        return self._uid


def host_constants():
    DIM_HEAD, WINDOW, QK_SCALE = 64, 512, 8.0
    half_idx = np.arange(0, DIM_HEAD, 2, dtype=np.float64)
    inv_freq = 1.0 / 10000.0 ** (half_idx / DIM_HEAD)
    tt = np.arange(T, dtype=np.float64)
    freqs = tt[:, None] * inv_freq[None, :]
    freqs = np.concatenate([freqs, freqs], axis=-1)
    base = (half_idx + 0.4 * DIM_HEAD) / (1.4 * DIM_HEAD)
    power = (tt - T // 2) / float(WINDOW // 2)
    scale = base[None, :] ** power[:, None]
    scale = np.concatenate([scale, scale], axis=-1)
    cos, sin = np.cos(freqs), np.sin(freqs)
    sign = np.where(np.arange(DIM_HEAD) < 32, -1.0, 1.0)
    a1q = (QK_SCALE * cos * scale).T
    a2q = (QK_SCALE * sin * scale).T * sign[:, None]
    a1k = (cos / scale).T
    a2k = (sin / scale).T * sign[:, None]
    rep = lambda a: np.concatenate([a, a], axis=0).astype(np.float32)
    r = np.arange(128)
    prot = np.zeros((128, 128), np.float32)
    for m in range(128):
        prot[(m + 32) % 64 + 64 * (m // 64), m] = 1.0
    return {
        "A1q": rep(a1q), "A2q": rep(a2q), "A1k": rep(a1k), "A2k": rep(a2k),
        "identb": np.eye(128, dtype=ml_dtypes.bfloat16),
        "identf": np.eye(128, dtype=np.float32),
        "maskL": (r[None, :] <= r[:, None]).astype(ml_dtypes.bfloat16),
        "maskU": (r[:, None] <= r[None, :]).astype(ml_dtypes.bfloat16),
        "prot": prot,
        "onesall": np.ones((128, 128), np.float32),
        "onesallb": np.stack([r * 0 + 1, r * 0 + 1, (r < 64), (r >= 64)], 1
                             ).astype(ml_dtypes.bfloat16),
        "onesrowb": np.ones((1, 128), ml_dtypes.bfloat16),
        "ones2f": np.stack([(r < 64).astype(np.float32), (r >= 64).astype(np.float32)], 1),
        "bc2f": np.stack([(r < 64).astype(np.float32), (r >= 64).astype(np.float32)])[:, :],
        "epscol": np.full((128, 1), 1e-5, np.float32),
        "eps24c": np.full((128, 1), 1e-24, np.float32),
    }


def alloc_shared(kb):
    nc, st = kb.nc, kb.st
    st["identb"] = nc.alloc_sbuf_tensor("sb_identb", [128, 128], BF16)
    st["identf"] = nc.alloc_sbuf_tensor("sb_identf", [128, 128], F32)
    st["maskL"] = nc.alloc_sbuf_tensor("sb_maskL", [128, 128], BF16)
    st["maskU"] = nc.alloc_sbuf_tensor("sb_maskU", [128, 128], BF16)
    st["prot"] = nc.alloc_sbuf_tensor("sb_prot", [128, 128], F32)
    st["onesall"] = nc.alloc_sbuf_tensor("sb_onesall", [128, 128], F32)
    st["onesallb"] = nc.alloc_sbuf_tensor("sb_onesallb", [128, 4], BF16)
    st["onesrowb"] = nc.alloc_sbuf_tensor("sb_onesrowb", [1, 128], BF16)
    st["ones2f"] = nc.alloc_sbuf_tensor("sb_ones2f", [128, 2], F32)
    st["epscol"] = nc.alloc_sbuf_tensor("sb_epscol", [128, 1], F32)
    st["eps24c"] = nc.alloc_sbuf_tensor("sb_eps24c", [128, 1], F32)
    st["bc2f"] = nc.alloc_sbuf_tensor("sb_bc2f", [2, 128], F32)


def emit_shared_init(kb, consts_in):
    sync, v = kb.sync, kb.v
    st = kb.st
    for name in ("identb", "identf", "maskL", "maskU", "prot", "onesall",
                 "onesallb", "onesrowb", "ones2f", "epscol", "eps24c", "bc2f"):
        kb.LD.inc16(sync.dma_start(st[name][:], consts_in[name].ap()))
    kb.VE.inc(v.engine_nop())



def emit_ln_natural(kb, x, z, cols, sq):
    v, s = kb.v, kb.s
    v.reduce_sum(cols[:, 0:1], x[:], AX.X)
    v.drain()
    mean_tok = kb.VE.inc(v.tensor_scalar(cols[:, 1:2], cols[:, 0:1], 1.0 / D, None, ALU.mult))
    s.wait_ge(kb.VE.h, mean_tok)
    sqd = kb.SE.inc(s.activation(sq[:], x[:], AF.Square))
    v.wait_ge(kb.SE.h, sqd)
    v.reduce_sum(cols[:, 2:3], sq[:], AX.X)
    v.drain()
    v.tensor_scalar(cols[:, 3:4], cols[:, 2:3], 1.0 / D, None, ALU.mult)
    v.tensor_mul(cols[:, 4:5], cols[:, 1:2], cols[:, 1:2])
    v.drain()
    var_tok = kb.VE.inc(v.tensor_sub(cols[:, 5:6], cols[:, 3:4], cols[:, 4:5]))
    s.wait_ge(kb.VE.h, var_tok)
    s.activation(cols[:, 6:7], cols[:, 5:6], AF.Ln, bias=kb.st["epscol"][:, 0:1])
    s.drain()
    sqr = kb.SE.inc(s.activation(cols[:, 7:8], cols[:, 6:7], AF.Exp, scale=-0.5))
    v.wait_ge(kb.SE.h, sqr)
    v.scalar_tensor_tensor(cols[:, 8:9], cols[:, 1:2], -1.0,
                           cols[:, 7:8], ALU.mult, ALU.mult)
    v.drain()
    return kb.VE.inc(v.tensor_scalar(z[:], x[:], cols[:, 7:8], cols[:, 8:9],
                                     ALU.mult, ALU.add))



def emit_quant(kb, src, tmp_f32, tmp_fp8, out, qcols, blocks, width=128):
    v = kb.v
    n = blocks * width
    am, s2, r2 = qcols[:, 0:blocks], qcols[:, blocks:2 * blocks], qcols[:, 2 * blocks:3 * blocks]
    r3 = lambda ap: ap[:, 0:n].rearrange("p (b i) -> p b i", i=width)
    bc = lambda ap: ap.unsqueeze(2).broadcast_to([128, blocks, width])
    v.tensor_reduce(am, r3(src), AX.X, ALU.max, apply_absolute_value=True)
    v.drain()
    v.tensor_scalar(s2, am, INV224, 2e-12, ALU.mult, ALU.max)
    v.drain()
    v.reciprocal(r2, s2)
    v.drain()
    v.tensor_tensor(r3(tmp_f32), r3(src), bc(r2), ALU.mult)
    v.drain()
    v.tensor_copy(tmp_fp8[:, 0:n], tmp_f32[:, 0:n])
    v.drain()
    return v.tensor_tensor(r3(out), r3(tmp_fp8), bc(s2), ALU.mult)



SERIAL_A0 = False


def emit_A0(kb, x_src, y_dst):
    nc, sync, gp, t, v, s = kb.nc, kb.sync, kb.gp, kb.t, kb.v, kb.s
    st = kb.st
    identf = st["identf"]
    with nc.sbuf_tensor("a0_x", [128, 2 * 1024], F32) as xt, \
         nc.sbuf_tensor("a0_z", [128, 1024], F32) as zt, \
         nc.sbuf_tensor("a0_sq", [128, 1024], F32) as sqt, \
         nc.sbuf_tensor("a0_cols", [128, 16], F32) as cols, \
         nc.sbuf_tensor("a0_yst", [128, 2 * 1024], F32) as yst, \
         nc.psum_tensor("a0_ps", [128, 1024], F32) as ps0, \
         nc.psum_tensor("a0_ps2", [128, 1024], F32) as ps1:
        store_toks = [0, 0]
        prev_t = 0
        prev_z = 0
        for tt in range(T // 128):
            par = tt % 2
            x = xt[:, par * 1024:(par + 1) * 1024]
            ys = yst[:, par * 1024:(par + 1) * 1024]
            ps = (ps0, ps1)[par]
            sync.wait_ge(kb.SE.h, store_toks[par])
            ldn = kb.LD.inc16(sync.dma_start(x, x_src.ap()[tt * 128:(tt + 1) * 128, :]))
            v.wait_ge(kb.LD.h, ldn)
            v.wait_ge(kb.TE.h, prev_t)
            s.wait_ge(kb.LD.h, ldn)
            zdone = emit_ln_natural(kb, x, zt, cols, sqt)
            t.wait_ge(kb.VE.h, zdone)
            t.wait_ge(kb.SE.h, store_toks[par])
            for c in range(8):
                ins = t.transpose(ps[:, c * 128:(c + 1) * 128],
                                  zt[:, c * 128:(c + 1) * 128], identf[:])
            prev_t = tdone = kb.TE.inc(ins)
            s.wait_ge(kb.TE.h, tdone)
            store_toks[par] = kb.SE.inc(s.copy(ys, ps[:]))
            gp.wait_ge(kb.SE.h, store_toks[par])
            kb.GD.inc16(gp.dma_start(
                y_dst.ap()[:, :, 1 + tt * 128:1 + (tt + 1) * 128].transpose([1, 0, 2]),
                ys.rearrange("p (c i) -> p c i", i=128)))
            if SERIAL_A0:
                for e in (kb.sync, kb.t, kb.v, kb.s, kb.gp):
                    e.wait_ge(kb.GD.h, kb.GD.n)
                    e.wait_ge(kb.VE.h, kb.VE.n)
                    e.wait_ge(kb.SE.h, kb.SE.n)
                    e.wait_ge(kb.TE.h, kb.TE.n)
                    e.wait_ge(kb.LD.h, kb.LD.n)
        gp.wait_ge(kb.GD.h, kb.GD.n)
        done = kb.GD.n
    for e in (kb.sync, kb.t, kb.v, kb.s):
        e.wait_ge(kb.GD.h, done)



def load_ffn_attn_weights(kb, l, ins, wts, pn):
    nc, sync, gp, t, v, s = kb.nc, kb.sync, kb.gp, kb.t, kb.v, kb.s
    for kc in range(8):
        kb.GD.inc16(gp.dma_start(
            wts["wqkv"][:, kc * 1536:(kc + 1) * 1536],
            ins["L0_ffn_wqkv"].ap()[l, kc * 128:(kc + 1) * 128, :]))
    for ic in range(4):
        kb.GD.inc16(gp.dma_start(
            wts["wo"][:, ic * 1024:(ic + 1) * 1024],
            ins["L0_ffn_wo"].ap()[l, ic * 128:(ic + 1) * 128, :]))
    qs, ks = ins["L0_ffn_q_scale"], ins["L0_ffn_k_scale"]
    for half in range(2):
        b = 64 * half
        kb.LD.inc16(sync.dma_start(wts["qscol"][b:b + 64, 0:1], qs.ap()[l, :].unsqueeze(1)))
        kb.LD.inc16(sync.dma_start(wts["kscol"][b:b + 64, 0:1], ks.ap()[l, :].unsqueeze(1)))
        kb.LD.inc16(sync.dma_start(wts["qssh"][b:b + 32, 0:1], qs.ap()[l, 32:64].unsqueeze(1)))
        kb.LD.inc16(sync.dma_start(wts["qssh"][b + 32:b + 64, 0:1], qs.ap()[l, 0:32].unsqueeze(1)))
        kb.LD.inc16(sync.dma_start(wts["kssh"][b:b + 32, 0:1], ks.ap()[l, 32:64].unsqueeze(1)))
        kb.LD.inc16(sync.dma_start(wts["kssh"][b + 32:b + 64, 0:1], ks.ap()[l, 0:32].unsqueeze(1)))
    ldr1 = kb.LD.inc16(sync.dma_start(wts["qrow"][0:1, 0:64], qs.ap()[l, :].unsqueeze(0)))
    ldr = kb.LD.inc16(sync.dma_start(wts["krow"][0:1, 0:64], ks.ap()[l, :].unsqueeze(0)))
    v.wait_ge(kb.LD.h, ldr)
    v.tensor_reduce(wts["qrow"][0:1, 64:65], wts["qrow"][0:1, 0:64], AX.X, ALU.max,
                    apply_absolute_value=True)
    v.tensor_reduce(wts["krow"][0:1, 64:65], wts["krow"][0:1, 0:64], AX.X, ALU.max,
                    apply_absolute_value=True)
    v.drain()
    v.tensor_mul(wts["qrow"][0:1, 65:66], wts["qrow"][0:1, 64:65], wts["krow"][0:1, 64:65])
    v.drain()
    cn = kb.VE.inc(v.tensor_scalar(wts["qrow"][0:1, 66:67], wts["qrow"][0:1, 65:66],
                                   -16.0, None, ALU.mult))
    t.wait_ge(kb.VE.h, cn)
    mmn = kb.TE.inc(t.matmul(pn.ps[:, 0:1], kb.st["onesall"][0:1, :],
                             wts["qrow"][0:1, 66:67], start=True, stop=True))
    s.wait_ge(kb.TE.h, mmn)
    tok = kb.SE.inc(s.copy(wts["negC"][:], pn.ps[:, 0:1]))
    pn.read_by(kb.SE.h, tok)
    return tok


def alloc_ffn_attn(kb, ns):
    import contextlib
    nc = kb.nc
    es = contextlib.ExitStack()
    a = lambda n, sh, dt: es.enter_context(nc.sbuf_tensor(ns + n, sh, dt))
    wts = {
        "wqkv": a("wqkv", [128, 8 * 1536], BF16),
        "wo": a("wo", [128, 4 * 1024], BF16),
        "qscol": a("qscol", [128, 1], F32), "qssh": a("qssh", [128, 1], F32),
        "kscol": a("kscol", [128, 1], F32), "kssh": a("kssh", [128, 1], F32),
        "qrow": a("qrow", [1, 80], F32), "krow": a("krow", [1, 80], F32),
        "negC": a("negC", [128, 1], F32),
    }
    bufs = {
        "yw": a("yw", [128, 2 * 8 * 512], F32),
        "ywb": a("ywb", [128, 2 * 8 * 513], BF16),
        "guard": a("guard", [128, 8], F32),
        "tabs": a("tabs", [128, 2 * 4 * 512], F32),
        "tabsrc": a("tabsrc", [128, 2 * 4 * 512], F32),
        "qh": a("qh", [128, 4 * 512], BF16),
        "kh0": a("kh0", [128, 4 * 512], BF16), "kh1": a("kh1", [128, 4 * 512], BF16),
        "vv0": a("vv0", [128, 4 * 520], BF16), "vv1": a("vv1", [128, 4 * 520], BF16),
        "qn0": a("qn0", [128, 3 * 512], F32),
        "sqf": a("sqf", [128, 3 * 512], F32),
        "rowa": a("rowa", [2, 512], F32), "rowb": a("rowb", [2, 512], F32),
        "tt1": a("tt1", [128, 512], F32), "tt2": a("tt2", [128, 512], F32),
        "h1": a("h1", [128, 512], F32), "h2": a("h2", [128, 512], F32),
        "expb": a("expb", [128, 2 * 512], BF16),
        "rowr": a("rowr", [128, 512], F32),
        "bcf": a("bcf", [128, 512], F32),
        "oT": a("oT", [128, 4 * 512], BF16),
    }
    return es, wts, bufs


VALID_SPAN = {kc: (max(0, (kc - 4)) * 128, min(512, kc * 128 + 128)) for kc in range(8)}


class Bank:
    def __init__(self, ps):
        self.ps = ps
        self.rd = None

    def guard(self, eng):
        if self.rd is None:
            return
        rds = self.rd if isinstance(self.rd, list) else [self.rd]
        for semh, val in rds:
            eng.wait_ge(semh, val)

    def read_by(self, semh, val):
        self.rd = (semh, val)

    def read_by2(self, rd1, rd2):
        self.rd = [rd1, rd2]


def emit_ffn_attn(kb, l, y_sc, ins, A_dram):
    nc, sync, gp, t, v, s = kb.nc, kb.sync, kb.gp, kb.t, kb.v, kb.s
    st = kb.st
    u = kb.uid()
    es, wts, bu = alloc_ffn_attn(kb, f"fa{u}_")
    with es, \
         nc.psum_tensor(f"fa{u}_g0", [128, 512], F32) as pg0, \
         nc.psum_tensor(f"fa{u}_g1", [128, 512], F32) as pg1, \
         nc.psum_tensor(f"fa{u}_g2", [128, 512], F32) as pg2, \
         nc.psum_tensor(f"fa{u}_g3", [128, 512], F32) as pg3, \
         nc.psum_tensor(f"fa{u}_g4", [128, 512], F32) as pg4, \
         nc.psum_tensor(f"fa{u}_g5", [128, 512], F32) as pg5, \
         nc.psum_tensor(f"fa{u}_row", [2, 512], F32) as prow_ps, \
         nc.psum_tensor(f"fa{u}_srow", [1, 512], F32) as srow_ps:
        g = [Bank(p) for p in (pg0, pg1, pg2, pg3, pg4, pg5)]
        growA, growB = Bank(prow_ps), Bank(srow_ps)
        negc_tok = load_ffn_attn_weights(kb, l, ins, wts, g[0])
        v.memset(bu["guard"][:], 0.0)
        for vb in (bu["vv0"], bu["vv1"]):
            v.memset(vb[:].rearrange("p (a h i) -> p a h i", h=8, i=65)[:, :, :, 64:65], 1.0)
        init_tok = kb.VE.inc(v.engine_nop())
        t.wait_ge(kb.SE.h, negc_tok)
        gps = kb.gp

        wstate = {}

        def emit_wload(w):
            par = w % 2
            sync.wait_ge(kb.GD.h, wstate.get(("store", w - 2), 0))
            kb.LD.inc16(sync.dma_start(
                bu["yw"][:, par * 4096:(par + 1) * 4096].rearrange("p (c i) -> p c i", i=512),
                y_sc.ap()[:, :, 1 + w * 512:1 + (w + 1) * 512].transpose([1, 0, 2])))
            ldt = 0
            for j, A in enumerate(A_dram):
                ldt = kb.LD.inc16(sync.dma_start(
                    bu["tabsrc"][:, par * 2048 + j * 512:par * 2048 + (j + 1) * 512],
                    A.ap()[:, w * 512:(w + 1) * 512]))
            wstate[("ld", w)] = ldt

        def emit_wprep(w):
            par = w % 2
            tb = par * 2048
            yb = par * 4096
            wb = par * 4104
            v.wait_ge(kb.LD.h, wstate[("ld", w)])
            v.wait_ge(kb.TE.h, wstate.get(("vmm", w - 2), 0))
            v.tensor_scalar_mul(bu["tabs"][:, tb:tb + 512], bu["tabsrc"][:, tb:tb + 512], wts["qscol"][:])
            v.tensor_scalar_mul(bu["tabs"][:, tb + 512:tb + 1024], bu["tabsrc"][:, tb + 512:tb + 1024], wts["qssh"][:])
            v.tensor_scalar_mul(bu["tabs"][:, tb + 1024:tb + 1536], bu["tabsrc"][:, tb + 1024:tb + 1536], wts["kscol"][:])
            v.tensor_scalar_mul(bu["tabs"][:, tb + 1536:tb + 2048], bu["tabsrc"][:, tb + 1536:tb + 2048], wts["kssh"][:])
            for c in range(8):
                v.tensor_copy(bu["ywb"][:, wb + c * 513 + 1:wb + (c + 1) * 513],
                              bu["yw"][:, yb + c * 512:yb + (c + 1) * 512])
                v.tensor_copy(bu["ywb"][:, wb + c * 513:wb + c * 513 + 1], bu["guard"][:, c:c + 1])
            for c in range(8):
                v.tensor_copy(bu["guard"][:, c:c + 1],
                              bu["yw"][:, yb + c * 512 + 511:yb + c * 512 + 512])
            wstate[("hT", w)] = kb.VE.inc(v.engine_nop())

        emit_wload(0)
        emit_wprep(0)
        for w in range(8):
            par_w = w % 2
            kv_cur = (bu["kh0"], bu["kh1"])[w % 2]
            kv_prv = (bu["kh1"], bu["kh0"])[w % 2]
            v_cur = (bu["vv0"], bu["vv1"])[w % 2]
            v_prv = (bu["vv1"], bu["vv0"])[w % 2]
            if w + 1 < 8:
                emit_wload(w + 1)

            def hsrc(kc, tokslice=None):
                base = par_w * 4104 + kc * 513 + (1 if kc < 4 else 0)
                if tokslice is None:
                    return bu["ywb"][:, base:base + 512]
                return bu["ywb"][:, base + tokslice[0]:base + tokslice[1]]

            t.wait_ge(kb.VE.h, wstate[("hT", w)])
            state = {}

            def emit_mmq(oc):
                par = oc % 3
                pqk = g[(0, 1, 4)[par]]
                pqk.guard(t)
                for kc in range(8):
                    mm = t.matmul(pqk.ps[:],
                                  wts["wqkv"][:, kc * 1536 + oc * 128:kc * 1536 + (oc + 1) * 128],
                                  hsrc(kc), start=(kc == 0), stop=(kc == 7))
                state[("mmq", oc)] = kb.TE.inc(mm)

            def emit_tailA(oc):
                par = oc % 3
                pqk = g[(0, 1, 4)[par]]
                qn0 = bu["qn0"][:, par * 512:(par + 1) * 512]
                sqf = bu["sqf"][:, par * 512:(par + 1) * 512]
                s.wait_ge(kb.TE.h, state[("mmq", oc)])
                s.wait_ge(kb.GP.h, state.get(("gfin", oc - 3), 0))
                s.wait_ge(kb.TE.h, state.get(("mrot", oc - 3), 0))
                s.wait_ge(kb.TE.h, state.get(("mms", oc - 3), 0))
                cpq = kb.SE.inc(s.copy(qn0, pqk.ps[:]))
                s.drain()
                sq = kb.SE.inc(s.activation(sqf, pqk.ps[:], AF.Square))
                pqk.read_by(kb.SE.h, sq)
                state[("sq", oc)] = sq
                state[("cpq", oc)] = cpq

            def emit_tailB(oc):
                par = oc % 3
                qn0 = bu["qn0"][:, par * 512:(par + 1) * 512]
                sqf = bu["sqf"][:, par * 512:(par + 1) * 512]
                is_q = oc < 4
                c = oc if is_q else oc - 4
                dst = bu["qh"] if is_q else kv_cur
                tb = par_w * 2048
                T1 = bu["tabs"][:, tb:tb + 512] if is_q else bu["tabs"][:, tb + 1024:tb + 1536]
                T2 = bu["tabs"][:, tb + 512:tb + 1024] if is_q else bu["tabs"][:, tb + 1536:tb + 2048]
                qn0_tok = state[("cpq", oc)]
                t.wait_ge(kb.SE.h, state[("sq", oc)])
                growA.guard(t)
                mms = kb.TE.inc(t.matmul(prow_ps[:], st["ones2f"][:], sqf,
                                         start=True, stop=True))
                state[("mms", oc)] = mms
                g[2].guard(t)
                t.wait_ge(kb.SE.h, qn0_tok)
                mrot = kb.TE.inc(t.matmul(g[2].ps[:], st["prot"][:], qn0,
                                          start=True, stop=True))
                state[("mrot", oc)] = mrot
                s.wait_ge(kb.TE.h, mms)
                s.activation(bu["rowa"][:], prow_ps[:], AF.Ln, bias=kb.st["eps24c"][0:2, 0:1])
                s.drain()
                rr = kb.SE.inc(s.activation(bu["rowb"][:], bu["rowa"][:], AF.Exp, scale=-0.5))
                growA.read_by(kb.SE.h, rr)
                t.wait_ge(kb.SE.h, rr)
                g[3].guard(t)
                mb = kb.TE.inc(t.matmul(g[3].ps[:], st["bc2f"][:], bu["rowb"][:],
                                        start=True, stop=True))
                v.wait_ge(kb.TE.h, max(mb, mrot))
                v.tensor_tensor(bu["tt1"][:], g[3].ps[:], T1, ALU.mult)
                g3r = kb.VE.inc(v.tensor_tensor(bu["tt2"][:], g[3].ps[:], T2, ALU.mult))
                g[3].read_by(kb.VE.h, g3r)
                gps.wait_ge(kb.VE.h, g3r)
                gps.wait_ge(kb.VE.h, state.get(("fin", oc - 1), 0))
                gf = kb.GP.inc(gps.tensor_tensor(bu["h1"][:], qn0, bu["tt1"][:], ALU.mult))
                state[("gfin", oc)] = gf
                v.wait_ge(kb.TE.h, mrot)
                h2t = kb.VE.inc(v.tensor_tensor(bu["h2"][:], g[2].ps[:], bu["tt2"][:], ALU.mult))
                g[2].read_by(kb.VE.h, h2t)
                v.wait_ge(kb.GP.h, gf)
                v.drain()
                fin = kb.VE.inc(v.tensor_tensor(dst[:, c * 512:(c + 1) * 512],
                                                bu["h1"][:], bu["h2"][:], ALU.add))
                state[("fin", oc)] = fin

            for oc in range(11):
                if oc < 8:
                    emit_mmq(oc)
                if 0 <= oc - 2 < 8:
                    emit_tailA(oc - 2)
                if 0 <= oc - 3 < 8:
                    emit_tailB(oc - 3)
            for tt in range(4):
                par = tt % 2
                pvm = g[par]
                pvm.guard(t)
                for kc in range(8):
                    mm = t.matmul(pvm.ps[:], hsrc(kc, (tt * 128, (tt + 1) * 128)),
                                  wts["wqkv"][:, kc * 1536 + 1024:kc * 1536 + 1536],
                                  start=(kc == 0), stop=(kc == 7))
                mmv = kb.TE.inc(mm)
                s.wait_ge(kb.TE.h, mmv)
                vz = kb.SE.inc(s.copy(
                    v_cur[:, tt * 520:(tt + 1) * 520].rearrange("p (h i) -> p h i", i=65)[:, :, 0:64],
                    pvm.ps[:].rearrange("p (h i) -> p h i", i=64)))
                pvm.read_by(kb.SE.h, vz)
            wstate[("vmm", w)] = kb.TE.n
            if w + 1 < 8:
                emit_wprep(w + 1)
            vready = kb.SE.n
            t.wait_ge(kb.SE.h, vready)
            t.wait_ge(kb.GP.h, kb.GP.n)
            kcs = [kc for kc in range(8) if (w > 0 or kc >= 4)]
            hstate = {}

            def head_kc_loop(h):
                c, sub = h // 2, h % 2
                P0, P1 = sub * 64, sub * 64 + 64
                pvb = g[(4, 5, 3)[h % 3]]
                srowb = (growB, growA)[(h // 2) % 2]
                pvb.guard(t)
                if sub == 1:
                    srowb.guard(t)
                toks = {}
                n = len(kcs)
                for it in range(n + 2):
                    if it < n:
                        kc = kcs[it]
                        lo, hi = VALID_SPAN[kc]
                        par = it % 2
                        psc = g[par]
                        kbuf = kv_prv if kc < 4 else kv_cur
                        kcol = (kc % 4) * 128
                        psc.guard(t)
                        mm = t.matmul(psc.ps[:, 0:hi - lo],
                                      kbuf[P0:P1, c * 512 + kcol:c * 512 + kcol + 128],
                                      bu["qh"][P0:P1, c * 512 + lo:c * 512 + hi],
                                      start=True, stop=True, tile_position=(P0, 0))
                        toks[("sc", it)] = kb.TE.inc(mm)
                    if 1 <= it <= n:
                        i = it - 1
                        kc = kcs[i]
                        lo, hi = VALID_SPAN[kc]
                        par = i % 2
                        psc = g[par]
                        s.wait_ge(kb.TE.h, toks[("sc", i)])
                        s.wait_ge(kb.TE.h, toks.get(("pv", i - 2), 0))
                        ex = kb.SE.inc(s.activation(bu["expb"][:, par * 512 + lo:par * 512 + hi],
                                                    psc.ps[:, 0:hi - lo], AF.Exp,
                                                    bias=wts["negC"][:]))
                        psc.read_by(kb.SE.h, ex)
                        dep = (kb.SE.h, ex)
                        if kc < 4:
                            off = par * 512 + kc * 128
                            v.wait_ge(kb.SE.h, ex)
                            mt = kb.VE.inc(v.tensor_mul(bu["expb"][:, off:off + 128],
                                                        bu["expb"][:, off:off + 128], st["maskL"][:]))
                            dep = (kb.VE.h, mt)
                        else:
                            off = par * 512 + (kc - 4) * 128
                            v.wait_ge(kb.SE.h, ex)
                            mt = kb.VE.inc(v.tensor_mul(bu["expb"][:, off:off + 128],
                                                        bu["expb"][:, off:off + 128], st["maskU"][:]))
                            dep = (kb.VE.h, mt)
                        if lo > 0:
                            mz = kb.VE.inc(v.memset(bu["expb"][:, par * 512:par * 512 + lo], 0.0))
                            dep = (kb.VE.h, mz)
                        if hi < 512:
                            mz = kb.VE.inc(v.memset(bu["expb"][:, par * 512 + hi:par * 512 + 512], 0.0))
                            dep = (kb.VE.h, mz)
                        toks[("dep", i)] = dep
                    if it >= 2:
                        i = it - 2
                        kc = kcs[i]
                        par = i % 2
                        dsem, dval = toks[("dep", i)]
                        t.wait_ge(dsem, dval)
                        vbuf = v_prv if kc < 4 else v_cur
                        vbase = (kc % 4) * 520 + h * 65
                        eall = bu["expb"][:, par * 512:par * 512 + 512]
                        if sub == 0:
                            mm2 = t.matmul(pvb.ps[0:65, 0:512], vbuf[:, vbase:vbase + 65],
                                           eall, start=(i == 0), stop=(i == n - 1),
                                           skip_group_check=True)
                            toks[("pv", i)] = kb.TE.inc(mm2)
                        else:
                            t.matmul(pvb.ps[64:128, 0:512], vbuf[:, vbase:vbase + 64],
                                     eall, start=(i == 0), stop=(i == n - 1),
                                     skip_group_check=True, tile_position=(0, 64))
                            mm3 = t.matmul(srowb.ps[0:1, 0:512], st["onesallb"][:, 0:1],
                                           eall, start=(i == 0), stop=(i == n - 1),
                                           skip_group_check=True)
                            toks[("pv", i)] = kb.TE.inc(mm3)
                hstate[("pvdone", h)] = toks[("pv", len(kcs) - 1)]

            def head_norm(h):
                c, sub = h // 2, h % 2
                pvb = g[(4, 5, 3)[h % 3]]
                srowb = (growB, growA)[(h // 2) % 2]
                if sub == 0:
                    s.wait_ge(kb.TE.h, hstate[("pvdone", h)])
                    s.wait_ge(kb.TE.h, hstate.get(("mb2", h - 2), 0))
                    s.activation(bu["rowr"][64:65, :], pvb.ps[64:65, :], AF.Ln)
                    s.drain()
                    rc = kb.SE.inc(s.activation(bu["rowr"][64:65, :], bu["rowr"][64:65, :],
                                                AF.Exp, scale=-1.0))
                    t.wait_ge(kb.SE.h, rc)
                    g[2].guard(t)
                    mb2 = kb.TE.inc(t.matmul(g[2].ps[0:64, :], st["onesall"][64:65, 0:64],
                                             bu["rowr"][64:65, :], start=True, stop=True))
                    hstate[("mb2", h)] = mb2
                    s.wait_ge(kb.TE.h, mb2)
                    s.wait_ge(kb.VE.h, hstate.get(("ord", h - 2), 0))
                    sc2 = kb.SE.inc(s.copy(bu["bcf"][0:64, :], g[2].ps[0:64, :]))
                    g[2].read_by(kb.SE.h, sc2)
                    v.wait_ge(kb.SE.h, sc2)
                    ord_ = kb.VE.inc(v.tensor_tensor(
                        bu["oT"][0:64, c * 512:(c + 1) * 512],
                        pvb.ps[0:64, :], bu["bcf"][0:64, :], ALU.mult))
                    hstate[("ord", h)] = ord_
                else:
                    s.wait_ge(kb.TE.h, hstate[("pvdone", h)])
                    s.wait_ge(kb.TE.h, hstate.get(("mb2", h - 2), 0))
                    s.activation(bu["rowr"][0:1, :], srowb.ps[0:1, 0:512], AF.Ln)
                    s.drain()
                    rc = kb.SE.inc(s.activation(bu["rowr"][0:1, :], bu["rowr"][0:1, :],
                                                AF.Exp, scale=-1.0))
                    srowb.read_by(kb.SE.h, rc)
                    t.wait_ge(kb.SE.h, rc)
                    g[2].guard(t)
                    mb2 = kb.TE.inc(t.matmul(g[2].ps[64:128, :], st["onesall"][0:1, 0:64],
                                             bu["rowr"][0:1, :], start=True, stop=True,
                                             tile_position=(0, 64)))
                    hstate[("mb2", h)] = mb2
                    s.wait_ge(kb.TE.h, mb2)
                    s.wait_ge(kb.VE.h, hstate.get(("ord", h - 2), 0))
                    sc2 = kb.SE.inc(s.copy(bu["bcf"][64:128, :], g[2].ps[64:128, :]))
                    g[2].read_by(kb.SE.h, sc2)
                    v.wait_ge(kb.SE.h, sc2)
                    ord_ = kb.VE.inc(v.tensor_tensor(
                        bu["oT"][64:128, c * 512:(c + 1) * 512],
                        pvb.ps[64:128, :], bu["bcf"][64:128, :], ALU.mult))
                    hstate[("ord", h)] = ord_
                pvb.read_by(kb.VE.h, ord_)

            for h in range(10):
                if h >= 2:
                    head_norm(h - 2)
                if h < 8:
                    head_kc_loop(h)
            oT_ready = kb.VE.n
            t.wait_ge(kb.VE.h, oT_ready)
            for oc in range(8):
                par = oc % 2
                pw = g[par]
                pw.guard(t)
                for ic in range(4):
                    mm = t.matmul(pw.ps[:],
                                  wts["wo"][:, ic * 1024 + oc * 128:ic * 1024 + (oc + 1) * 128],
                                  bu["oT"][:, ic * 512:(ic + 1) * 512],
                                  start=(ic == 0), stop=(ic == 3))
                mmw = kb.TE.inc(mm)
                v.wait_ge(kb.TE.h, mmw)
                yu = kb.VE.inc(v.tensor_add(
                    bu["yw"][:, par_w * 4096 + oc * 512:par_w * 4096 + (oc + 1) * 512],
                    bu["yw"][:, par_w * 4096 + oc * 512:par_w * 4096 + (oc + 1) * 512], pw.ps[:]))
                pw.read_by(kb.VE.h, yu)
            gp.wait_ge(kb.VE.h, kb.VE.n)
            wstate[("store", w)] = kb.GD.inc16(gp.dma_start(
                y_sc.ap()[:, :, 1 + w * 512:1 + (w + 1) * 512].transpose([1, 0, 2]),
                bu["yw"][:, par_w * 4096:(par_w + 1) * 4096].rearrange("p (c i) -> p c i", i=512)))
        gp.wait_ge(kb.GD.h, kb.GD.n)
        done = kb.GD.n
    for e in (kb.sync, kb.t, kb.v, kb.s):
        e.wait_ge(kb.GD.h, done)



W1CH = [(wc * 512, min(2730, wc * 512 + 512)) for wc in range(6)]


def emit_ffn_mlp(kb, l, y_sc, ins):
    import contextlib
    nc, sync, gp, t, v, s = kb.nc, kb.sync, kb.gp, kb.t, kb.v, kb.s
    st = kb.st
    es = contextlib.ExitStack()
    u = kb.uid()
    a = lambda n, sh, dt: es.enter_context(nc.sbuf_tensor(f"mlp{u}_" + n, sh, dt))
    w1 = a("w1", [128, 8 * 5460], BF16)
    w2 = a("w2", [128, 22 * 1024], BF16)
    b1a = a("b1a", [128, 22], F32)
    b1g = a("b1g", [128, 22], F32)
    lg = a("lg", [128, 8], F32)
    lb = a("lb", [128, 8], F32)
    b2c = a("b2c", [128, 8], F32)
    ytg = a("ytg", [128, 3 * 8 * 257], F32)
    hsq = a("hsq", [128, 8 * 257], F32)
    hT = a("hT", [128, 2 * 8 * 256], BF16)
    guard = a("guard", [128, 8], F32)
    rows = a("rows", [1, 1024], F32)
    gel = a("gel", [128, 2 * 256], F32)
    abuf = a("abuf", [128, 2 * 256], F32)
    mT = a("mT", [128, 2 * 256], BF16)
    SUBS = []
    for wc, (c0, c1) in enumerate(W1CH):
        for sub in range((c1 - c0 + 127) // 128):
            s0 = c0 + sub * 128
            SUBS.append((s0, min(c1, s0 + 128)))
    with es, \
         nc.psum_tensor(f"mlpp{u}_y0", [128, 512], F32) as py0, \
         nc.psum_tensor(f"mlpp{u}_y1", [128, 512], F32) as py1, \
         nc.psum_tensor(f"mlpp{u}_y2", [128, 512], F32) as py2, \
         nc.psum_tensor(f"mlpp{u}_y3", [128, 512], F32) as py3, \
         nc.psum_tensor(f"mlpp{u}_ag0", [128, 512], F32) as pag0, \
         nc.psum_tensor(f"mlpp{u}_ag1", [128, 512], F32) as pag1, \
         nc.psum_tensor(f"mlpp{u}_r", [1, 512], F32) as pr, \
         nc.psum_tensor(f"mlpp{u}_bc", [128, 512], F32) as pbc:
        py = [Bank(p) for p in (py0, py1, py2, py3)]
        pag = [Bank(pag0), Bank(pag1)]
        brow, bbc = Bank(pr), Bank(pbc)
        for kc in range(8):
            kb.GD.inc16(gp.dma_start(w1[:, kc * 5460:(kc + 1) * 5460],
                                     ins["L0_ffn_w1"].ap()[l, kc * 128:(kc + 1) * 128, :]))
        for rc in range(21):
            kb.GD.inc16(gp.dma_start(w2[:, rc * 1024:(rc + 1) * 1024],
                                     ins["L0_ffn_w2"].ap()[l, rc * 128:(rc + 1) * 128, :]))
        kb.GD.inc16(gp.dma_start(w2[0:42, 21 * 1024:22 * 1024],
                                 ins["L0_ffn_w2"].ap()[l, 2688:2730, :]))
        b1 = ins["L0_ffn_b1"]
        ncd = nc.allow_non_contiguous_dma(reason="small bias-column loads")
        ncd.__enter__()
        kb.LD.inc16(sync.dma_start(b1a[:, 0:21],
                                   b1.ap()[l, 0:2688].rearrange("(c p) -> p c", p=128)))
        kb.LD.inc16(sync.dma_start(b1a[0:42, 21:22], b1.ap()[l, 2688:2730].unsqueeze(1)))
        kb.LD.inc16(sync.dma_start(b1g[:, 0:21],
                                   b1.ap()[l, 2730:5418].rearrange("(c p) -> p c", p=128)))
        kb.LD.inc16(sync.dma_start(b1g[0:42, 21:22], b1.ap()[l, 5418:5460].unsqueeze(1)))
        kb.LD.inc16(sync.dma_start(lg[:], ins["L0_ffn_ln_g"].ap()[l, :].rearrange("(c p) -> p c", p=128)))
        kb.LD.inc16(sync.dma_start(lb[:], ins["L0_ffn_ln_b"].ap()[l, :].rearrange("(c p) -> p c", p=128)))
        wtok = kb.LD.inc16(sync.dma_start(b2c[:], ins["L0_ffn_b2"].ap()[l, :].rearrange("(c p) -> p c", p=128)))
        ncd.__exit__(None, None, None)
        gtok = kb.GD.n
        v.memset(guard[:], 0.0)
        kb.VE.inc(v.engine_nop())
        t.wait_ge(kb.LD.h, wtok)
        t.wait_ge(kb.GD.h, gtok)
        s.wait_ge(kb.LD.h, wtok)
        v.wait_ge(kb.LD.h, wtok)

        stt_ = {}
        def ysl(tc, kc, shifted):
            base = (tc % 3) * 2056 + kc * 257 + (0 if shifted else 1)
            return ytg[:, base:base + 256]

        def emit_load(tc):
            sync.wait_ge(kb.GD.h, stt_.get(("store", tc - 3), 0))
            stt_[("ld", tc)] = kb.LD.inc16(sync.dma_start(
                ytg[:, (tc % 3) * 2056:(tc % 3) * 2056 + 2056]
                .rearrange("p (c i) -> p c i", i=257)[:, :, 1:257],
                y_sc.ap()[:, :, 1 + tc * 256:1 + (tc + 1) * 256].transpose([1, 0, 2])))

        def emit_prep(tc):
            par = tc % 3
            b0 = par * 2056
            v.wait_ge(kb.LD.h, stt_[("ld", tc)])
            for c in range(8):
                v.tensor_copy(ytg[:, b0 + c * 257:b0 + c * 257 + 1], guard[:, c:c + 1])
            for c in range(8):
                v.tensor_copy(guard[:, c:c + 1], ytg[:, b0 + c * 257 + 256:b0 + c * 257 + 257])
            v.drain()
            for c in range(8):
                sl = ytg[:, b0 + c * 257:b0 + (c + 1) * 257]
                v.tensor_mul(hsq[:, c * 257:(c + 1) * 257], sl, sl)
            hready = kb.VE.inc(v.engine_nop())
            t.wait_ge(kb.VE.h, hready)
            brow.guard(t)
            for kc in range(8):
                t.matmul(pr[0:1, 0:256], st["onesall"][:, 0:1], ysl(tc, kc, kc >= 4),
                         start=(kc == 0), stop=(kc == 7), skip_group_check=True)
            for kc in range(8):
                sl = hsq[:, kc * 257 + (0 if kc >= 4 else 1):kc * 257 + (0 if kc >= 4 else 1) + 256]
                mm = t.matmul(pr[0:1, 256:512], st["onesall"][:, 0:1], sl,
                              start=False, stop=(kc == 7), skip_group_check=True)
            mmr = kb.TE.inc(mm)
            v.wait_ge(kb.TE.h, mmr)
            v.tensor_scalar(rows[0:1, 0:256], pr[0:1, 0:256], 1.0 / D, None, ALU.mult)
            rd0 = kb.VE.inc(v.tensor_scalar(rows[0:1, 256:512], pr[0:1, 256:512],
                                            1.0 / D, None, ALU.mult))
            brow.read_by(kb.VE.h, rd0)
            v.drain()
            v.tensor_mul(rows[0:1, 512:768], rows[0:1, 0:256], rows[0:1, 0:256])
            v.drain()
            vv = kb.VE.inc(v.tensor_sub(rows[0:1, 256:512], rows[0:1, 256:512],
                                        rows[0:1, 512:768]))
            s.wait_ge(kb.VE.h, vv)
            s.activation(rows[0:1, 512:768], rows[0:1, 256:512], AF.Ln,
                         bias=kb.st["epscol"][0:1, 0:1])
            s.drain()
            sr = kb.SE.inc(s.activation(rows[0:1, 256:512], rows[0:1, 512:768],
                                        AF.Exp, scale=-0.5))
            v.wait_ge(kb.SE.h, sr)
            nm = kb.VE.inc(v.scalar_tensor_tensor(rows[0:1, 512:768], rows[0:1, 0:256], -1.0,
                                                  rows[0:1, 256:512], ALU.mult, ALU.mult))
            t.wait_ge(kb.VE.h, nm)
            bbc.guard(t)
            t.matmul(pbc[0:64, 0:256], st["onesall"][0:1, 0:64], rows[0:1, 256:512],
                     start=True, stop=False, skip_group_check=True)
            t.matmul(pbc[64:128, 0:256], st["onesall"][0:1, 0:64], rows[0:1, 256:512],
                     start=True, stop=False, skip_group_check=True, tile_position=(0, 64))
            t.matmul(pbc[0:64, 256:512], st["onesall"][0:1, 0:64], rows[0:1, 512:768],
                     start=False, stop=False, skip_group_check=True)
            mmb = kb.TE.inc(t.matmul(pbc[64:128, 256:512], st["onesall"][0:1, 0:64],
                                     rows[0:1, 512:768], start=False, stop=True,
                                     skip_group_check=True, tile_position=(0, 64)))
            v.wait_ge(kb.TE.h, mmb)
            last = 0
            hpar = tc % 2
            for kc in range(8):
                sh = kc >= 4
                sl = hsq[:, kc * 257:kc * 257 + 256]
                v.tensor_tensor(sl, ysl(tc, kc, sh), pbc[:, 0:256], ALU.mult)
                v.drain()
                v.tensor_add(sl, sl, pbc[:, 256:512])
                v.drain()
                last = kb.VE.inc(v.tensor_scalar(
                    hT[:, hpar * 2048 + kc * 256:hpar * 2048 + (kc + 1) * 256], sl,
                    lg[:, kc:kc + 1], lb[:, kc:kc + 1], ALU.mult, ALU.add))
            bbc.read_by(kb.VE.h, last)
            stt_[("hT", tc)] = last

        def emit_ag(tc, si):
            s0, s1 = SUBS[si]
            wdt = s1 - s0
            par = si % 2
            bank = pag[par]
            hTb = (tc % 2) * 2048
            bank.guard(t)
            if si == 0:
                t.wait_ge(kb.VE.h, stt_[("hT", tc)])
            for kc in range(8):
                t.matmul(bank.ps[0:wdt, 0:256],
                         w1[:, kc * 5460 + s0:kc * 5460 + s1],
                         hT[:, hTb + kc * 256:hTb + (kc + 1) * 256],
                         start=(kc == 0), stop=False, skip_group_check=True)
            for kc in range(8):
                mmg = t.matmul(bank.ps[0:wdt, 256:512],
                               w1[:, kc * 5460 + 2730 + s0:kc * 5460 + 2730 + s1],
                               hT[:, hTb + kc * 256:hTb + (kc + 1) * 256],
                               start=False, stop=(kc == 7), skip_group_check=True)
            stt_[("ag", si)] = kb.TE.inc(mmg)
            rc = s0 // 128
            s.wait_ge(kb.TE.h, stt_[("ag", si)])
            s.wait_ge(kb.VE.h, stt_.get(("mdone", si - 2), 0))
            sg = kb.SE.inc(s.activation(gel[0:wdt, par * 256:par * 256 + 256],
                                        bank.ps[0:wdt, 256:512], AF.Gelu,
                                        bias=b1g[0:wdt, rc:rc + 1]))
            v.wait_ge(kb.SE.h, sg)
            md = kb.VE.inc(v.scalar_tensor_tensor(
                mT[0:wdt, par * 256:par * 256 + 256], bank.ps[0:wdt, 0:256],
                b1a[0:wdt, rc:rc + 1], gel[0:wdt, par * 256:par * 256 + 256],
                ALU.add, ALU.mult))
            bank.read_by(kb.VE.h, md)
            stt_[("mdone", si)] = md

        def emit_w2(si, first_any, last_any):
            s0, s1 = SUBS[si]
            wdt = s1 - s0
            par = si % 2
            rc = s0 // 128
            t.wait_ge(kb.VE.h, stt_[("mdone", si)])
            for oc in range(8):
                bank, half = py[oc // 2], (oc % 2) * 256
                if first_any and oc % 2 == 0:
                    bank.guard(t)
                mm2 = t.matmul(bank.ps[:, half:half + 256],
                               w2[0:wdt, rc * 1024 + oc * 128:rc * 1024 + (oc + 1) * 128],
                               mT[0:wdt, par * 256:par * 256 + 256],
                               start=(first_any and oc % 2 == 0), stop=last_any,
                               skip_group_check=True)
            return kb.TE.inc(mm2)

        NS = len(SUBS)
        emit_load(0)
        emit_prep(0)
        for tc in range(16):
            if tc + 1 < 16:
                emit_load(tc + 1)
            for si in range(NS + 1):
                if si < NS:
                    emit_ag(tc, si)
                if si >= 1:
                    y2tok = emit_w2(si - 1, si - 1 == 0, si - 1 == NS - 1)
                if si == 0 and tc + 1 < 16:
                    emit_prep(tc + 1)
            par = tc % 3
            b0 = par * 2056
            v.wait_ge(kb.TE.h, y2tok)
            for oc in range(8):
                bank, half = py[oc // 2], (oc % 2) * 256
                yo = kb.VE.inc(v.scalar_tensor_tensor(
                    ytg[:, b0 + oc * 257 + 1:b0 + oc * 257 + 257], bank.ps[:, half:half + 256],
                    b2c[:, oc:oc + 1], ytg[:, b0 + oc * 257 + 1:b0 + oc * 257 + 257],
                    ALU.add, ALU.add))
                bank.read_by(kb.VE.h, yo)
            gp.wait_ge(kb.VE.h, kb.VE.n)
            stt_[("store", tc)] = kb.GD.inc16(gp.dma_start(
                y_sc.ap()[:, :, 1 + tc * 256:1 + (tc + 1) * 256].transpose([1, 0, 2]),
                ytg[:, b0:b0 + 2056].rearrange("p (c i) -> p c i", i=257)[:, :, 1:257]))
        gp.wait_ge(kb.GD.h, kb.GD.n)
        done = kb.GD.n
    for e in (kb.sync, kb.t, kb.v, kb.s):
        e.wait_ge(kb.GD.h, done)



def emit_weight_quant(kb, wdram, kcs, wq, stage, fp8st, qcols, blocks):
    sync, v = kb.sync, kb.v
    N = blocks * 128
    for kc in range(kcs):
        ldw = kb.LD.inc16(sync.dma_start(stage[:, 0:N], wdram.ap()[kc * 128:(kc + 1) * 128, :]))
        v.wait_ge(kb.LD.h, ldw)
        tok = kb.VE.inc(emit_quant(kb, stage[:, 0:N], stage[:, 0:N], fp8st,
                                   wq[:, kc * N:(kc + 1) * N], qcols, blocks))
        sync.wait_ge(kb.VE.h, tok)
    return kb.VE.n


def emit_dense_block(kb, x0_src, y_sc, x1_sc, x2_sc, qT_sc, ins):
    import contextlib
    nc, sync, gp, t, v, s = kb.nc, kb.sync, kb.gp, kb.t, kb.v, kb.s
    st = kb.st
    es = contextlib.ExitStack()
    a = lambda n, sh, dt: es.enter_context(nc.sbuf_tensor("dn_" + n, sh, dt))
    kT = a("kT", [128, 8 * 4096], BF16)
    vres = a("vres", [128, 32 * 1040], BF16)
    bcols = a("bcols", [128, 24], F32)
    obias = a("obias", [128, 8], F32)
    ncq = a("ncq", [2, 26], F32)
    negrow = a("negrow", [1, 16], F32)
    negc16 = a("negc16", [128, 16], F32)
    qcolsw = a("qcolsw", [128, 72], F32)
    with es:
        es1 = contextlib.ExitStack()
        b = lambda n, sh, dt: es1.enter_context(nc.sbuf_tensor("b1_" + n, sh, dt))
        wq = b("wq", [128, 8 * 3072], BF16)
        vbrow = b("vbrow", [1, 1024], BF16)
        fp8s = b("fp8s", [128, 1024], FP8)
        with nc.sbuf_tensor("b1_wst", [128, 3072], F32) as wst, \
             nc.sbuf_tensor("b1_f8w", [128, 3072], FP8) as f8w:
            emit_weight_quant(kb, ins["L0_attn_qkv_kernel"], 8, wq, wst,
                              f8w[:, 0:3072], qcolsw, 24)
        xt = b("xt", [128, 2 * 1024], F32)
        zt = b("zt", [128, 1024], F32)
        zl = b("zl", [128, 1024], F32)
        zdq = b("zdq", [128, 1024], BF16)
        zdqT = b("zdqT", [128, 1024], BF16)
        qst = b("qst", [128, 8 * 128], BF16)
        qsqb = b("qsqb", [128, 512], BF16)
        cols = b("cols", [128, 16], F32)
        with es1, \
             nc.psum_tensor("b1_p0", [128, 512], F32) as p0, \
             nc.psum_tensor("b1_p1", [128, 512], F32) as p1, \
             nc.psum_tensor("b1_p2", [128, 512], F32) as p2, \
             nc.psum_tensor("b1_p3", [128, 512], F32) as p3, \
             nc.psum_tensor("b1_pn", [128, 1024], F32) as pn, \
             nc.psum_tensor("b1_pb", [128, 1024], BF16) as pb, \
             nc.psum_tensor("b1_pr", [2, 512], F32) as pr:
            bk = [Bank(p) for p in (p0, p1, p2, p3)]
            bn, bb, br = Bank(pn), Bank(pb), Bank(pr)
            with nc.allow_non_contiguous_dma(reason="small bias cols"):
                kb.LD.inc16(sync.dma_start(
                    bcols[:], ins["L0_attn_qkv_bias"].ap().rearrange("(c p) -> p c", p=128)))
                btok = kb.LD.inc16(sync.dma_start(
                    obias[:], ins["L0_attn_out_bias"].ap().rearrange("(c p) -> p c", p=128)))
            kb.GD.inc16(gp.dma_start(vbrow[:], ins["L0_attn_qkv_bias"].ap()[2048:3072].unsqueeze(0)))
            v.memset(ncq[:], 0.0)
            vo = kb.VE.inc(v.memset(
                vres[:].rearrange("p (a h i) -> p a h i", h=16, i=65)[:, :, :, 64:65], 1.0))
            s.wait_ge(kb.LD.h, btok)
            t.wait_ge(kb.VE.h, vo)
            t.wait_ge(kb.GD.h, kb.GD.n)
            bst = {}

            def b1_load(tc):
                par = tc % 2
                sync.wait_ge(kb.VE.h, bst.get(("lnz", tc - 2), 0))
                sync.wait_ge(kb.GD.h, bst.get(("x1st", tc - 2), 0))
                ldx = kb.LD.inc16(sync.dma_start(xt[:, par * 1024:(par + 1) * 1024],
                                                 x0_src.ap()[tc * 128:(tc + 1) * 128, :]))
                sync.wait_ge(kb.TE.h, bst.get(("ty", tc - 1), 0))
                ldy = kb.LD.inc16(sync.dma_start(
                    zt[:].rearrange("p (c i) -> p c i", i=128),
                    y_sc.ap()[:, :, 1 + tc * 128:1 + (tc + 1) * 128].transpose([1, 0, 2])))
                bst[("ld", tc)] = ldy

            def b1_x1(tc):
                par = tc % 2
                x = xt[:, par * 1024:(par + 1) * 1024]
                t.wait_ge(kb.LD.h, bst[("ld", tc)])
                bn.guard(t)
                for c in range(8):
                    mm = t.transpose(pn[:, c * 128:(c + 1) * 128],
                                     zt[:, c * 128:(c + 1) * 128],
                                     st["identf"][:])
                ty = kb.TE.inc(mm)
                bst[("ty", tc)] = ty
                v.wait_ge(kb.TE.h, ty)
                x1done = kb.VE.inc(v.scalar_tensor_tensor(x, pn[:], 0.5, x,
                                                          ALU.mult, ALU.add))
                bn.read_by(kb.VE.h, x1done)
                bst[("x1done", tc)] = x1done
                gp.wait_ge(kb.VE.h, x1done)
                bst[("x1st", tc)] = kb.GD.inc16(gp.dma_start(
                    x1_sc.ap()[tc * 128:(tc + 1) * 128, :], x))

            def b1_lnq(tc):
                par = tc % 2
                x = xt[:, par * 1024:(par + 1) * 1024]
                s.wait_ge(kb.VE.h, bst[("x1done", tc)])
                bst[("lnz", tc)] = emit_ln_natural(kb, x, zl[:], cols, zl[:])
                v.wait_ge(kb.TE.h, bst.get(("tz", tc - 1), 0))
                qd = kb.VE.inc(emit_quant(kb, zl[:], zl[:], fp8s[:, 0:1024],
                                          zdq[:], qcolsw, 8))
                bst[("qd", tc)] = qd

            def b1_qkv(tc):
                par = tc % 2
                t.wait_ge(kb.VE.h, bst[("qd", tc)])
                bb.guard(t)
                for c in range(8):
                    mm = t.transpose(pb[:, c * 128:(c + 1) * 128],
                                     zdq[:, c * 128:(c + 1) * 128],
                                     st["identb"][:])
                tz = kb.TE.inc(mm)
                bst[("tz", tc)] = tz
                s.wait_ge(kb.TE.h, tz)
                s.wait_ge(kb.TE.h, bst.get(("lastmm", tc - 1), 0))
                ztok = kb.SE.inc(s.copy(zdqT[:], pb[:]))
                bb.read_by(kb.SE.h, ztok)
                t.wait_ge(kb.SE.h, ztok)
                zT = zdqT[:]
                for oc in range(16):
                    pq = bk[oc % 2]
                    pq.guard(t)
                    for c in range(8):
                        mm = t.matmul(pq.ps[:, 0:128],
                                      wq[:, c * 3072 + oc * 128:c * 3072 + (oc + 1) * 128],
                                      zT[:, c * 128:(c + 1) * 128],
                                      start=(c == 0), stop=(c == 7))
                    mq = kb.TE.inc(mm)
                    is_q = oc < 8
                    dst = (qst[:, oc * 128:(oc + 1) * 128] if is_q else
                           kT[:, (oc - 8) * 4096 + tc * 128:(oc - 8) * 4096 + (tc + 1) * 128])
                    v.wait_ge(kb.TE.h, mq)
                    if is_q:
                        v.wait_ge(kb.GD.h, bst.get(("qstst", tc - 1), 0))
                    if oc == 0:
                        v.wait_ge(kb.SE.h, bst.get(("nsq", tc - 1), 0))
                    ev = kb.VE.inc(v.tensor_scalar(dst, pq.ps[:, 0:128],
                                                   bcols[:, oc:oc + 1], None, ALU.add))
                    pq.read_by(kb.VE.h, ev)
                qkv_done = kb.VE.n
                gp.wait_ge(kb.VE.h, qkv_done)
                bst[("qstst", tc)] = kb.GD.inc16(gp.dma_start(
                    qT_sc.ap()[:, :, tc * 128:(tc + 1) * 128].transpose([1, 0, 2]),
                    qst[:].rearrange("p (c i) -> p c i", i=128)))
                s.wait_ge(kb.VE.h, qkv_done)
                for half in range(2):
                    for rnd in range(2):
                        s.wait_ge(kb.TE.h, bst.get(("nmm",), 0))
                        if half == 0:
                            srcq = qst[:, rnd * 512:(rnd + 1) * 512]
                            sqi = s.activation(qsqb[:].rearrange("p (c i) -> p c i", i=128),
                                               srcq.rearrange("p (c i) -> p c i", i=128),
                                               AF.Square)
                        else:
                            sqi = s.activation(
                                qsqb[:].rearrange("p (c i) -> p c i", i=128),
                                kT[:].rearrange("p (c i) -> p c i", i=4096)
                                [:, rnd * 4:(rnd + 1) * 4,
                                 tc * 128:(tc + 1) * 128], AF.Square)
                        nsq = kb.SE.inc(sqi)
                        bst[("nsq", tc)] = nsq
                        t.wait_ge(kb.SE.h, nsq)
                        br.guard(t)
                        for j in range(4):
                            mm = t.matmul(pr[:, j * 128:(j + 1) * 128],
                                          st["onesallb"][:, 2:4],
                                          qsqb[:, j * 128:(j + 1) * 128],
                                          start=True, stop=True, skip_group_check=True)
                        msr = kb.TE.inc(mm)
                        bst[("nmm",)] = msr
                        v.wait_ge(kb.TE.h, msr)
                        v.tensor_reduce(ncq[0:2, 20:24],
                                        pr[:].rearrange("p (a i) -> p a i", i=128),
                                        AX.X, ALU.max)
                        v.drain()
                        c0 = half * 8 + rnd * 4
                        mx = kb.VE.inc(v.tensor_max(ncq[0:2, c0:c0 + 4],
                                                    ncq[0:2, c0:c0 + 4],
                                                    ncq[0:2, 20:24]))
                        br.read_by(kb.VE.h, mx)
                        v.drain()
                for vc in range(2):
                    pv = bk[2 + vc]
                    pv.guard(t)
                    for c in range(8):
                        t.matmul(pv.ps[:], zT[:, c * 128:(c + 1) * 128],
                                 wq[:, c * 3072 + 2048 + vc * 512:c * 3072 + 2048 + (vc + 1) * 512],
                                 start=(c == 0), stop=False)
                    mmv = t.matmul(pv.ps[:], st["onesrowb"][:, :],
                                   vbrow[:, vc * 512:(vc + 1) * 512],
                                   start=False, stop=True)
                    mvt = kb.TE.inc(mmv)
                    bst[("lastmm", tc)] = mvt
                    s.wait_ge(kb.TE.h, mvt)
                    vz = kb.SE.inc(s.copy(
                        vres[:, tc * 1040 + vc * 520:tc * 1040 + (vc + 1) * 520]
                        .rearrange("p (h i) -> p h i", i=65)[:, :, 0:64],
                        pv.ps[:].rearrange("p (h i) -> p h i", i=64)))
                    pv.read_by(kb.SE.h, vz)

            b1_load(0)
            b1_x1(0)
            b1_lnq(0)
            for tc in range(32):
                if tc + 1 < 32:
                    b1_load(tc + 1)
                    b1_x1(tc + 1)
                b1_qkv(tc)
                if tc + 1 < 32:
                    b1_lnq(tc + 1)
            v.wait_ge(kb.SE.h, kb.SE.n)
            s.wait_ge(kb.VE.h, kb.VE.n)
            sqn = kb.SE.inc(s.activation(ncq[0:2, 16:24], ncq[0:2, 0:8], AF.Sqrt))
            s.drain()
            sqk = kb.SE.inc(s.activation(ncq[0:2, 0:8], ncq[0:2, 8:16], AF.Sqrt))
            v.wait_ge(kb.SE.h, sqk)
            v.tensor_mul(ncq[0:2, 8:16], ncq[0:2, 16:24], ncq[0:2, 0:8])
            v.drain()
            cn = kb.VE.inc(v.tensor_scalar(ncq[0:2, 16:24], ncq[0:2, 8:16],
                                           -0.1255, None, ALU.mult))
            gp.wait_ge(kb.VE.h, cn)
            with nc.allow_non_contiguous_dma(reason="tiny negC shuffle"):
                gp.dma_start(negrow[:].rearrange("a (c p) -> a c p", p=2)[:, :, 0:1],
                             ncq[0:1, 16:24]).then_inc(kb.GD.h, 16)
                kb.GD.n += 16
                ng = kb.GD.inc16(gp.dma_start(
                    negrow[:].rearrange("a (c p) -> a c p", p=2)[:, :, 1:2],
                    ncq[1:2, 16:24]))
            t.wait_ge(kb.GD.h, ng)
            bn.guard(t)
            mmn = kb.TE.inc(t.matmul(pn[:, 0:16], st["onesall"][0:1, :], negrow[:],
                                     start=True, stop=True))
            s.wait_ge(kb.TE.h, mmn)
            nct = kb.SE.inc(s.copy(negc16[:], pn[:, 0:16]))
            bn.read_by(kb.SE.h, nct)
            for e in (kb.sync, kb.t, kb.v):
                e.wait_ge(kb.SE.h, kb.SE.n)
            fin1 = kb.GD.n
            gp.wait_ge(kb.GD.h, fin1)
        es2 = contextlib.ExitStack()
        b2 = lambda n, sh, dt: es2.enter_context(nc.sbuf_tensor("b2_" + n, sh, dt))
        wout = b2("wout", [128, 8 * 1024], BF16)
        fp8s = b2("fp8s2", [128, 1024], FP8)
        with nc.sbuf_tensor("b2_wst2", [128, 1024], F32) as wst2:
            emit_weight_quant(kb, ins["L0_attn_out_kernel"], 8, wout, wst2,
                              fp8s[:, 0:1024], qcolsw, 8)
        qT = b2("qT", [128, 8 * 512], BF16)
        expb = b2("expb", [128, 4 * 512], BF16)
        oT = b2("oT", [128, 8 * 512], BF16)
        odqT = b2("odqT", [128, 8 * 512], BF16)
        oNat = b2("oNat", [128, 2 * 1024], BF16)
        odq = b2("odq", [128, 1024], BF16)
        x1t = b2("x1t", [128, 2 * 1024], F32)
        z3 = b2("z3", [128, 2 * 1024], F32)
        qtmp = b2("qtmp", [128, 1024], F32)
        rowr = b2("rowr", [128, 512], F32)
        bcf = b2("bcf", [128, 512], F32)
        cols2 = b2("cols2", [128, 16], F32)
        with es2, \
             nc.psum_tensor("b2_a0", [128, 512], F32) as pa0, \
             nc.psum_tensor("b2_a1", [128, 512], F32) as pa1, \
             nc.psum_tensor("b2_b0", [128, 512], F32) as pb0, \
             nc.psum_tensor("b2_b1", [128, 512], F32) as pb1, \
             nc.psum_tensor("b2_v0", [128, 512], F32) as pv0, \
             nc.psum_tensor("b2_v1", [128, 512], F32) as pv1, \
             nc.psum_tensor("b2_sr", [1, 512], F32) as psr, \
             nc.psum_tensor("b2_bT", [128, 1024], BF16) as pbT:
            bA = [Bank(pa0), Bank(pa1)]
            bB = [Bank(pb0), Bank(pb1)]
            bpA, bpB, bsr, bbt = Bank(pv0), Bank(pv1), Bank(psr), Bank(pbT)
            t.wait_ge(kb.VE.h, kb.VE.n)
            for qc in range(8):
                sync.wait_ge(kb.TE.h, kb.TE.n)
                ldq = kb.LD.inc16(sync.dma_start(
                    qT[:].rearrange("p (c i) -> p c i", i=512),
                    qT_sc.ap()[:, :, qc * 512:(qc + 1) * 512].transpose([1, 0, 2])))
                t.wait_ge(kb.LD.h, ldq)
                nstate = {}

                def b2_norm(hp):
                    h0, h1 = 2 * hp, 2 * hp + 1
                    pvt = nstate[("pvt", hp)]
                    s.wait_ge(kb.TE.h, pvt)
                    s.activation(rowr[64:65, :], pv0[64:65, :], AF.Ln)
                    s.drain()
                    rc0 = kb.SE.inc(s.activation(rowr[64:65, :], rowr[64:65, :],
                                                 AF.Exp, scale=-1.0))
                    t.wait_ge(kb.SE.h, rc0)
                    bA[0].guard(t)
                    mb0 = kb.TE.inc(t.matmul(pa0[0:64, :], st["onesall"][64:65, 0:64],
                                             rowr[64:65, :], start=True, stop=True))
                    s.wait_ge(kb.TE.h, mb0)
                    s.wait_ge(kb.VE.h, nstate.get(("o0", hp - 1), 0))
                    sc0 = kb.SE.inc(s.copy(bcf[0:64, :], pa0[0:64, :]))
                    bA[0].read_by(kb.SE.h, sc0)
                    v.wait_ge(kb.SE.h, sc0)
                    o0 = kb.VE.inc(v.tensor_tensor(oT[0:64, hp * 512:(hp + 1) * 512],
                                                   pv0[0:64, :], bcf[0:64, :], ALU.mult))
                    nstate[("o0", hp)] = o0
                    bpA.read_by(kb.VE.h, o0)
                    s.activation(rowr[0:1, :], psr[0:1, :], AF.Ln)
                    s.drain()
                    rc1 = kb.SE.inc(s.activation(rowr[0:1, :], rowr[0:1, :],
                                                 AF.Exp, scale=-1.0))
                    bsr.read_by(kb.SE.h, rc1)
                    t.wait_ge(kb.SE.h, rc1)
                    bA[1].guard(t)
                    mb1 = kb.TE.inc(t.matmul(pa1[64:128, :], st["onesall"][0:1, 0:64],
                                             rowr[0:1, :], start=True, stop=True,
                                             tile_position=(0, 64)))
                    s.wait_ge(kb.TE.h, mb1)
                    s.wait_ge(kb.VE.h, nstate.get(("o1", hp - 1), 0))
                    sc1 = kb.SE.inc(s.copy(bcf[64:128, :], pa1[64:128, :]))
                    bA[1].read_by(kb.SE.h, sc1)
                    v.wait_ge(kb.SE.h, sc1)
                    o1 = kb.VE.inc(v.tensor_tensor(oT[64:128, hp * 512:(hp + 1) * 512],
                                                   pv1[64:128, :], bcf[64:128, :], ALU.mult))
                    nstate[("o1", hp)] = o1
                    bpB.read_by(kb.VE.h, o1)

                for hp in range(8):
                  if True:
                    h0, h1 = 2 * hp, 2 * hp + 1
                    bpA.guard(t)
                    bpB.guard(t)
                    bsr.guard(t)
                    exp_tok = {}
                    for it in range(34):
                        if it < 32:
                            kc = it
                            par = kc % 2
                            pA, pB = bA[par], bB[par]
                            pA.guard(t)
                            t.matmul(pA.ps[:], kT[0:64, hp * 4096 + kc * 128:hp * 4096 + (kc + 1) * 128],
                                     qT[0:64, hp * 512:(hp + 1) * 512],
                                     start=True, stop=True)
                            pB.guard(t)
                            mmB = t.matmul(pB.ps[:], kT[64:128, hp * 4096 + kc * 128:hp * 4096 + (kc + 1) * 128],
                                           qT[64:128, hp * 512:(hp + 1) * 512],
                                           start=True, stop=True, tile_position=(64, 0))
                            kb.TE.inc(mmB)
                            sc_tok = kb.TE.n
                        if 1 <= it <= 32:
                            kc = it - 1
                            par = kc % 2
                            pA, pB = bA[par], bB[par]
                            s.wait_ge(kb.TE.h, exp_tok.get(("sc", kc), kb.TE.n))
                            s.wait_ge(kb.TE.h, exp_tok.get(("pv", kc - 2), 0))
                            exA = kb.SE.inc(s.activation(expb[:, (par * 2) * 512:(par * 2) * 512 + 512],
                                                         pA.ps[:], AF.Exp, bias=negc16[:, h0:h0 + 1],
                                                         scale=0.125))
                            pA.read_by(kb.SE.h, exA)
                            exB = kb.SE.inc(s.activation(expb[:, (par * 2 + 1) * 512:(par * 2 + 1) * 512 + 512],
                                                         pB.ps[:], AF.Exp, bias=negc16[:, h1:h1 + 1],
                                                         scale=0.125))
                            pB.read_by(kb.SE.h, exB)
                            exp_tok[("ex", kc)] = exB
                        if it >= 2:
                            kc = it - 2
                            par = kc % 2
                            t.wait_ge(kb.SE.h, exp_tok[("ex", kc)])
                            eA = expb[:, (par * 2) * 512:(par * 2) * 512 + 512]
                            eB = expb[:, (par * 2 + 1) * 512:(par * 2 + 1) * 512 + 512]
                            t.matmul(pv0[0:65, :], vres[:, kc * 1040 + h0 * 65:kc * 1040 + h0 * 65 + 65],
                                     eA, start=(kc == 0), stop=(kc == 31), skip_group_check=True)
                            t.matmul(pv1[64:128, :], vres[:, kc * 1040 + h1 * 65:kc * 1040 + h1 * 65 + 64],
                                     eB, start=(kc == 0), stop=(kc == 31), skip_group_check=True,
                                     tile_position=(0, 64))
                            pvt = kb.TE.inc(t.matmul(psr[0:1, :], st["onesallb"][:, 0:1], eB,
                                                     start=(kc == 0), stop=(kc == 31),
                                                     skip_group_check=True))
                            exp_tok[("pv", kc)] = pvt
                            nstate[("pvt", hp)] = pvt
                        if it < 32:
                            exp_tok[("sc", it)] = sc_tok
                    b2_norm(hp)
                tst = {}
                oT_ok = kb.VE.n
                t.wait_ge(kb.VE.h, oT_ok)

                def q_stageA(tt):
                    par = tt % 2
                    bbt.guard(t)
                    for c in range(8):
                        mm = t.transpose(pbT[:, c * 128:(c + 1) * 128],
                                         oT[:, c * 512 + tt * 128:c * 512 + (tt + 1) * 128],
                                         st["identb"][:])
                    to1 = kb.TE.inc(mm)
                    s.wait_ge(kb.TE.h, to1)
                    s.wait_ge(kb.VE.h, tst.get(("qv", tt - 2), 0))
                    so1 = kb.SE.inc(s.copy(oNat[:, par * 1024:(par + 1) * 1024], pbT[:]))
                    bbt.read_by(kb.SE.h, so1)
                    tst[("ev", tt)] = so1

                def q_stageB(tt):
                    par = tt % 2
                    v.wait_ge(kb.SE.h, tst[("ev", tt)])
                    v.wait_ge(kb.TE.h, tst.get(("tb", tt - 1), 0))
                    qo = kb.VE.inc(emit_quant(kb, oNat[:, par * 1024:(par + 1) * 1024],
                                              qtmp[:], fp8s[:, 0:1024], odq[:], qcolsw, 8))
                    tst[("qv", tt)] = qo
                    t.wait_ge(kb.VE.h, qo)
                    bbt.guard(t)
                    for c in range(8):
                        mm = t.transpose(pbT[:, c * 128:(c + 1) * 128],
                                         odq[:, c * 128:(c + 1) * 128], st["identb"][:])
                    to2 = kb.TE.inc(mm)
                    tst[("tb", tt)] = to2
                    s.wait_ge(kb.TE.h, to2)
                    so2 = kb.SE.inc(s.copy(
                        odqT[:].rearrange("p (c i) -> p c i", i=512)[:, :, tt * 128:(tt + 1) * 128],
                        pbT[:].rearrange("p (c i) -> p c i", i=128)))
                    bbt.read_by(kb.SE.h, so2)

                for it in range(5):
                    if it < 4:
                        q_stageA(it)
                    if it >= 1:
                        q_stageB(it - 1)
                odqready = kb.SE.n
                t.wait_ge(kb.SE.h, odqready)
                for oc in range(8):
                    par = oc % 2
                    pO = bA[par]
                    pO.guard(t)
                    for c in range(8):
                        mm = t.matmul(pO.ps[:], wout[:, c * 1024 + oc * 128:c * 1024 + (oc + 1) * 128],
                                      odqT[:, c * 512:(c + 1) * 512],
                                      start=(c == 0), stop=(c == 7))
                    mo = kb.TE.inc(mm)
                    v.wait_ge(kb.TE.h, mo)
                    ot = kb.VE.inc(v.tensor_scalar(oT[:, oc * 512:(oc + 1) * 512],
                                                   pO.ps[:], obias[:, oc:oc + 1], None, ALU.add))
                    pO.read_by(kb.VE.h, ot)
                outTready = kb.VE.n
                t.wait_ge(kb.VE.h, outTready)

                def x_stage1(tt):
                    par = tt % 2
                    tc0 = qc * 512 + tt * 128
                    x2v = x1t[:, par * 1024:(par + 1) * 1024]
                    sync.wait_ge(kb.VE.h, tst.get(("lnz", tt - 2), 0))
                    sync.wait_ge(kb.GD.h, tst.get(("x2st", tt - 2), 0))
                    ldx1 = kb.LD.inc16(sync.dma_start(x2v, x1_sc.ap()[tc0:tc0 + 128, :]))
                    bbt.guard(t)
                    for c in range(8):
                        mm = t.transpose(pbT[:, c * 128:(c + 1) * 128],
                                         oT[:, c * 512 + tt * 128:c * 512 + (tt + 1) * 128],
                                         st["identb"][:])
                    tx = kb.TE.inc(mm)
                    v.wait_ge(kb.TE.h, tx)
                    v.wait_ge(kb.LD.h, ldx1)
                    x2d = kb.VE.inc(v.tensor_add(x2v, x2v, pbT[:]))
                    bbt.read_by(kb.VE.h, x2d)
                    tst[("x2", tt)] = x2d
                    gp.wait_ge(kb.VE.h, x2d)
                    tst[("x2st", tt)] = kb.GD.inc16(gp.dma_start(
                        x2_sc.ap()[tc0:tc0 + 128, :], x2v))

                def x_stage2(tt):
                    par = tt % 2
                    tc0 = qc * 512 + tt * 128
                    x2v = x1t[:, par * 1024:(par + 1) * 1024]
                    z3v = z3[:, par * 1024:(par + 1) * 1024]
                    s.wait_ge(kb.VE.h, tst[("x2", tt)])
                    tst[("lnz", tt)] = emit_ln_natural(kb, x2v, z3v, cols2, z3v)
                    t.wait_ge(kb.VE.h, tst[("lnz", tt)])
                    s.wait_ge(kb.GD.h, tst.get(("yst", tt - 1), 0))
                    for half in range(2):
                        bA[0].guard(t)
                        for c in range(4):
                            cc = half * 4 + c
                            mm = t.transpose(pa0[:, c * 128:(c + 1) * 128],
                                             z3v[:, cc * 128:(cc + 1) * 128], st["identf"][:])
                        tz3 = kb.TE.inc(mm)
                        s.wait_ge(kb.TE.h, tz3)
                        sz3 = kb.SE.inc(s.copy(qtmp[:, half * 512:(half + 1) * 512]
                                               .bitcast(F32), pa0[:]))
                        bA[0].read_by(kb.SE.h, sz3)
                    gp.wait_ge(kb.SE.h, kb.SE.n)
                    tst[("yst", tt)] = kb.GD.inc16(gp.dma_start(
                        y_sc.ap()[:, :, 1 + tc0:1 + tc0 + 128].transpose([1, 0, 2]),
                        qtmp[:].bitcast(F32).rearrange("p (c i) -> p c i", i=128)))

                for it in range(5):
                    if it < 4:
                        x_stage1(it)
                    if it >= 1:
                        x_stage2(it - 1)
        gp.wait_ge(kb.GD.h, kb.GD.n)
        done = kb.GD.n
    for e in (kb.sync, kb.t, kb.v, kb.s):
        e.wait_ge(kb.GD.h, done)



def emit_Cfin(kb, x2_sc, y_sc, out_dram):
    nc, sync, gp, t, v, s = kb.nc, kb.sync, kb.gp, kb.t, kb.v, kb.s
    st = kb.st
    with nc.sbuf_tensor("cf_x", [128, 1024], F32) as xt, \
         nc.sbuf_tensor("cf_y", [128, 1024], F32) as yt, \
         nc.sbuf_tensor("cf_o", [128, 2 * 1024], F32) as ot, \
         nc.psum_tensor("cf_p", [128, 1024], F32) as pn:
        bn = Bank(pn)
        for tc in range(32):
            par = tc % 2
            o = ot[:, par * 1024:(par + 1) * 1024]
            sync.wait_ge(kb.GD.h, max(0, kb.GD.n - 16))
            sync.wait_ge(kb.TE.h, kb.TE.n)
            ldx = kb.LD.inc16(sync.dma_start(xt[:], x2_sc.ap()[tc * 128:(tc + 1) * 128, :]))
            ldy = kb.LD.inc16(sync.dma_start(
                yt[:].rearrange("p (c i) -> p c i", i=128),
                y_sc.ap()[:, :, 1 + tc * 128:1 + (tc + 1) * 128].transpose([1, 0, 2])))
            t.wait_ge(kb.LD.h, ldy)
            bn.guard(t)
            for c in range(8):
                mm = t.transpose(pn[:, c * 128:(c + 1) * 128], yt[:, c * 128:(c + 1) * 128],
                                 st["identf"][:])
            ty = kb.TE.inc(mm)
            v.wait_ge(kb.TE.h, ty)
            v.wait_ge(kb.LD.h, ldx)
            v.wait_ge(kb.GD.h, max(0, kb.GD.n - 16))
            xo = kb.VE.inc(v.scalar_tensor_tensor(o, pn[:], 0.5, xt[:], ALU.mult, ALU.add))
            bn.read_by(kb.VE.h, xo)
            gp.wait_ge(kb.VE.h, xo)
            kb.GD.inc16(gp.dma_start(out_dram.ap()[tc * 128:(tc + 1) * 128, :], o))
        gp.wait_ge(kb.GD.h, kb.GD.n)



WSHAPES_ALL = {
    "L0_attn_qkv_kernel": (1024, 3072), "L0_attn_qkv_bias": (3072,),
    "L0_attn_out_kernel": (1024, 1024), "L0_attn_out_bias": (1024,),
    "L0_ffn_wqkv": (2, 1024, 1536), "L0_ffn_q_scale": (2, 64),
    "L0_ffn_k_scale": (2, 64), "L0_ffn_wo": (2, 512, 1024),
    "L0_ffn_ln_g": (2, 1024), "L0_ffn_ln_b": (2, 1024),
    "L0_ffn_w1": (2, 1024, 5460), "L0_ffn_b1": (2, 5460),
    "L0_ffn_w2": (2, 2730, 1024), "L0_ffn_b2": (2, 1024),
}


def build_full():
    nc = bass.Bass("TRN2", target_bir_lowering=False, detect_race_conditions=False)
    ins = {n: nc.dram_tensor(n, list(sh), F32, kind="ExternalInput")
           for n, sh in WSHAPES_ALL.items()}
    xcore = nc.dram_tensor("xcore", [T, D], F32, kind="ExternalInput")
    consts = host_constants()
    cds = {}
    for n, a in consts.items():
        a = np.asarray(a)
        dt = F32 if a.dtype == np.float32 else BF16
        cds[n] = nc.dram_tensor("c_" + n, list(a.shape), dt, kind="ExternalInput")
    y_sc = nc.dram_tensor("y_sc", [8, 128, 1 + T], F32, kind="Internal")
    x1_sc = nc.dram_tensor("x1_sc", [T, D], F32, kind="Internal")
    x2_sc = nc.dram_tensor("x2_sc", [T, D], F32, kind="Internal")
    qT_sc = nc.dram_tensor("qT_sc", [8, 128, T], BF16, kind="Internal")
    xout = nc.dram_tensor("xout", [T, D], F32, kind="ExternalOutput")
    A_dram = [cds[n] for n in ("A1q", "A2q", "A1k", "A2k")]
    kb = KB(nc)
    alloc_shared(kb)
    emit_shared_init(kb, cds)
    emit_A0(kb, xcore, y_sc)
    for l in (0, 1):
        emit_ffn_attn(kb, l, y_sc, ins, A_dram)
        emit_ffn_mlp(kb, l, y_sc, ins)
    emit_dense_block(kb, xcore, y_sc, x1_sc, x2_sc, qT_sc, ins)
    for l in (0, 1):
        emit_ffn_attn(kb, l, y_sc, ins, A_dram)
        emit_ffn_mlp(kb, l, y_sc, ins)
    emit_Cfin(kb, x2_sc, y_sc, xout)
    return nc, consts


def kernel(**inputs):
    from concourse.bass_utils import run_bass_kernel_spmd
    nc, consts = build_full()
    base = {n: np.ascontiguousarray(np.asarray(inputs[n], dtype=np.float32))
            for n in WSHAPES_ALL}
    for n, a in consts.items():
        base["c_" + n] = np.ascontiguousarray(a)
    x = np.asarray(inputs["L0_attn_x"], dtype=np.float32)
    in_maps = []
    for i in range(8):
        m = dict(base)
        m["xcore"] = np.ascontiguousarray(x[i])
        in_maps.append(m)
    res = run_bass_kernel_spmd(nc, in_maps, core_ids=list(range(8)))
    return np.stack([np.asarray(r["xout"]) for r in res.results], axis=0)


# version 13
# speedup vs baseline: 1.9011x; 1.0130x over previous
import numpy as np
import ml_dtypes
import concourse.bass as bass
import concourse.mybir as mybir

F32 = mybir.dt.float32
BF16 = mybir.dt.bfloat16
FP8 = mybir.dt.float8e4
AF = mybir.ActivationFunctionType
ALU = mybir.AluOpType
AX = mybir.AxisListType

T = 4096
D = 1024
INV224 = float(np.float32(1.0 / 224.0))


class Sem:
    def __init__(self, nc, name):
        self.h = nc.alloc_semaphore(name)
        self.n = 0

    def inc(self, instr, k=1):
        instr.then_inc(self.h, k)
        self.n += k
        return self.n

    def inc16(self, instr):
        return self.inc(instr, 16)


class KB:
    def __init__(self, nc):
        self.nc = nc
        self.LD = Sem(nc, "LD")
        self.GD = Sem(nc, "GD")
        self.TE = Sem(nc, "TE")
        self.VE = Sem(nc, "VE")
        self.SE = Sem(nc, "SE")
        self.GP = Sem(nc, "GP")
        self.sync, self.gp = nc.sync, nc.gpsimd
        self.t, self.v, self.s = nc.tensor, nc.vector, nc.scalar
        self.st = {}
        self._uid = 0

    def uid(self):
        self._uid += 1
        return self._uid


def host_constants():
    DIM_HEAD, WINDOW, QK_SCALE = 64, 512, 8.0
    half_idx = np.arange(0, DIM_HEAD, 2, dtype=np.float64)
    inv_freq = 1.0 / 10000.0 ** (half_idx / DIM_HEAD)
    tt = np.arange(T, dtype=np.float64)
    freqs = tt[:, None] * inv_freq[None, :]
    freqs = np.concatenate([freqs, freqs], axis=-1)
    base = (half_idx + 0.4 * DIM_HEAD) / (1.4 * DIM_HEAD)
    power = (tt - T // 2) / float(WINDOW // 2)
    scale = base[None, :] ** power[:, None]
    scale = np.concatenate([scale, scale], axis=-1)
    cos, sin = np.cos(freqs), np.sin(freqs)
    sign = np.where(np.arange(DIM_HEAD) < 32, -1.0, 1.0)
    a1q = (QK_SCALE * cos * scale).T
    a2q = (QK_SCALE * sin * scale).T * sign[:, None]
    a1k = (cos / scale).T
    a2k = (sin / scale).T * sign[:, None]
    rep = lambda a: np.concatenate([a, a], axis=0).astype(np.float32)
    r = np.arange(128)
    prot = np.zeros((128, 128), np.float32)
    for m in range(128):
        prot[(m + 32) % 64 + 64 * (m // 64), m] = 1.0
    return {
        "A1q": rep(a1q), "A2q": rep(a2q), "A1k": rep(a1k), "A2k": rep(a2k),
        "identb": np.eye(128, dtype=ml_dtypes.bfloat16),
        "identf": np.eye(128, dtype=np.float32),
        "maskL": (r[None, :] <= r[:, None]).astype(ml_dtypes.bfloat16),
        "maskU": (r[:, None] <= r[None, :]).astype(ml_dtypes.bfloat16),
        "prot": prot,
        "onesall": np.ones((128, 128), np.float32),
        "onesallb": np.stack([r * 0 + 1, r * 0 + 1, (r < 64), (r >= 64)], 1
                             ).astype(ml_dtypes.bfloat16),
        "onesrowb": np.ones((1, 128), ml_dtypes.bfloat16),
        "ones2f": np.stack([(r < 64).astype(np.float32), (r >= 64).astype(np.float32)], 1),
        "bc2f": np.stack([(r < 64).astype(np.float32), (r >= 64).astype(np.float32)])[:, :],
        "epscol": np.full((128, 1), 1e-5, np.float32),
        "eps24c": np.full((128, 1), 1e-24, np.float32),
    }


def alloc_shared(kb):
    nc, st = kb.nc, kb.st
    st["identb"] = nc.alloc_sbuf_tensor("sb_identb", [128, 128], BF16)
    st["identf"] = nc.alloc_sbuf_tensor("sb_identf", [128, 128], F32)
    st["maskL"] = nc.alloc_sbuf_tensor("sb_maskL", [128, 128], BF16)
    st["maskU"] = nc.alloc_sbuf_tensor("sb_maskU", [128, 128], BF16)
    st["prot"] = nc.alloc_sbuf_tensor("sb_prot", [128, 128], F32)
    st["onesall"] = nc.alloc_sbuf_tensor("sb_onesall", [128, 128], F32)
    st["onesallb"] = nc.alloc_sbuf_tensor("sb_onesallb", [128, 4], BF16)
    st["onesrowb"] = nc.alloc_sbuf_tensor("sb_onesrowb", [1, 128], BF16)
    st["ones2f"] = nc.alloc_sbuf_tensor("sb_ones2f", [128, 2], F32)
    st["epscol"] = nc.alloc_sbuf_tensor("sb_epscol", [128, 1], F32)
    st["eps24c"] = nc.alloc_sbuf_tensor("sb_eps24c", [128, 1], F32)
    st["bc2f"] = nc.alloc_sbuf_tensor("sb_bc2f", [2, 128], F32)


def emit_shared_init(kb, consts_in):
    sync, v = kb.sync, kb.v
    st = kb.st
    for name in ("identb", "identf", "maskL", "maskU", "prot", "onesall",
                 "onesallb", "onesrowb", "ones2f", "epscol", "eps24c", "bc2f"):
        kb.LD.inc16(sync.dma_start(st[name][:], consts_in[name].ap()))
    kb.VE.inc(v.engine_nop())



def emit_ln_natural(kb, x, z, cols, sq):
    v, s = kb.v, kb.s
    v.reduce_sum(cols[:, 0:1], x[:], AX.X)
    v.drain()
    mean_tok = kb.VE.inc(v.tensor_scalar(cols[:, 1:2], cols[:, 0:1], 1.0 / D, None, ALU.mult))
    s.wait_ge(kb.VE.h, mean_tok)
    sqd = kb.SE.inc(s.activation(sq[:], x[:], AF.Square))
    v.wait_ge(kb.SE.h, sqd)
    v.reduce_sum(cols[:, 2:3], sq[:], AX.X)
    v.drain()
    v.tensor_scalar(cols[:, 3:4], cols[:, 2:3], 1.0 / D, None, ALU.mult)
    v.tensor_mul(cols[:, 4:5], cols[:, 1:2], cols[:, 1:2])
    v.drain()
    var_tok = kb.VE.inc(v.tensor_sub(cols[:, 5:6], cols[:, 3:4], cols[:, 4:5]))
    s.wait_ge(kb.VE.h, var_tok)
    s.activation(cols[:, 6:7], cols[:, 5:6], AF.Ln, bias=kb.st["epscol"][:, 0:1])
    s.drain()
    sqr = kb.SE.inc(s.activation(cols[:, 7:8], cols[:, 6:7], AF.Exp, scale=-0.5))
    v.wait_ge(kb.SE.h, sqr)
    v.scalar_tensor_tensor(cols[:, 8:9], cols[:, 1:2], -1.0,
                           cols[:, 7:8], ALU.mult, ALU.mult)
    v.drain()
    return kb.VE.inc(v.tensor_scalar(z[:], x[:], cols[:, 7:8], cols[:, 8:9],
                                     ALU.mult, ALU.add))



def emit_quant(kb, src, tmp_f32, tmp_fp8, out, qcols, blocks, width=128):
    v = kb.v
    n = blocks * width
    am, s2, r2 = qcols[:, 0:blocks], qcols[:, blocks:2 * blocks], qcols[:, 2 * blocks:3 * blocks]
    r3 = lambda ap: ap[:, 0:n].rearrange("p (b i) -> p b i", i=width)
    bc = lambda ap: ap.unsqueeze(2).broadcast_to([128, blocks, width])
    v.tensor_reduce(am, r3(src), AX.X, ALU.max, apply_absolute_value=True)
    v.drain()
    v.tensor_scalar(s2, am, INV224, 2e-12, ALU.mult, ALU.max)
    v.drain()
    v.reciprocal(r2, s2)
    v.drain()
    v.tensor_tensor(r3(tmp_f32), r3(src), bc(r2), ALU.mult)
    v.drain()
    v.tensor_copy(tmp_fp8[:, 0:n], tmp_f32[:, 0:n])
    v.drain()
    return v.tensor_tensor(r3(out), r3(tmp_fp8), bc(s2), ALU.mult)



SERIAL_A0 = False


def emit_A0(kb, x_src, y_dst):
    nc, sync, gp, t, v, s = kb.nc, kb.sync, kb.gp, kb.t, kb.v, kb.s
    st = kb.st
    identf = st["identf"]
    with nc.sbuf_tensor("a0_x", [128, 2 * 1024], F32) as xt, \
         nc.sbuf_tensor("a0_z", [128, 1024], F32) as zt, \
         nc.sbuf_tensor("a0_sq", [128, 1024], F32) as sqt, \
         nc.sbuf_tensor("a0_cols", [128, 16], F32) as cols, \
         nc.sbuf_tensor("a0_yst", [128, 2 * 1024], F32) as yst, \
         nc.psum_tensor("a0_ps", [128, 1024], F32) as ps0, \
         nc.psum_tensor("a0_ps2", [128, 1024], F32) as ps1:
        store_toks = [0, 0]
        prev_t = 0
        prev_z = 0
        for tt in range(T // 128):
            par = tt % 2
            x = xt[:, par * 1024:(par + 1) * 1024]
            ys = yst[:, par * 1024:(par + 1) * 1024]
            ps = (ps0, ps1)[par]
            sync.wait_ge(kb.SE.h, store_toks[par])
            ldn = kb.LD.inc16(sync.dma_start(x, x_src.ap()[tt * 128:(tt + 1) * 128, :]))
            v.wait_ge(kb.LD.h, ldn)
            v.wait_ge(kb.TE.h, prev_t)
            s.wait_ge(kb.LD.h, ldn)
            zdone = emit_ln_natural(kb, x, zt, cols, sqt)
            t.wait_ge(kb.VE.h, zdone)
            t.wait_ge(kb.SE.h, store_toks[par])
            for c in range(8):
                ins = t.transpose(ps[:, c * 128:(c + 1) * 128],
                                  zt[:, c * 128:(c + 1) * 128], identf[:])
            prev_t = tdone = kb.TE.inc(ins)
            s.wait_ge(kb.TE.h, tdone)
            store_toks[par] = kb.SE.inc(s.copy(ys, ps[:]))
            gp.wait_ge(kb.SE.h, store_toks[par])
            kb.GD.inc16(gp.dma_start(
                y_dst.ap()[:, :, 1 + tt * 128:1 + (tt + 1) * 128].transpose([1, 0, 2]),
                ys.rearrange("p (c i) -> p c i", i=128)))
            if SERIAL_A0:
                for e in (kb.sync, kb.t, kb.v, kb.s, kb.gp):
                    e.wait_ge(kb.GD.h, kb.GD.n)
                    e.wait_ge(kb.VE.h, kb.VE.n)
                    e.wait_ge(kb.SE.h, kb.SE.n)
                    e.wait_ge(kb.TE.h, kb.TE.n)
                    e.wait_ge(kb.LD.h, kb.LD.n)
        gp.wait_ge(kb.GD.h, kb.GD.n)
        done = kb.GD.n
    for e in (kb.sync, kb.t, kb.v, kb.s):
        e.wait_ge(kb.GD.h, done)



def load_ffn_attn_weights(kb, l, ins, wts, pn):
    nc, sync, gp, t, v, s = kb.nc, kb.sync, kb.gp, kb.t, kb.v, kb.s
    for kc in range(8):
        kb.GD.inc16(gp.dma_start(
            wts["wqkv"][:, kc * 1536:(kc + 1) * 1536],
            ins["L0_ffn_wqkv"].ap()[l, kc * 128:(kc + 1) * 128, :]))
    for ic in range(4):
        kb.GD.inc16(gp.dma_start(
            wts["wo"][:, ic * 1024:(ic + 1) * 1024],
            ins["L0_ffn_wo"].ap()[l, ic * 128:(ic + 1) * 128, :]))
    qs, ks = ins["L0_ffn_q_scale"], ins["L0_ffn_k_scale"]
    for half in range(2):
        b = 64 * half
        kb.LD.inc16(sync.dma_start(wts["qscol"][b:b + 64, 0:1], qs.ap()[l, :].unsqueeze(1)))
        kb.LD.inc16(sync.dma_start(wts["kscol"][b:b + 64, 0:1], ks.ap()[l, :].unsqueeze(1)))
        kb.LD.inc16(sync.dma_start(wts["qssh"][b:b + 32, 0:1], qs.ap()[l, 32:64].unsqueeze(1)))
        kb.LD.inc16(sync.dma_start(wts["qssh"][b + 32:b + 64, 0:1], qs.ap()[l, 0:32].unsqueeze(1)))
        kb.LD.inc16(sync.dma_start(wts["kssh"][b:b + 32, 0:1], ks.ap()[l, 32:64].unsqueeze(1)))
        kb.LD.inc16(sync.dma_start(wts["kssh"][b + 32:b + 64, 0:1], ks.ap()[l, 0:32].unsqueeze(1)))
    ldr1 = kb.LD.inc16(sync.dma_start(wts["qrow"][0:1, 0:64], qs.ap()[l, :].unsqueeze(0)))
    ldr = kb.LD.inc16(sync.dma_start(wts["krow"][0:1, 0:64], ks.ap()[l, :].unsqueeze(0)))
    v.wait_ge(kb.LD.h, ldr)
    v.tensor_reduce(wts["qrow"][0:1, 64:65], wts["qrow"][0:1, 0:64], AX.X, ALU.max,
                    apply_absolute_value=True)
    v.tensor_reduce(wts["krow"][0:1, 64:65], wts["krow"][0:1, 0:64], AX.X, ALU.max,
                    apply_absolute_value=True)
    v.drain()
    v.tensor_mul(wts["qrow"][0:1, 65:66], wts["qrow"][0:1, 64:65], wts["krow"][0:1, 64:65])
    v.drain()
    cn = kb.VE.inc(v.tensor_scalar(wts["qrow"][0:1, 66:67], wts["qrow"][0:1, 65:66],
                                   -16.0, None, ALU.mult))
    t.wait_ge(kb.VE.h, cn)
    mmn = kb.TE.inc(t.matmul(pn.ps[:, 0:1], kb.st["onesall"][0:1, :],
                             wts["qrow"][0:1, 66:67], start=True, stop=True))
    s.wait_ge(kb.TE.h, mmn)
    tok = kb.SE.inc(s.copy(wts["negC"][:], pn.ps[:, 0:1]))
    pn.read_by(kb.SE.h, tok)
    return tok


def alloc_ffn_attn(kb, ns):
    import contextlib
    nc = kb.nc
    es = contextlib.ExitStack()
    a = lambda n, sh, dt: es.enter_context(nc.sbuf_tensor(ns + n, sh, dt))
    wts = {
        "wqkv": a("wqkv", [128, 8 * 1536], BF16),
        "wo": a("wo", [128, 4 * 1024], BF16),
        "qscol": a("qscol", [128, 1], F32), "qssh": a("qssh", [128, 1], F32),
        "kscol": a("kscol", [128, 1], F32), "kssh": a("kssh", [128, 1], F32),
        "qrow": a("qrow", [1, 80], F32), "krow": a("krow", [1, 80], F32),
        "negC": a("negC", [128, 1], F32),
    }
    bufs = {
        "yw": a("yw", [128, 2 * 8 * 512], F32),
        "ywb": a("ywb", [128, 2 * 8 * 513], BF16),
        "guard": a("guard", [128, 8], F32),
        "tabs": a("tabs", [128, 2 * 4 * 512], F32),
        "tabsrc": a("tabsrc", [128, 2 * 4 * 512], F32),
        "qh": a("qh", [128, 4 * 512], BF16),
        "kh0": a("kh0", [128, 4 * 512], BF16), "kh1": a("kh1", [128, 4 * 512], BF16),
        "vv0": a("vv0", [128, 4 * 520], BF16), "vv1": a("vv1", [128, 4 * 520], BF16),
        "qn0": a("qn0", [128, 3 * 512], F32),
        "sqf": a("sqf", [128, 3 * 512], F32),
        "rowa": a("rowa", [2, 512], F32), "rowb": a("rowb", [2, 512], F32),
        "tt1": a("tt1", [128, 512], F32), "tt2": a("tt2", [128, 512], F32),
        "h1": a("h1", [128, 512], F32), "h2": a("h2", [128, 512], F32),
        "expb": a("expb", [128, 2 * 512], BF16),
        "rowr": a("rowr", [128, 512], F32),
        "bcf": a("bcf", [128, 512], F32),
        "oT": a("oT", [128, 4 * 512], BF16),
    }
    return es, wts, bufs


VALID_SPAN = {kc: (max(0, (kc - 4)) * 128, min(512, kc * 128 + 128)) for kc in range(8)}


class Bank:
    def __init__(self, ps):
        self.ps = ps
        self.rd = None

    def guard(self, eng):
        if self.rd is None:
            return
        rds = self.rd if isinstance(self.rd, list) else [self.rd]
        for semh, val in rds:
            eng.wait_ge(semh, val)

    def read_by(self, semh, val):
        self.rd = (semh, val)

    def read_by2(self, rd1, rd2):
        self.rd = [rd1, rd2]


def emit_ffn_attn(kb, l, y_sc, ins, A_dram):
    nc, sync, gp, t, v, s = kb.nc, kb.sync, kb.gp, kb.t, kb.v, kb.s
    st = kb.st
    u = kb.uid()
    es, wts, bu = alloc_ffn_attn(kb, f"fa{u}_")
    with es, \
         nc.psum_tensor(f"fa{u}_g0", [128, 512], F32) as pg0, \
         nc.psum_tensor(f"fa{u}_g1", [128, 512], F32) as pg1, \
         nc.psum_tensor(f"fa{u}_g2", [128, 512], F32) as pg2, \
         nc.psum_tensor(f"fa{u}_g3", [128, 512], F32) as pg3, \
         nc.psum_tensor(f"fa{u}_g4", [128, 512], F32) as pg4, \
         nc.psum_tensor(f"fa{u}_g5", [128, 512], F32) as pg5, \
         nc.psum_tensor(f"fa{u}_row", [2, 512], F32) as prow_ps, \
         nc.psum_tensor(f"fa{u}_srow", [1, 512], F32) as srow_ps:
        g = [Bank(p) for p in (pg0, pg1, pg2, pg3, pg4, pg5)]
        growA, growB = Bank(prow_ps), Bank(srow_ps)
        negc_tok = load_ffn_attn_weights(kb, l, ins, wts, g[0])
        v.memset(bu["guard"][:], 0.0)
        for vb in (bu["vv0"], bu["vv1"]):
            v.memset(vb[:].rearrange("p (a h i) -> p a h i", h=8, i=65)[:, :, :, 64:65], 1.0)
        init_tok = kb.VE.inc(v.engine_nop())
        t.wait_ge(kb.SE.h, negc_tok)
        gps = kb.gp

        wstate = {}

        def emit_wload(w):
            par = w % 2
            sync.wait_ge(kb.GD.h, wstate.get(("store", w - 2), 0))
            kb.LD.inc16(sync.dma_start(
                bu["yw"][:, par * 4096:(par + 1) * 4096].rearrange("p (c i) -> p c i", i=512),
                y_sc.ap()[:, :, 1 + w * 512:1 + (w + 1) * 512].transpose([1, 0, 2])))
            ldt = 0
            for j, A in enumerate(A_dram):
                ldt = kb.LD.inc16(sync.dma_start(
                    bu["tabsrc"][:, par * 2048 + j * 512:par * 2048 + (j + 1) * 512],
                    A.ap()[:, w * 512:(w + 1) * 512]))
            wstate[("ld", w)] = ldt

        def emit_wprep(w):
            par = w % 2
            tb = par * 2048
            yb = par * 4096
            wb = par * 4104
            v.wait_ge(kb.LD.h, wstate[("ld", w)])
            v.wait_ge(kb.TE.h, wstate.get(("vmm", w - 2), 0))
            v.tensor_scalar_mul(bu["tabs"][:, tb:tb + 512], bu["tabsrc"][:, tb:tb + 512], wts["qscol"][:])
            v.tensor_scalar_mul(bu["tabs"][:, tb + 512:tb + 1024], bu["tabsrc"][:, tb + 512:tb + 1024], wts["qssh"][:])
            v.tensor_scalar_mul(bu["tabs"][:, tb + 1024:tb + 1536], bu["tabsrc"][:, tb + 1024:tb + 1536], wts["kscol"][:])
            v.tensor_scalar_mul(bu["tabs"][:, tb + 1536:tb + 2048], bu["tabsrc"][:, tb + 1536:tb + 2048], wts["kssh"][:])
            for c in range(8):
                v.tensor_copy(bu["ywb"][:, wb + c * 513 + 1:wb + (c + 1) * 513],
                              bu["yw"][:, yb + c * 512:yb + (c + 1) * 512])
                v.tensor_copy(bu["ywb"][:, wb + c * 513:wb + c * 513 + 1], bu["guard"][:, c:c + 1])
            for c in range(8):
                v.tensor_copy(bu["guard"][:, c:c + 1],
                              bu["yw"][:, yb + c * 512 + 511:yb + c * 512 + 512])
            wstate[("hT", w)] = kb.VE.inc(v.engine_nop())

        emit_wload(0)
        emit_wprep(0)
        for w in range(8):
            par_w = w % 2
            kv_cur = (bu["kh0"], bu["kh1"])[w % 2]
            kv_prv = (bu["kh1"], bu["kh0"])[w % 2]
            v_cur = (bu["vv0"], bu["vv1"])[w % 2]
            v_prv = (bu["vv1"], bu["vv0"])[w % 2]
            if w + 1 < 8:
                emit_wload(w + 1)

            def hsrc(kc, tokslice=None):
                base = par_w * 4104 + kc * 513 + (1 if kc < 4 else 0)
                if tokslice is None:
                    return bu["ywb"][:, base:base + 512]
                return bu["ywb"][:, base + tokslice[0]:base + tokslice[1]]

            t.wait_ge(kb.VE.h, wstate[("hT", w)])
            state = {}

            def emit_mmq(oc):
                par = oc % 3
                pqk = g[(0, 1, 4)[par]]
                pqk.guard(t)
                for kc in range(8):
                    mm = t.matmul(pqk.ps[:],
                                  wts["wqkv"][:, kc * 1536 + oc * 128:kc * 1536 + (oc + 1) * 128],
                                  hsrc(kc), start=(kc == 0), stop=(kc == 7))
                state[("mmq", oc)] = kb.TE.inc(mm)

            def emit_tailA(oc):
                par = oc % 3
                pqk = g[(0, 1, 4)[par]]
                qn0 = bu["qn0"][:, par * 512:(par + 1) * 512]
                sqf = bu["sqf"][:, par * 512:(par + 1) * 512]
                s.wait_ge(kb.TE.h, state[("mmq", oc)])
                s.wait_ge(kb.GP.h, state.get(("gfin", oc - 3), 0))
                s.wait_ge(kb.TE.h, state.get(("mrot", oc - 3), 0))
                s.wait_ge(kb.TE.h, state.get(("mms", oc - 3), 0))
                cpq = kb.SE.inc(s.copy(qn0, pqk.ps[:]))
                s.drain()
                sq = kb.SE.inc(s.activation(sqf, pqk.ps[:], AF.Square))
                pqk.read_by(kb.SE.h, sq)
                state[("sq", oc)] = sq
                state[("cpq", oc)] = cpq

            def emit_tailB(oc):
                par = oc % 3
                qn0 = bu["qn0"][:, par * 512:(par + 1) * 512]
                sqf = bu["sqf"][:, par * 512:(par + 1) * 512]
                is_q = oc < 4
                c = oc if is_q else oc - 4
                dst = bu["qh"] if is_q else kv_cur
                tb = par_w * 2048
                T1 = bu["tabs"][:, tb:tb + 512] if is_q else bu["tabs"][:, tb + 1024:tb + 1536]
                T2 = bu["tabs"][:, tb + 512:tb + 1024] if is_q else bu["tabs"][:, tb + 1536:tb + 2048]
                qn0_tok = state[("cpq", oc)]
                t.wait_ge(kb.SE.h, state[("sq", oc)])
                growA.guard(t)
                mms = kb.TE.inc(t.matmul(prow_ps[:], st["ones2f"][:], sqf,
                                         start=True, stop=True))
                state[("mms", oc)] = mms
                g[2].guard(t)
                t.wait_ge(kb.SE.h, qn0_tok)
                mrot = kb.TE.inc(t.matmul(g[2].ps[:], st["prot"][:], qn0,
                                          start=True, stop=True))
                state[("mrot", oc)] = mrot
                s.wait_ge(kb.TE.h, mms)
                s.activation(bu["rowa"][:], prow_ps[:], AF.Ln, bias=kb.st["eps24c"][0:2, 0:1])
                s.drain()
                rr = kb.SE.inc(s.activation(bu["rowb"][:], bu["rowa"][:], AF.Exp, scale=-0.5))
                growA.read_by(kb.SE.h, rr)
                t.wait_ge(kb.SE.h, rr)
                g[3].guard(t)
                mb = kb.TE.inc(t.matmul(g[3].ps[:], st["bc2f"][:], bu["rowb"][:],
                                        start=True, stop=True))
                v.wait_ge(kb.TE.h, max(mb, mrot))
                v.tensor_tensor(bu["tt1"][:], g[3].ps[:], T1, ALU.mult)
                g3r = kb.VE.inc(v.tensor_tensor(bu["tt2"][:], g[3].ps[:], T2, ALU.mult))
                g[3].read_by(kb.VE.h, g3r)
                gps.wait_ge(kb.VE.h, g3r)
                gps.wait_ge(kb.VE.h, state.get(("fin", oc - 1), 0))
                gf = kb.GP.inc(gps.tensor_tensor(bu["h1"][:], qn0, bu["tt1"][:], ALU.mult))
                state[("gfin", oc)] = gf
                v.wait_ge(kb.TE.h, mrot)
                h2t = kb.VE.inc(v.tensor_tensor(bu["h2"][:], g[2].ps[:], bu["tt2"][:], ALU.mult))
                g[2].read_by(kb.VE.h, h2t)
                v.wait_ge(kb.GP.h, gf)
                v.drain()
                fin = kb.VE.inc(v.tensor_tensor(dst[:, c * 512:(c + 1) * 512],
                                                bu["h1"][:], bu["h2"][:], ALU.add))
                state[("fin", oc)] = fin

            for oc in range(11):
                if oc < 8:
                    emit_mmq(oc)
                if 0 <= oc - 2 < 8:
                    emit_tailA(oc - 2)
                if 0 <= oc - 3 < 8:
                    emit_tailB(oc - 3)
            for tt in range(4):
                par = tt % 2
                pvm = g[par]
                pvm.guard(t)
                for kc in range(8):
                    mm = t.matmul(pvm.ps[:], hsrc(kc, (tt * 128, (tt + 1) * 128)),
                                  wts["wqkv"][:, kc * 1536 + 1024:kc * 1536 + 1536],
                                  start=(kc == 0), stop=(kc == 7))
                mmv = kb.TE.inc(mm)
                s.wait_ge(kb.TE.h, mmv)
                vz = kb.SE.inc(s.copy(
                    v_cur[:, tt * 520:(tt + 1) * 520].rearrange("p (h i) -> p h i", i=65)[:, :, 0:64],
                    pvm.ps[:].rearrange("p (h i) -> p h i", i=64)))
                pvm.read_by(kb.SE.h, vz)
            wstate[("vmm", w)] = kb.TE.n
            if w + 1 < 8:
                emit_wprep(w + 1)
            vready = kb.SE.n
            t.wait_ge(kb.SE.h, vready)
            t.wait_ge(kb.GP.h, kb.GP.n)
            kcs = [kc for kc in range(8) if (w > 0 or kc >= 4)]
            hstate = {}

            def head_kc_loop(h):
                c, sub = h // 2, h % 2
                P0, P1 = sub * 64, sub * 64 + 64
                pvb = g[(4, 5, 3)[h % 3]]
                srowb = (growB, growA)[(h // 2) % 2]
                pvb.guard(t)
                if sub == 1:
                    srowb.guard(t)
                toks = {}
                n = len(kcs)
                for it in range(n + 2):
                    if it < n:
                        kc = kcs[it]
                        lo, hi = VALID_SPAN[kc]
                        par = it % 2
                        psc = g[par]
                        kbuf = kv_prv if kc < 4 else kv_cur
                        kcol = (kc % 4) * 128
                        psc.guard(t)
                        mm = t.matmul(psc.ps[:, 0:hi - lo],
                                      kbuf[P0:P1, c * 512 + kcol:c * 512 + kcol + 128],
                                      bu["qh"][P0:P1, c * 512 + lo:c * 512 + hi],
                                      start=True, stop=True, tile_position=(P0, 0))
                        toks[("sc", it)] = kb.TE.inc(mm)
                    if 1 <= it <= n:
                        i = it - 1
                        kc = kcs[i]
                        lo, hi = VALID_SPAN[kc]
                        par = i % 2
                        psc = g[par]
                        s.wait_ge(kb.TE.h, toks[("sc", i)])
                        s.wait_ge(kb.TE.h, toks.get(("pv", i - 2), 0))
                        ex = kb.SE.inc(s.activation(bu["expb"][:, par * 512 + lo:par * 512 + hi],
                                                    psc.ps[:, 0:hi - lo], AF.Exp,
                                                    bias=wts["negC"][:]))
                        psc.read_by(kb.SE.h, ex)
                        dep = (kb.SE.h, ex)
                        if kc < 4:
                            off = par * 512 + kc * 128
                            v.wait_ge(kb.SE.h, ex)
                            mt = kb.VE.inc(v.tensor_mul(bu["expb"][:, off:off + 128],
                                                        bu["expb"][:, off:off + 128], st["maskL"][:]))
                            dep = (kb.VE.h, mt)
                        else:
                            off = par * 512 + (kc - 4) * 128
                            v.wait_ge(kb.SE.h, ex)
                            mt = kb.VE.inc(v.tensor_mul(bu["expb"][:, off:off + 128],
                                                        bu["expb"][:, off:off + 128], st["maskU"][:]))
                            dep = (kb.VE.h, mt)
                        if lo > 0:
                            mz = kb.VE.inc(v.memset(bu["expb"][:, par * 512:par * 512 + lo], 0.0))
                            dep = (kb.VE.h, mz)
                        if hi < 512:
                            mz = kb.VE.inc(v.memset(bu["expb"][:, par * 512 + hi:par * 512 + 512], 0.0))
                            dep = (kb.VE.h, mz)
                        toks[("dep", i)] = dep
                    if it >= 2:
                        i = it - 2
                        kc = kcs[i]
                        par = i % 2
                        dsem, dval = toks[("dep", i)]
                        t.wait_ge(dsem, dval)
                        vbuf = v_prv if kc < 4 else v_cur
                        vbase = (kc % 4) * 520 + h * 65
                        eall = bu["expb"][:, par * 512:par * 512 + 512]
                        if sub == 0:
                            mm2 = t.matmul(pvb.ps[0:65, 0:512], vbuf[:, vbase:vbase + 65],
                                           eall, start=(i == 0), stop=(i == n - 1),
                                           skip_group_check=True)
                            toks[("pv", i)] = kb.TE.inc(mm2)
                        else:
                            t.matmul(pvb.ps[64:128, 0:512], vbuf[:, vbase:vbase + 64],
                                     eall, start=(i == 0), stop=(i == n - 1),
                                     skip_group_check=True, tile_position=(0, 64))
                            mm3 = t.matmul(srowb.ps[0:1, 0:512], st["onesallb"][:, 0:1],
                                           eall, start=(i == 0), stop=(i == n - 1),
                                           skip_group_check=True)
                            toks[("pv", i)] = kb.TE.inc(mm3)
                hstate[("pvdone", h)] = toks[("pv", len(kcs) - 1)]

            def head_norm(h):
                c, sub = h // 2, h % 2
                pvb = g[(4, 5, 3)[h % 3]]
                srowb = (growB, growA)[(h // 2) % 2]
                if sub == 0:
                    s.wait_ge(kb.TE.h, hstate[("pvdone", h)])
                    s.wait_ge(kb.TE.h, hstate.get(("mb2", h - 2), 0))
                    s.activation(bu["rowr"][64:65, :], pvb.ps[64:65, :], AF.Ln)
                    s.drain()
                    rc = kb.SE.inc(s.activation(bu["rowr"][64:65, :], bu["rowr"][64:65, :],
                                                AF.Exp, scale=-1.0))
                    t.wait_ge(kb.SE.h, rc)
                    g[2].guard(t)
                    mb2 = kb.TE.inc(t.matmul(g[2].ps[0:64, :], st["onesall"][64:65, 0:64],
                                             bu["rowr"][64:65, :], start=True, stop=True))
                    hstate[("mb2", h)] = mb2
                    s.wait_ge(kb.TE.h, mb2)
                    s.wait_ge(kb.VE.h, hstate.get(("ord", h - 2), 0))
                    sc2 = kb.SE.inc(s.copy(bu["bcf"][0:64, :], g[2].ps[0:64, :]))
                    g[2].read_by(kb.SE.h, sc2)
                    v.wait_ge(kb.SE.h, sc2)
                    ord_ = kb.VE.inc(v.tensor_tensor(
                        bu["oT"][0:64, c * 512:(c + 1) * 512],
                        pvb.ps[0:64, :], bu["bcf"][0:64, :], ALU.mult))
                    hstate[("ord", h)] = ord_
                else:
                    s.wait_ge(kb.TE.h, hstate[("pvdone", h)])
                    s.wait_ge(kb.TE.h, hstate.get(("mb2", h - 2), 0))
                    s.activation(bu["rowr"][0:1, :], srowb.ps[0:1, 0:512], AF.Ln)
                    s.drain()
                    rc = kb.SE.inc(s.activation(bu["rowr"][0:1, :], bu["rowr"][0:1, :],
                                                AF.Exp, scale=-1.0))
                    srowb.read_by(kb.SE.h, rc)
                    t.wait_ge(kb.SE.h, rc)
                    g[2].guard(t)
                    mb2 = kb.TE.inc(t.matmul(g[2].ps[64:128, :], st["onesall"][0:1, 0:64],
                                             bu["rowr"][0:1, :], start=True, stop=True,
                                             tile_position=(0, 64)))
                    hstate[("mb2", h)] = mb2
                    s.wait_ge(kb.TE.h, mb2)
                    s.wait_ge(kb.VE.h, hstate.get(("ord", h - 2), 0))
                    sc2 = kb.SE.inc(s.copy(bu["bcf"][64:128, :], g[2].ps[64:128, :]))
                    g[2].read_by(kb.SE.h, sc2)
                    v.wait_ge(kb.SE.h, sc2)
                    ord_ = kb.VE.inc(v.tensor_tensor(
                        bu["oT"][64:128, c * 512:(c + 1) * 512],
                        pvb.ps[64:128, :], bu["bcf"][64:128, :], ALU.mult))
                    hstate[("ord", h)] = ord_
                pvb.read_by(kb.VE.h, ord_)

            for h in range(10):
                if h >= 2:
                    head_norm(h - 2)
                if h < 8:
                    head_kc_loop(h)
            oT_ready = kb.VE.n
            t.wait_ge(kb.VE.h, oT_ready)
            for oc in range(8):
                par = oc % 2
                pw = g[par]
                pw.guard(t)
                for ic in range(4):
                    mm = t.matmul(pw.ps[:],
                                  wts["wo"][:, ic * 1024 + oc * 128:ic * 1024 + (oc + 1) * 128],
                                  bu["oT"][:, ic * 512:(ic + 1) * 512],
                                  start=(ic == 0), stop=(ic == 3))
                mmw = kb.TE.inc(mm)
                v.wait_ge(kb.TE.h, mmw)
                yu = kb.VE.inc(v.tensor_add(
                    bu["yw"][:, par_w * 4096 + oc * 512:par_w * 4096 + (oc + 1) * 512],
                    bu["yw"][:, par_w * 4096 + oc * 512:par_w * 4096 + (oc + 1) * 512], pw.ps[:]))
                pw.read_by(kb.VE.h, yu)
            gp.wait_ge(kb.VE.h, kb.VE.n)
            wstate[("store", w)] = kb.GD.inc16(gp.dma_start(
                y_sc.ap()[:, :, 1 + w * 512:1 + (w + 1) * 512].transpose([1, 0, 2]),
                bu["yw"][:, par_w * 4096:(par_w + 1) * 4096].rearrange("p (c i) -> p c i", i=512)))
        gp.wait_ge(kb.GD.h, kb.GD.n)
        done = kb.GD.n
    for e in (kb.sync, kb.t, kb.v, kb.s):
        e.wait_ge(kb.GD.h, done)



W1CH = [(wc * 512, min(2730, wc * 512 + 512)) for wc in range(6)]


def emit_ffn_mlp(kb, l, y_sc, ins):
    import contextlib
    nc, sync, gp, t, v, s = kb.nc, kb.sync, kb.gp, kb.t, kb.v, kb.s
    st = kb.st
    es = contextlib.ExitStack()
    u = kb.uid()
    a = lambda n, sh, dt: es.enter_context(nc.sbuf_tensor(f"mlp{u}_" + n, sh, dt))
    w1 = a("w1", [128, 8 * 5460], BF16)
    w2 = a("w2", [128, 22 * 1024], BF16)
    b1a = a("b1a", [128, 22], F32)
    b1g = a("b1g", [128, 22], F32)
    lg = a("lg", [128, 8], F32)
    lb = a("lb", [128, 8], F32)
    b2c = a("b2c", [128, 8], F32)
    ytg = a("ytg", [128, 3 * 8 * 257], F32)
    hsq = a("hsq", [128, 8 * 257], F32)
    hT = a("hT", [128, 2 * 8 * 256], BF16)
    guard = a("guard", [128, 8], F32)
    rows = a("rows", [1, 1024], F32)
    gel = a("gel", [128, 2 * 256], F32)
    abuf = a("abuf", [128, 2 * 256], F32)
    mT = a("mT", [128, 2 * 256], BF16)
    SUBS = []
    for wc, (c0, c1) in enumerate(W1CH):
        for sub in range((c1 - c0 + 127) // 128):
            s0 = c0 + sub * 128
            SUBS.append((s0, min(c1, s0 + 128)))
    with es, \
         nc.psum_tensor(f"mlpp{u}_y0", [128, 512], F32) as py0, \
         nc.psum_tensor(f"mlpp{u}_y1", [128, 512], F32) as py1, \
         nc.psum_tensor(f"mlpp{u}_y2", [128, 512], F32) as py2, \
         nc.psum_tensor(f"mlpp{u}_y3", [128, 512], F32) as py3, \
         nc.psum_tensor(f"mlpp{u}_ag0", [128, 512], F32) as pag0, \
         nc.psum_tensor(f"mlpp{u}_ag1", [128, 512], F32) as pag1, \
         nc.psum_tensor(f"mlpp{u}_r", [1, 512], F32) as pr, \
         nc.psum_tensor(f"mlpp{u}_bc", [128, 512], F32) as pbc:
        py = [Bank(p) for p in (py0, py1, py2, py3)]
        pag = [Bank(pag0), Bank(pag1)]
        brow, bbc = Bank(pr), Bank(pbc)
        for kc in range(8):
            kb.GD.inc16(gp.dma_start(w1[:, kc * 5460:(kc + 1) * 5460],
                                     ins["L0_ffn_w1"].ap()[l, kc * 128:(kc + 1) * 128, :]))
        for rc in range(21):
            kb.GD.inc16(gp.dma_start(w2[:, rc * 1024:(rc + 1) * 1024],
                                     ins["L0_ffn_w2"].ap()[l, rc * 128:(rc + 1) * 128, :]))
        kb.GD.inc16(gp.dma_start(w2[0:42, 21 * 1024:22 * 1024],
                                 ins["L0_ffn_w2"].ap()[l, 2688:2730, :]))
        b1 = ins["L0_ffn_b1"]
        ncd = nc.allow_non_contiguous_dma(reason="small bias-column loads")
        ncd.__enter__()
        kb.LD.inc16(sync.dma_start(b1a[:, 0:21],
                                   b1.ap()[l, 0:2688].rearrange("(c p) -> p c", p=128)))
        kb.LD.inc16(sync.dma_start(b1a[0:42, 21:22], b1.ap()[l, 2688:2730].unsqueeze(1)))
        kb.LD.inc16(sync.dma_start(b1g[:, 0:21],
                                   b1.ap()[l, 2730:5418].rearrange("(c p) -> p c", p=128)))
        kb.LD.inc16(sync.dma_start(b1g[0:42, 21:22], b1.ap()[l, 5418:5460].unsqueeze(1)))
        kb.LD.inc16(sync.dma_start(lg[:], ins["L0_ffn_ln_g"].ap()[l, :].rearrange("(c p) -> p c", p=128)))
        kb.LD.inc16(sync.dma_start(lb[:], ins["L0_ffn_ln_b"].ap()[l, :].rearrange("(c p) -> p c", p=128)))
        wtok = kb.LD.inc16(sync.dma_start(b2c[:], ins["L0_ffn_b2"].ap()[l, :].rearrange("(c p) -> p c", p=128)))
        ncd.__exit__(None, None, None)
        gtok = kb.GD.n
        v.memset(guard[:], 0.0)
        kb.VE.inc(v.engine_nop())
        t.wait_ge(kb.LD.h, wtok)
        t.wait_ge(kb.GD.h, gtok)
        s.wait_ge(kb.LD.h, wtok)
        v.wait_ge(kb.LD.h, wtok)

        stt_ = {}
        def ysl(tc, kc, shifted):
            base = (tc % 3) * 2056 + kc * 257 + (0 if shifted else 1)
            return ytg[:, base:base + 256]

        def emit_load(tc):
            sync.wait_ge(kb.GD.h, stt_.get(("store", tc - 3), 0))
            stt_[("ld", tc)] = kb.LD.inc16(sync.dma_start(
                ytg[:, (tc % 3) * 2056:(tc % 3) * 2056 + 2056]
                .rearrange("p (c i) -> p c i", i=257)[:, :, 1:257],
                y_sc.ap()[:, :, 1 + tc * 256:1 + (tc + 1) * 256].transpose([1, 0, 2])))

        def emit_prep(tc):
            par = tc % 3
            b0 = par * 2056
            v.wait_ge(kb.LD.h, stt_[("ld", tc)])
            for c in range(8):
                v.tensor_copy(ytg[:, b0 + c * 257:b0 + c * 257 + 1], guard[:, c:c + 1])
            for c in range(8):
                v.tensor_copy(guard[:, c:c + 1], ytg[:, b0 + c * 257 + 256:b0 + c * 257 + 257])
            v.drain()
            for c in range(8):
                sl = ytg[:, b0 + c * 257:b0 + (c + 1) * 257]
                v.tensor_mul(hsq[:, c * 257:(c + 1) * 257], sl, sl)
            hready = kb.VE.inc(v.engine_nop())
            t.wait_ge(kb.VE.h, hready)
            brow.guard(t)
            for kc in range(8):
                t.matmul(pr[0:1, 0:256], st["onesall"][:, 0:1], ysl(tc, kc, kc >= 4),
                         start=(kc == 0), stop=(kc == 7), skip_group_check=True)
            for kc in range(8):
                sl = hsq[:, kc * 257 + (0 if kc >= 4 else 1):kc * 257 + (0 if kc >= 4 else 1) + 256]
                mm = t.matmul(pr[0:1, 256:512], st["onesall"][:, 0:1], sl,
                              start=False, stop=(kc == 7), skip_group_check=True)
            mmr = kb.TE.inc(mm)
            v.wait_ge(kb.TE.h, mmr)
            v.tensor_scalar(rows[0:1, 0:256], pr[0:1, 0:256], 1.0 / D, None, ALU.mult)
            rd0 = kb.VE.inc(v.tensor_scalar(rows[0:1, 256:512], pr[0:1, 256:512],
                                            1.0 / D, None, ALU.mult))
            brow.read_by(kb.VE.h, rd0)
            v.drain()
            v.tensor_mul(rows[0:1, 512:768], rows[0:1, 0:256], rows[0:1, 0:256])
            v.drain()
            vv = kb.VE.inc(v.tensor_sub(rows[0:1, 256:512], rows[0:1, 256:512],
                                        rows[0:1, 512:768]))
            s.wait_ge(kb.VE.h, vv)
            s.activation(rows[0:1, 512:768], rows[0:1, 256:512], AF.Ln,
                         bias=kb.st["epscol"][0:1, 0:1])
            s.drain()
            sr = kb.SE.inc(s.activation(rows[0:1, 256:512], rows[0:1, 512:768],
                                        AF.Exp, scale=-0.5))
            v.wait_ge(kb.SE.h, sr)
            nm = kb.VE.inc(v.scalar_tensor_tensor(rows[0:1, 512:768], rows[0:1, 0:256], -1.0,
                                                  rows[0:1, 256:512], ALU.mult, ALU.mult))
            t.wait_ge(kb.VE.h, nm)
            bbc.guard(t)
            t.matmul(pbc[0:64, 0:256], st["onesall"][0:1, 0:64], rows[0:1, 256:512],
                     start=True, stop=False, skip_group_check=True)
            t.matmul(pbc[64:128, 0:256], st["onesall"][0:1, 0:64], rows[0:1, 256:512],
                     start=True, stop=False, skip_group_check=True, tile_position=(0, 64))
            t.matmul(pbc[0:64, 256:512], st["onesall"][0:1, 0:64], rows[0:1, 512:768],
                     start=False, stop=False, skip_group_check=True)
            mmb = kb.TE.inc(t.matmul(pbc[64:128, 256:512], st["onesall"][0:1, 0:64],
                                     rows[0:1, 512:768], start=False, stop=True,
                                     skip_group_check=True, tile_position=(0, 64)))
            v.wait_ge(kb.TE.h, mmb)
            last = 0
            hpar = tc % 2
            for kc in range(8):
                sh = kc >= 4
                sl = hsq[:, kc * 257:kc * 257 + 256]
                v.tensor_tensor(sl, ysl(tc, kc, sh), pbc[:, 0:256], ALU.mult)
                v.drain()
                v.tensor_add(sl, sl, pbc[:, 256:512])
                v.drain()
                last = kb.VE.inc(v.tensor_scalar(
                    hT[:, hpar * 2048 + kc * 256:hpar * 2048 + (kc + 1) * 256], sl,
                    lg[:, kc:kc + 1], lb[:, kc:kc + 1], ALU.mult, ALU.add))
            bbc.read_by(kb.VE.h, last)
            stt_[("hT", tc)] = last

        def emit_ag(tc, si):
            s0, s1 = SUBS[si]
            wdt = s1 - s0
            par = si % 2
            bank = pag[par]
            hTb = (tc % 2) * 2048
            bank.guard(t)
            if si == 0:
                t.wait_ge(kb.VE.h, stt_[("hT", tc)])
            for kc in range(8):
                t.matmul(bank.ps[0:wdt, 0:256],
                         w1[:, kc * 5460 + s0:kc * 5460 + s1],
                         hT[:, hTb + kc * 256:hTb + (kc + 1) * 256],
                         start=(kc == 0), stop=False, skip_group_check=True)
            for kc in range(8):
                mmg = t.matmul(bank.ps[0:wdt, 256:512],
                               w1[:, kc * 5460 + 2730 + s0:kc * 5460 + 2730 + s1],
                               hT[:, hTb + kc * 256:hTb + (kc + 1) * 256],
                               start=False, stop=(kc == 7), skip_group_check=True)
            stt_[("ag", si)] = kb.TE.inc(mmg)
            rc = s0 // 128
            s.wait_ge(kb.TE.h, stt_[("ag", si)])
            s.wait_ge(kb.VE.h, stt_.get(("mdone", si - 2), 0))
            sg = kb.SE.inc(s.activation(gel[0:wdt, par * 256:par * 256 + 256],
                                        bank.ps[0:wdt, 256:512], AF.Gelu,
                                        bias=b1g[0:wdt, rc:rc + 1]))
            v.wait_ge(kb.SE.h, sg)
            md = kb.VE.inc(v.scalar_tensor_tensor(
                mT[0:wdt, par * 256:par * 256 + 256], bank.ps[0:wdt, 0:256],
                b1a[0:wdt, rc:rc + 1], gel[0:wdt, par * 256:par * 256 + 256],
                ALU.add, ALU.mult))
            bank.read_by(kb.VE.h, md)
            stt_[("mdone", si)] = md

        def emit_w2(si, first_any, last_any):
            s0, s1 = SUBS[si]
            wdt = s1 - s0
            par = si % 2
            rc = s0 // 128
            t.wait_ge(kb.VE.h, stt_[("mdone", si)])
            for oc in range(8):
                bank, half = py[oc // 2], (oc % 2) * 256
                if first_any and oc % 2 == 0:
                    bank.guard(t)
                mm2 = t.matmul(bank.ps[:, half:half + 256],
                               w2[0:wdt, rc * 1024 + oc * 128:rc * 1024 + (oc + 1) * 128],
                               mT[0:wdt, par * 256:par * 256 + 256],
                               start=(first_any and oc % 2 == 0), stop=last_any,
                               skip_group_check=True)
            return kb.TE.inc(mm2)

        NS = len(SUBS)
        emit_load(0)
        emit_prep(0)
        for tc in range(16):
            if tc + 1 < 16:
                emit_load(tc + 1)
            for si in range(NS + 1):
                if si < NS:
                    emit_ag(tc, si)
                if si >= 1:
                    y2tok = emit_w2(si - 1, si - 1 == 0, si - 1 == NS - 1)
                if si == 0 and tc + 1 < 16:
                    emit_prep(tc + 1)
            par = tc % 3
            b0 = par * 2056
            v.wait_ge(kb.TE.h, y2tok)
            for oc in range(8):
                bank, half = py[oc // 2], (oc % 2) * 256
                yo = kb.VE.inc(v.scalar_tensor_tensor(
                    ytg[:, b0 + oc * 257 + 1:b0 + oc * 257 + 257], bank.ps[:, half:half + 256],
                    b2c[:, oc:oc + 1], ytg[:, b0 + oc * 257 + 1:b0 + oc * 257 + 257],
                    ALU.add, ALU.add))
                bank.read_by(kb.VE.h, yo)
            gp.wait_ge(kb.VE.h, kb.VE.n)
            stt_[("store", tc)] = kb.GD.inc16(gp.dma_start(
                y_sc.ap()[:, :, 1 + tc * 256:1 + (tc + 1) * 256].transpose([1, 0, 2]),
                ytg[:, b0:b0 + 2056].rearrange("p (c i) -> p c i", i=257)[:, :, 1:257]))
        gp.wait_ge(kb.GD.h, kb.GD.n)
        done = kb.GD.n
    for e in (kb.sync, kb.t, kb.v, kb.s):
        e.wait_ge(kb.GD.h, done)



def emit_weight_quant(kb, wdram, kcs, wq, stage, fp8st, qcols, blocks):
    sync, v = kb.sync, kb.v
    N = blocks * 128
    for kc in range(kcs):
        ldw = kb.LD.inc16(sync.dma_start(stage[:, 0:N], wdram.ap()[kc * 128:(kc + 1) * 128, :]))
        v.wait_ge(kb.LD.h, ldw)
        tok = kb.VE.inc(emit_quant(kb, stage[:, 0:N], stage[:, 0:N], fp8st,
                                   wq[:, kc * N:(kc + 1) * N], qcols, blocks))
        sync.wait_ge(kb.VE.h, tok)
    return kb.VE.n


def emit_dense_block(kb, x0_src, y_sc, x1_sc, x2_sc, qT_sc, ins):
    import contextlib
    nc, sync, gp, t, v, s = kb.nc, kb.sync, kb.gp, kb.t, kb.v, kb.s
    st = kb.st
    es = contextlib.ExitStack()
    a = lambda n, sh, dt: es.enter_context(nc.sbuf_tensor("dn_" + n, sh, dt))
    kT = a("kT", [128, 8 * 4096], BF16)
    vres = a("vres", [128, 32 * 1040], BF16)
    bcols = a("bcols", [128, 24], F32)
    obias = a("obias", [128, 8], F32)
    ncq = a("ncq", [2, 26], F32)
    negrow = a("negrow", [1, 16], F32)
    negc16 = a("negc16", [128, 16], F32)
    qcolsw = a("qcolsw", [128, 72], F32)
    with es:
        es1 = contextlib.ExitStack()
        b = lambda n, sh, dt: es1.enter_context(nc.sbuf_tensor("b1_" + n, sh, dt))
        wq = b("wq", [128, 8 * 3072], BF16)
        vbrow = b("vbrow", [1, 1024], BF16)
        fp8s = b("fp8s", [128, 1024], FP8)
        with nc.sbuf_tensor("b1_wst", [128, 3072], F32) as wst, \
             nc.sbuf_tensor("b1_f8w", [128, 3072], FP8) as f8w:
            emit_weight_quant(kb, ins["L0_attn_qkv_kernel"], 8, wq, wst,
                              f8w[:, 0:3072], qcolsw, 24)
        xt = b("xt", [128, 2 * 1024], F32)
        zt = b("zt", [128, 1024], F32)
        zl = b("zl", [128, 1024], F32)
        zdq = b("zdq", [128, 1024], BF16)
        zdqT = b("zdqT", [128, 1024], BF16)
        qst = b("qst", [128, 8 * 128], BF16)
        qsqb = b("qsqb", [128, 512], BF16)
        cols = b("cols", [128, 16], F32)
        with es1, \
             nc.psum_tensor("b1_p0", [128, 512], F32) as p0, \
             nc.psum_tensor("b1_p1", [128, 512], F32) as p1, \
             nc.psum_tensor("b1_p2", [128, 512], F32) as p2, \
             nc.psum_tensor("b1_p3", [128, 512], F32) as p3, \
             nc.psum_tensor("b1_pn", [128, 1024], F32) as pn, \
             nc.psum_tensor("b1_pb", [128, 1024], BF16) as pb, \
             nc.psum_tensor("b1_pr", [2, 512], F32) as pr:
            bk = [Bank(p) for p in (p0, p1, p2, p3)]
            bn, bb, br = Bank(pn), Bank(pb), Bank(pr)
            with nc.allow_non_contiguous_dma(reason="small bias cols"):
                kb.LD.inc16(sync.dma_start(
                    bcols[:], ins["L0_attn_qkv_bias"].ap().rearrange("(c p) -> p c", p=128)))
                btok = kb.LD.inc16(sync.dma_start(
                    obias[:], ins["L0_attn_out_bias"].ap().rearrange("(c p) -> p c", p=128)))
            kb.GD.inc16(gp.dma_start(vbrow[:], ins["L0_attn_qkv_bias"].ap()[2048:3072].unsqueeze(0)))
            v.memset(ncq[:], 0.0)
            vo = kb.VE.inc(v.memset(
                vres[:].rearrange("p (a h i) -> p a h i", h=16, i=65)[:, :, :, 64:65], 1.0))
            s.wait_ge(kb.LD.h, btok)
            t.wait_ge(kb.VE.h, vo)
            t.wait_ge(kb.GD.h, kb.GD.n)
            bst = {}

            def b1_load(tc):
                par = tc % 2
                sync.wait_ge(kb.VE.h, bst.get(("lnz", tc - 2), 0))
                sync.wait_ge(kb.GD.h, bst.get(("x1st", tc - 2), 0))
                ldx = kb.LD.inc16(sync.dma_start(xt[:, par * 1024:(par + 1) * 1024],
                                                 x0_src.ap()[tc * 128:(tc + 1) * 128, :]))
                sync.wait_ge(kb.TE.h, bst.get(("ty", tc - 1), 0))
                ldy = kb.LD.inc16(sync.dma_start(
                    zt[:].rearrange("p (c i) -> p c i", i=128),
                    y_sc.ap()[:, :, 1 + tc * 128:1 + (tc + 1) * 128].transpose([1, 0, 2])))
                bst[("ld", tc)] = ldy

            def b1_x1(tc):
                par = tc % 2
                x = xt[:, par * 1024:(par + 1) * 1024]
                t.wait_ge(kb.LD.h, bst[("ld", tc)])
                bn.guard(t)
                for c in range(8):
                    mm = t.transpose(pn[:, c * 128:(c + 1) * 128],
                                     zt[:, c * 128:(c + 1) * 128],
                                     st["identf"][:])
                ty = kb.TE.inc(mm)
                bst[("ty", tc)] = ty
                v.wait_ge(kb.TE.h, ty)
                x1done = kb.VE.inc(v.scalar_tensor_tensor(x, pn[:], 0.5, x,
                                                          ALU.mult, ALU.add))
                bn.read_by(kb.VE.h, x1done)
                bst[("x1done", tc)] = x1done
                gp.wait_ge(kb.VE.h, x1done)
                bst[("x1st", tc)] = kb.GD.inc16(gp.dma_start(
                    x1_sc.ap()[tc * 128:(tc + 1) * 128, :], x))

            def b1_lnq(tc):
                par = tc % 2
                x = xt[:, par * 1024:(par + 1) * 1024]
                s.wait_ge(kb.VE.h, bst[("x1done", tc)])
                bst[("lnz", tc)] = emit_ln_natural(kb, x, zl[:], cols, zl[:])
                v.wait_ge(kb.TE.h, bst.get(("tz", tc - 1), 0))
                qd = kb.VE.inc(emit_quant(kb, zl[:], zl[:], fp8s[:, 0:1024],
                                          zdq[:], qcolsw, 8))
                bst[("qd", tc)] = qd

            def b1_qkv(tc):
                par = tc % 2
                t.wait_ge(kb.VE.h, bst[("qd", tc)])
                bb.guard(t)
                for c in range(8):
                    mm = t.transpose(pb[:, c * 128:(c + 1) * 128],
                                     zdq[:, c * 128:(c + 1) * 128],
                                     st["identb"][:])
                tz = kb.TE.inc(mm)
                bst[("tz", tc)] = tz
                s.wait_ge(kb.TE.h, tz)
                s.wait_ge(kb.TE.h, bst.get(("lastmm", tc - 1), 0))
                ztok = kb.SE.inc(s.copy(zdqT[:], pb[:]))
                bb.read_by(kb.SE.h, ztok)
                t.wait_ge(kb.SE.h, ztok)
                zT = zdqT[:]
                if tc >= 1:
                    b1_chh(tc - 1)
                for oc in range(16):
                    pq = bk[oc % 4]
                    pq.guard(t)
                    for c in range(8):
                        mm = t.matmul(pq.ps[:, 0:128],
                                      wq[:, c * 3072 + oc * 128:c * 3072 + (oc + 1) * 128],
                                      zT[:, c * 128:(c + 1) * 128],
                                      start=(c == 0), stop=(c == 7))
                    mq = kb.TE.inc(mm)
                    is_q = oc < 8
                    dst = (qst[:, oc * 128:(oc + 1) * 128] if is_q else
                           kT[:, (oc - 8) * 4096 + tc * 128:(oc - 8) * 4096 + (tc + 1) * 128])
                    v.wait_ge(kb.TE.h, mq)
                    if is_q:
                        v.wait_ge(kb.GD.h, bst.get(("qstst", tc - 1), 0))
                    if oc == 0:
                        v.wait_ge(kb.SE.h, bst.get(("nsq", tc - 1), 0))
                    ev = kb.VE.inc(v.tensor_scalar(dst, pq.ps[:, 0:128],
                                                   bcols[:, oc:oc + 1], None, ALU.add))
                    pq.read_by(kb.VE.h, ev)
                qkv_done = kb.VE.n
                bst[("qkvd", tc)] = qkv_done
                gp.wait_ge(kb.VE.h, qkv_done)
                bst[("qstst", tc)] = kb.GD.inc16(gp.dma_start(
                    qT_sc.ap()[:, :, tc * 128:(tc + 1) * 128].transpose([1, 0, 2]),
                    qst[:].rearrange("p (c i) -> p c i", i=128)))
                b1_vnat(tc, zT)

            def b1_chh(tc):
                s.wait_ge(kb.VE.h, bst[("qkvd", tc)])
                for half in range(2):
                    for rnd in range(2):
                        s.wait_ge(kb.TE.h, bst.get(("nmm",), 0))
                        if half == 0:
                            srcq = qst[:, rnd * 512:(rnd + 1) * 512]
                            sqi = s.activation(qsqb[:].rearrange("p (c i) -> p c i", i=128),
                                               srcq.rearrange("p (c i) -> p c i", i=128),
                                               AF.Square)
                        else:
                            sqi = s.activation(
                                qsqb[:].rearrange("p (c i) -> p c i", i=128),
                                kT[:].rearrange("p (c i) -> p c i", i=4096)
                                [:, rnd * 4:(rnd + 1) * 4,
                                 tc * 128:(tc + 1) * 128], AF.Square)
                        nsq = kb.SE.inc(sqi)
                        bst[("nsq", tc)] = nsq
                        t.wait_ge(kb.SE.h, nsq)
                        br.guard(t)
                        for j in range(4):
                            mm = t.matmul(pr[:, j * 128:(j + 1) * 128],
                                          st["onesallb"][:, 2:4],
                                          qsqb[:, j * 128:(j + 1) * 128],
                                          start=True, stop=True, skip_group_check=True)
                        msr = kb.TE.inc(mm)
                        bst[("nmm",)] = msr
                        v.wait_ge(kb.TE.h, msr)
                        v.tensor_reduce(ncq[0:2, 20:24],
                                        pr[:].rearrange("p (a i) -> p a i", i=128),
                                        AX.X, ALU.max)
                        v.drain()
                        c0 = half * 8 + rnd * 4
                        mx = kb.VE.inc(v.tensor_max(ncq[0:2, c0:c0 + 4],
                                                    ncq[0:2, c0:c0 + 4],
                                                    ncq[0:2, 20:24]))
                        br.read_by(kb.VE.h, mx)
                        v.drain()

            def b1_vnat(tc, zT):
                for vc in range(2):
                    pv = bk[2 + vc]
                    pv.guard(t)
                    for c in range(8):
                        t.matmul(pv.ps[:], zT[:, c * 128:(c + 1) * 128],
                                 wq[:, c * 3072 + 2048 + vc * 512:c * 3072 + 2048 + (vc + 1) * 512],
                                 start=(c == 0), stop=False)
                    mmv = t.matmul(pv.ps[:], st["onesrowb"][:, :],
                                   vbrow[:, vc * 512:(vc + 1) * 512],
                                   start=False, stop=True)
                    mvt = kb.TE.inc(mmv)
                    bst[("lastmm", tc)] = mvt
                    s.wait_ge(kb.TE.h, mvt)
                    vz = kb.SE.inc(s.copy(
                        vres[:, tc * 1040 + vc * 520:tc * 1040 + (vc + 1) * 520]
                        .rearrange("p (h i) -> p h i", i=65)[:, :, 0:64],
                        pv.ps[:].rearrange("p (h i) -> p h i", i=64)))
                    pv.read_by(kb.SE.h, vz)

            b1_load(0)
            b1_x1(0)
            b1_lnq(0)
            for tc in range(32):
                if tc + 1 < 32:
                    b1_load(tc + 1)
                    b1_x1(tc + 1)
                b1_qkv(tc)
                if tc + 1 < 32:
                    b1_lnq(tc + 1)
            b1_chh(31)
            v.wait_ge(kb.SE.h, kb.SE.n)
            s.wait_ge(kb.VE.h, kb.VE.n)
            sqn = kb.SE.inc(s.activation(ncq[0:2, 16:24], ncq[0:2, 0:8], AF.Sqrt))
            s.drain()
            sqk = kb.SE.inc(s.activation(ncq[0:2, 0:8], ncq[0:2, 8:16], AF.Sqrt))
            v.wait_ge(kb.SE.h, sqk)
            v.tensor_mul(ncq[0:2, 8:16], ncq[0:2, 16:24], ncq[0:2, 0:8])
            v.drain()
            cn = kb.VE.inc(v.tensor_scalar(ncq[0:2, 16:24], ncq[0:2, 8:16],
                                           -0.1255, None, ALU.mult))
            gp.wait_ge(kb.VE.h, cn)
            with nc.allow_non_contiguous_dma(reason="tiny negC shuffle"):
                gp.dma_start(negrow[:].rearrange("a (c p) -> a c p", p=2)[:, :, 0:1],
                             ncq[0:1, 16:24]).then_inc(kb.GD.h, 16)
                kb.GD.n += 16
                ng = kb.GD.inc16(gp.dma_start(
                    negrow[:].rearrange("a (c p) -> a c p", p=2)[:, :, 1:2],
                    ncq[1:2, 16:24]))
            t.wait_ge(kb.GD.h, ng)
            bn.guard(t)
            mmn = kb.TE.inc(t.matmul(pn[:, 0:16], st["onesall"][0:1, :], negrow[:],
                                     start=True, stop=True))
            s.wait_ge(kb.TE.h, mmn)
            nct = kb.SE.inc(s.copy(negc16[:], pn[:, 0:16]))
            bn.read_by(kb.SE.h, nct)
            for e in (kb.sync, kb.t, kb.v):
                e.wait_ge(kb.SE.h, kb.SE.n)
            fin1 = kb.GD.n
            gp.wait_ge(kb.GD.h, fin1)
        es2 = contextlib.ExitStack()
        b2 = lambda n, sh, dt: es2.enter_context(nc.sbuf_tensor("b2_" + n, sh, dt))
        wout = b2("wout", [128, 8 * 1024], BF16)
        fp8s = b2("fp8s2", [128, 1024], FP8)
        with nc.sbuf_tensor("b2_wst2", [128, 1024], F32) as wst2:
            emit_weight_quant(kb, ins["L0_attn_out_kernel"], 8, wout, wst2,
                              fp8s[:, 0:1024], qcolsw, 8)
        qT = b2("qT", [128, 8 * 512], BF16)
        expb = b2("expb", [128, 4 * 512], BF16)
        oT = b2("oT", [128, 8 * 512], BF16)
        odqT = b2("odqT", [128, 8 * 512], BF16)
        oNat = b2("oNat", [128, 2 * 1024], BF16)
        odq = b2("odq", [128, 1024], BF16)
        x1t = b2("x1t", [128, 2 * 1024], F32)
        z3 = b2("z3", [128, 2 * 1024], F32)
        qtmp = b2("qtmp", [128, 1024], F32)
        rowr = b2("rowr", [128, 512], F32)
        bcf = b2("bcf", [128, 512], F32)
        cols2 = b2("cols2", [128, 16], F32)
        with es2, \
             nc.psum_tensor("b2_a0", [128, 512], F32) as pa0, \
             nc.psum_tensor("b2_a1", [128, 512], F32) as pa1, \
             nc.psum_tensor("b2_b0", [128, 512], F32) as pb0, \
             nc.psum_tensor("b2_b1", [128, 512], F32) as pb1, \
             nc.psum_tensor("b2_v0", [128, 512], F32) as pv0, \
             nc.psum_tensor("b2_v1", [128, 512], F32) as pv1, \
             nc.psum_tensor("b2_bT", [128, 1024], BF16) as pbT:
            bA = [Bank(pa0), Bank(pa1)]
            bB = [Bank(pb0), Bank(pb1)]
            bpA, bpB, bbt = Bank(pv0), Bank(pv1), Bank(pbT)
            t.wait_ge(kb.VE.h, kb.VE.n)
            for qc in range(8):
                sync.wait_ge(kb.TE.h, kb.TE.n)
                ldq = kb.LD.inc16(sync.dma_start(
                    qT[:].rearrange("p (c i) -> p c i", i=512),
                    qT_sc.ap()[:, :, qc * 512:(qc + 1) * 512].transpose([1, 0, 2])))
                t.wait_ge(kb.LD.h, ldq)
                nstate = {}

                def b2_norm(hp):
                    h0, h1 = 2 * hp, 2 * hp + 1
                    pvt = nstate[("pvt", hp)]
                    s.wait_ge(kb.TE.h, pvt)
                    s.activation(rowr[64:65, :], pv0[64:65, :], AF.Ln)
                    s.drain()
                    rc0 = kb.SE.inc(s.activation(rowr[64:65, :], rowr[64:65, :],
                                                 AF.Exp, scale=-1.0))
                    t.wait_ge(kb.SE.h, rc0)
                    bA[0].guard(t)
                    mb0 = kb.TE.inc(t.matmul(pa0[0:64, :], st["onesall"][64:65, 0:64],
                                             rowr[64:65, :], start=True, stop=True))
                    s.wait_ge(kb.TE.h, mb0)
                    s.wait_ge(kb.VE.h, nstate.get(("o0", hp - 1), 0))
                    sc0 = kb.SE.inc(s.copy(bcf[0:64, :], pa0[0:64, :]))
                    bA[0].read_by(kb.SE.h, sc0)
                    v.wait_ge(kb.SE.h, sc0)
                    o0 = kb.VE.inc(v.tensor_tensor(oT[0:64, hp * 512:(hp + 1) * 512],
                                                   pv0[0:64, :], bcf[0:64, :], ALU.mult))
                    nstate[("o0", hp)] = o0
                    bpA.read_by(kb.VE.h, o0)
                    s.activation(rowr[64:65, :], pv1[64:65, :], AF.Ln)
                    s.drain()
                    rc1 = kb.SE.inc(s.activation(rowr[64:65, :], rowr[64:65, :],
                                                 AF.Exp, scale=-1.0))
                    t.wait_ge(kb.SE.h, rc1)
                    bA[1].guard(t)
                    mb1 = kb.TE.inc(t.matmul(pa1[0:64, :], st["onesall"][64:65, 0:64],
                                             rowr[64:65, :], start=True, stop=True))
                    s.wait_ge(kb.TE.h, mb1)
                    s.wait_ge(kb.VE.h, o0)
                    s.wait_ge(kb.VE.h, nstate.get(("o1", hp - 1), 0))
                    sc1 = kb.SE.inc(s.copy(bcf[0:64, :], pa1[0:64, :]))
                    bA[1].read_by(kb.SE.h, sc1)
                    v.wait_ge(kb.SE.h, sc1)
                    o1 = kb.VE.inc(v.tensor_tensor(odqT[0:64, hp * 512:(hp + 1) * 512],
                                                   pv1[0:64, :], bcf[0:64, :], ALU.mult))
                    nstate[("o1", hp)] = o1
                    bpB.read_by(kb.VE.h, o1)

                for hp in range(8):
                  if True:
                    h0, h1 = 2 * hp, 2 * hp + 1
                    bpA.guard(t)
                    bpB.guard(t)
                    exp_tok = {}
                    for it in range(34):
                        if it < 32:
                            kc = it
                            par = kc % 2
                            pA, pB = bA[par], bB[par]
                            pA.guard(t)
                            t.matmul(pA.ps[:], kT[0:64, hp * 4096 + kc * 128:hp * 4096 + (kc + 1) * 128],
                                     qT[0:64, hp * 512:(hp + 1) * 512],
                                     start=True, stop=True)
                            pB.guard(t)
                            mmB = t.matmul(pB.ps[:], kT[64:128, hp * 4096 + kc * 128:hp * 4096 + (kc + 1) * 128],
                                           qT[64:128, hp * 512:(hp + 1) * 512],
                                           start=True, stop=True, tile_position=(64, 0))
                            kb.TE.inc(mmB)
                            sc_tok = kb.TE.n
                        if 1 <= it <= 32:
                            kc = it - 1
                            par = kc % 2
                            pA, pB = bA[par], bB[par]
                            s.wait_ge(kb.TE.h, exp_tok.get(("sc", kc), kb.TE.n))
                            s.wait_ge(kb.TE.h, exp_tok.get(("pv", kc - 2), 0))
                            exA = kb.SE.inc(s.activation(expb[:, (par * 2) * 512:(par * 2) * 512 + 512],
                                                         pA.ps[:], AF.Exp, bias=negc16[:, h0:h0 + 1],
                                                         scale=0.125))
                            pA.read_by(kb.SE.h, exA)
                            exB = kb.SE.inc(s.activation(expb[:, (par * 2 + 1) * 512:(par * 2 + 1) * 512 + 512],
                                                         pB.ps[:], AF.Exp, bias=negc16[:, h1:h1 + 1],
                                                         scale=0.125))
                            pB.read_by(kb.SE.h, exB)
                            exp_tok[("ex", kc)] = exB
                        if it >= 2:
                            kc = it - 2
                            par = kc % 2
                            t.wait_ge(kb.SE.h, exp_tok[("ex", kc)])
                            eA = expb[:, (par * 2) * 512:(par * 2) * 512 + 512]
                            eB = expb[:, (par * 2 + 1) * 512:(par * 2 + 1) * 512 + 512]
                            t.matmul(pv0[0:65, :], vres[:, kc * 1040 + h0 * 65:kc * 1040 + h0 * 65 + 65],
                                     eA, start=(kc == 0), stop=(kc == 31), skip_group_check=True)
                            pvt = kb.TE.inc(t.matmul(
                                pv1[0:65, :], vres[:, kc * 1040 + h1 * 65:kc * 1040 + h1 * 65 + 65],
                                eB, start=(kc == 0), stop=(kc == 31), skip_group_check=True))
                            exp_tok[("pv", kc)] = pvt
                            nstate[("pvt", hp)] = pvt
                        if it < 32:
                            exp_tok[("sc", it)] = sc_tok
                    b2_norm(hp)
                tst = {}
                oT_ok = kb.VE.n
                t.wait_ge(kb.VE.h, oT_ok)

                def q_stageA(tt):
                    par = tt % 2
                    bbt.guard(t)
                    for c in range(8):
                        t.transpose(pbT[:, c * 128:c * 128 + 64],
                                    oT[0:64, c * 512 + tt * 128:c * 512 + (tt + 1) * 128],
                                    st["identb"][0:64, 0:64])
                        mm = t.transpose(pbT[:, c * 128 + 64:(c + 1) * 128],
                                         odqT[0:64, c * 512 + tt * 128:c * 512 + (tt + 1) * 128],
                                         st["identb"][0:64, 0:64])
                    to1 = kb.TE.inc(mm)
                    s.wait_ge(kb.TE.h, to1)
                    s.wait_ge(kb.VE.h, tst.get(("qv", tt - 2), 0))
                    so1 = kb.SE.inc(s.copy(oNat[:, par * 1024:(par + 1) * 1024], pbT[:]))
                    bbt.read_by(kb.SE.h, so1)
                    tst[("ev", tt)] = so1

                def q_stageB(tt):
                    par = tt % 2
                    v.wait_ge(kb.SE.h, tst[("ev", tt)])
                    v.wait_ge(kb.TE.h, tst.get(("tb", tt - 1), 0))
                    qo = kb.VE.inc(emit_quant(kb, oNat[:, par * 1024:(par + 1) * 1024],
                                              qtmp[:], fp8s[:, 0:1024], odq[:], qcolsw, 8))
                    tst[("qv", tt)] = qo
                    t.wait_ge(kb.VE.h, qo)
                    bbt.guard(t)
                    for c in range(8):
                        mm = t.transpose(pbT[:, c * 128:(c + 1) * 128],
                                         odq[:, c * 128:(c + 1) * 128], st["identb"][:])
                    to2 = kb.TE.inc(mm)
                    tst[("tb", tt)] = to2
                    s.wait_ge(kb.TE.h, to2)
                    so2 = kb.SE.inc(s.copy(
                        odqT[:].rearrange("p (c i) -> p c i", i=512)[:, :, tt * 128:(tt + 1) * 128],
                        pbT[:].rearrange("p (c i) -> p c i", i=128)))
                    bbt.read_by(kb.SE.h, so2)

                for it in range(5):
                    if it < 4:
                        q_stageA(it)
                    if it >= 1:
                        q_stageB(it - 1)
                odqready = kb.SE.n
                t.wait_ge(kb.SE.h, odqready)
                for oc in range(8):
                    par = oc % 2
                    pO = bA[par]
                    pO.guard(t)
                    for c in range(8):
                        mm = t.matmul(pO.ps[:], wout[:, c * 1024 + oc * 128:c * 1024 + (oc + 1) * 128],
                                      odqT[:, c * 512:(c + 1) * 512],
                                      start=(c == 0), stop=(c == 7))
                    mo = kb.TE.inc(mm)
                    v.wait_ge(kb.TE.h, mo)
                    ot = kb.VE.inc(v.tensor_scalar(oT[:, oc * 512:(oc + 1) * 512],
                                                   pO.ps[:], obias[:, oc:oc + 1], None, ALU.add))
                    pO.read_by(kb.VE.h, ot)
                outTready = kb.VE.n
                t.wait_ge(kb.VE.h, outTready)

                def x_stage1(tt):
                    par = tt % 2
                    tc0 = qc * 512 + tt * 128
                    x2v = x1t[:, par * 1024:(par + 1) * 1024]
                    sync.wait_ge(kb.VE.h, tst.get(("lnz", tt - 2), 0))
                    sync.wait_ge(kb.GD.h, tst.get(("x2st", tt - 2), 0))
                    ldx1 = kb.LD.inc16(sync.dma_start(x2v, x1_sc.ap()[tc0:tc0 + 128, :]))
                    bbt.guard(t)
                    for c in range(8):
                        mm = t.transpose(pbT[:, c * 128:(c + 1) * 128],
                                         oT[:, c * 512 + tt * 128:c * 512 + (tt + 1) * 128],
                                         st["identb"][:])
                    tx = kb.TE.inc(mm)
                    v.wait_ge(kb.TE.h, tx)
                    v.wait_ge(kb.LD.h, ldx1)
                    x2d = kb.VE.inc(v.tensor_add(x2v, x2v, pbT[:]))
                    bbt.read_by(kb.VE.h, x2d)
                    tst[("x2", tt)] = x2d
                    gp.wait_ge(kb.VE.h, x2d)
                    tst[("x2st", tt)] = kb.GD.inc16(gp.dma_start(
                        x2_sc.ap()[tc0:tc0 + 128, :], x2v))

                def x_stage2(tt):
                    par = tt % 2
                    tc0 = qc * 512 + tt * 128
                    x2v = x1t[:, par * 1024:(par + 1) * 1024]
                    z3v = z3[:, par * 1024:(par + 1) * 1024]
                    s.wait_ge(kb.VE.h, tst[("x2", tt)])
                    tst[("lnz", tt)] = emit_ln_natural(kb, x2v, z3v, cols2, z3v)
                    t.wait_ge(kb.VE.h, tst[("lnz", tt)])
                    s.wait_ge(kb.GD.h, tst.get(("yst", tt - 1), 0))
                    for half in range(2):
                        bA[0].guard(t)
                        for c in range(4):
                            cc = half * 4 + c
                            mm = t.transpose(pa0[:, c * 128:(c + 1) * 128],
                                             z3v[:, cc * 128:(cc + 1) * 128], st["identf"][:])
                        tz3 = kb.TE.inc(mm)
                        s.wait_ge(kb.TE.h, tz3)
                        sz3 = kb.SE.inc(s.copy(qtmp[:, half * 512:(half + 1) * 512]
                                               .bitcast(F32), pa0[:]))
                        bA[0].read_by(kb.SE.h, sz3)
                    gp.wait_ge(kb.SE.h, kb.SE.n)
                    tst[("yst", tt)] = kb.GD.inc16(gp.dma_start(
                        y_sc.ap()[:, :, 1 + tc0:1 + tc0 + 128].transpose([1, 0, 2]),
                        qtmp[:].bitcast(F32).rearrange("p (c i) -> p c i", i=128)))

                for it in range(5):
                    if it < 4:
                        x_stage1(it)
                    if it >= 1:
                        x_stage2(it - 1)
        gp.wait_ge(kb.GD.h, kb.GD.n)
        done = kb.GD.n
    for e in (kb.sync, kb.t, kb.v, kb.s):
        e.wait_ge(kb.GD.h, done)



def emit_Cfin(kb, x2_sc, y_sc, out_dram):
    nc, sync, gp, t, v, s = kb.nc, kb.sync, kb.gp, kb.t, kb.v, kb.s
    st = kb.st
    with nc.sbuf_tensor("cf_x", [128, 1024], F32) as xt, \
         nc.sbuf_tensor("cf_y", [128, 1024], F32) as yt, \
         nc.sbuf_tensor("cf_o", [128, 2 * 1024], F32) as ot, \
         nc.psum_tensor("cf_p", [128, 1024], F32) as pn:
        bn = Bank(pn)
        for tc in range(32):
            par = tc % 2
            o = ot[:, par * 1024:(par + 1) * 1024]
            sync.wait_ge(kb.GD.h, max(0, kb.GD.n - 16))
            sync.wait_ge(kb.TE.h, kb.TE.n)
            ldx = kb.LD.inc16(sync.dma_start(xt[:], x2_sc.ap()[tc * 128:(tc + 1) * 128, :]))
            ldy = kb.LD.inc16(sync.dma_start(
                yt[:].rearrange("p (c i) -> p c i", i=128),
                y_sc.ap()[:, :, 1 + tc * 128:1 + (tc + 1) * 128].transpose([1, 0, 2])))
            t.wait_ge(kb.LD.h, ldy)
            bn.guard(t)
            for c in range(8):
                mm = t.transpose(pn[:, c * 128:(c + 1) * 128], yt[:, c * 128:(c + 1) * 128],
                                 st["identf"][:])
            ty = kb.TE.inc(mm)
            v.wait_ge(kb.TE.h, ty)
            v.wait_ge(kb.LD.h, ldx)
            v.wait_ge(kb.GD.h, max(0, kb.GD.n - 16))
            xo = kb.VE.inc(v.scalar_tensor_tensor(o, pn[:], 0.5, xt[:], ALU.mult, ALU.add))
            bn.read_by(kb.VE.h, xo)
            gp.wait_ge(kb.VE.h, xo)
            kb.GD.inc16(gp.dma_start(out_dram.ap()[tc * 128:(tc + 1) * 128, :], o))
        gp.wait_ge(kb.GD.h, kb.GD.n)



WSHAPES_ALL = {
    "L0_attn_qkv_kernel": (1024, 3072), "L0_attn_qkv_bias": (3072,),
    "L0_attn_out_kernel": (1024, 1024), "L0_attn_out_bias": (1024,),
    "L0_ffn_wqkv": (2, 1024, 1536), "L0_ffn_q_scale": (2, 64),
    "L0_ffn_k_scale": (2, 64), "L0_ffn_wo": (2, 512, 1024),
    "L0_ffn_ln_g": (2, 1024), "L0_ffn_ln_b": (2, 1024),
    "L0_ffn_w1": (2, 1024, 5460), "L0_ffn_b1": (2, 5460),
    "L0_ffn_w2": (2, 2730, 1024), "L0_ffn_b2": (2, 1024),
}


def build_full():
    nc = bass.Bass("TRN2", target_bir_lowering=False, detect_race_conditions=False)
    ins = {n: nc.dram_tensor(n, list(sh), F32, kind="ExternalInput")
           for n, sh in WSHAPES_ALL.items()}
    xcore = nc.dram_tensor("xcore", [T, D], F32, kind="ExternalInput")
    consts = host_constants()
    cds = {}
    for n, a in consts.items():
        a = np.asarray(a)
        dt = F32 if a.dtype == np.float32 else BF16
        cds[n] = nc.dram_tensor("c_" + n, list(a.shape), dt, kind="ExternalInput")
    y_sc = nc.dram_tensor("y_sc", [8, 128, 1 + T], F32, kind="Internal")
    x1_sc = nc.dram_tensor("x1_sc", [T, D], F32, kind="Internal")
    x2_sc = nc.dram_tensor("x2_sc", [T, D], F32, kind="Internal")
    qT_sc = nc.dram_tensor("qT_sc", [8, 128, T], BF16, kind="Internal")
    xout = nc.dram_tensor("xout", [T, D], F32, kind="ExternalOutput")
    A_dram = [cds[n] for n in ("A1q", "A2q", "A1k", "A2k")]
    kb = KB(nc)
    alloc_shared(kb)
    emit_shared_init(kb, cds)
    emit_A0(kb, xcore, y_sc)
    for l in (0, 1):
        emit_ffn_attn(kb, l, y_sc, ins, A_dram)
        emit_ffn_mlp(kb, l, y_sc, ins)
    emit_dense_block(kb, xcore, y_sc, x1_sc, x2_sc, qT_sc, ins)
    for l in (0, 1):
        emit_ffn_attn(kb, l, y_sc, ins, A_dram)
        emit_ffn_mlp(kb, l, y_sc, ins)
    emit_Cfin(kb, x2_sc, y_sc, xout)
    return nc, consts


def kernel(**inputs):
    from concourse.bass_utils import run_bass_kernel_spmd
    nc, consts = build_full()
    base = {n: np.ascontiguousarray(np.asarray(inputs[n], dtype=np.float32))
            for n in WSHAPES_ALL}
    for n, a in consts.items():
        base["c_" + n] = np.ascontiguousarray(a)
    x = np.asarray(inputs["L0_attn_x"], dtype=np.float32)
    in_maps = []
    for i in range(8):
        m = dict(base)
        m["xcore"] = np.ascontiguousarray(x[i])
        in_maps.append(m)
    res = run_bass_kernel_spmd(nc, in_maps, core_ids=list(range(8)))
    return np.stack([np.asarray(r["xout"]) for r in res.results], axis=0)
